# version 14
# baseline (speedup 1.0000x reference)
import numpy as np
import ml_dtypes
import concourse.bass as bass
import concourse.mybir as mybir
from concourse.bass_utils import run_bass_kernel_spmd
from contextlib import ExitStack

F32 = mybir.dt.float32
BF16 = mybir.dt.bfloat16
AF = mybir.ActivationFunctionType
ALU = mybir.AluOpType
AX = mybir.AxisListType

SEQ, OWN, D, HH, HID, HEADS = 4096, 2048, 2048, 2048, 8192, 16
LN_EPS = 1e-5
B1 = 512
NB1 = SEQ // B1
B3B = 512
NB3B = OWN // B3B
B3A = 256
NB3A = OWN // B3A
LADDER = (128.0, 64.0, 32.0, 16.0, 8.0, 4.0, 2.0, 1.0)


class Sem:
    def __init__(self, es, nc, name):
        self.h = es.enter_context(nc.semaphore(name))
        self.n = 0

    def inc(self, inst, k=1):
        inst.then_inc(self.h, k)
        self.n += k
        return self.n

    def mark(self, eng):
        eng.drain()
        eng.sem_inc(self.h, 1)
        self.n += 1
        return self.n


def build_program(upto="all", debug=False):
    nc = bass.Bass("TRN2", target_bir_lowering=False)
    dk = "ExternalOutput" if debug else "Internal"

    x_h = nc.dram_tensor("x_seq", [SEQ, D], F32, kind="ExternalInput").ap()
    wqk_h = nc.dram_tensor("Wqk", [D, HH], F32, kind="ExternalInput").ap()
    bqk_h = nc.dram_tensor("bqk", [HH], F32, kind="ExternalInput").ap()
    wv_h = nc.dram_tensor("Wv", [D, HH], F32, kind="ExternalInput").ap()
    bv_h = nc.dram_tensor("bv", [HH], F32, kind="ExternalInput").ap()
    wu_h = nc.dram_tensor("Wu", [D, HH], F32, kind="ExternalInput").ap()
    bu_h = nc.dram_tensor("bu", [HH], F32, kind="ExternalInput").ap()
    wo_h = nc.dram_tensor("Wo", [HH, D], F32, kind="ExternalInput").ap()
    bo_h = nc.dram_tensor("bo", [D], F32, kind="ExternalInput").ap()
    lng_h = nc.dram_tensor("ln_g", [HH], F32, kind="ExternalInput").ap()
    lnb_h = nc.dram_tensor("ln_b", [HH], F32, kind="ExternalInput").ap()
    gw_h = nc.dram_tensor("gate_w", [HID, D], F32, kind="ExternalInput").ap()
    gb_h = nc.dram_tensor("gate_b", [HID], F32, kind="ExternalInput").ap()
    vw_h = nc.dram_tensor("val_w", [HID, D], F32, kind="ExternalInput").ap()
    vb_h = nc.dram_tensor("val_b", [HID], F32, kind="ExternalInput").ap()
    ow_h = nc.dram_tensor("out_w", [D, HID], F32, kind="ExternalInput").ap()
    ob_h = nc.dram_tensor("out_b", [D], F32, kind="ExternalInput").ap()
    eyef_h = nc.dram_tensor("eyef", [128, 128], F32, kind="ExternalInput").ap()
    eyeb_h = nc.dram_tensor("eyeb", [128, 128], BF16, kind="ExternalInput").ap()
    y_h = nc.dram_tensor("y", [OWN, D], F32, kind="ExternalOutput").ap()

    q_hi_s = nc.dram_tensor("q_hi_s", [OWN, HH], BF16, kind=dk).ap()
    q_lo_s = nc.dram_tensor("q_lo_s", [OWN, HH], BF16, kind=dk).ap()
    xn_hi_s = nc.dram_tensor("xn_hi_s", [128, 16 * OWN], BF16, kind=dk).ap()
    xn_lo_s = nc.dram_tensor("xn_lo_s", [128, 16 * OWN], BF16, kind=dk).ap()
    nx_s = nc.dram_tensor("nx_s", [128, 16 * OWN], BF16, kind=dk).ap()
    gwt_s = nc.dram_tensor("gwt_s", [128, 16 * HID], BF16, kind=dk).ap()
    vwt_s = nc.dram_tensor("vwt_s", [128, 16 * HID], BF16, kind=dk).ap()
    owt_s = nc.dram_tensor("owt_s", [128, 64 * D], BF16, kind=dk).ap()
    h_sc = nc.dram_tensor("h_sc", [OWN, HID], F32, kind=dk).ap()
    ffn_s = nc.dram_tensor("ffn_s", [OWN, D], F32, kind=dk).ap()
    bq2_s = nc.dram_tensor("bq2_s", [2, HH], BF16, kind=dk).ap()
    bv2_s = nc.dram_tensor("bv2_s", [2, HH], BF16, kind=dk).ap()
    bu2_s = nc.dram_tensor("bu2_s", [2, HH], BF16, kind=dk).ap()
    bo2_s = nc.dram_tensor("bo2_s", [2, D], BF16, kind=dk).ap()
    rg_s = nc.dram_tensor("rg_s", [2, HID], BF16, kind=dk).ap()
    rv_s = nc.dram_tensor("rv_s", [2, HID], BF16, kind=dk).ap()
    ro_s = nc.dram_tensor("ro_s", [2, D], BF16, kind=dk).ap()
    if debug:
        kv_dbg = nc.dram_tensor("kv_dbg", [128, HH], F32, kind="ExternalOutput").ap()
        ar_dbg = nc.dram_tensor("ar_dbg", [128, 64], F32, kind="ExternalOutput").ap()
        sc_dbg = nc.dram_tensor("sc_dbg", [128, 8], F32, kind="ExternalOutput").ap()

    es = ExitStack()
    with es:
        def sb(name, shape, dt):
            return es.enter_context(nc.sbuf_tensor(name, shape, dt))

        kv_acc = sb("kv_acc", [128, HH], F32)
        kv_hi = sb("kv_hi", [128, HH], BF16)
        kv_lo = sb("kv_lo", [128, HH], BF16)
        AR = sb("AR", [128, 64], F32)
        A2 = sb("A2", [128, 8], F32)
        SC = sb("SC", [128, 8], F32)
        onesc = sb("onesc", [128, 1], F32)
        ones1 = sb("ones1", [1, 128], F32)
        ones2 = sb("ones2", [2, 128], BF16)
        eyef = sb("eyef_t", [128, 128], F32)
        eyeb = sb("eyeb_t", [128, 128], BF16)
        b2t = sb("b2t", [2, 1024], BF16)
        la = sb("la", [2, OWN], BF16)
        lb = sb("lb", [2, OWN], BF16)
        g_cols = sb("g_cols", [128, 16], F32)
        b_cols = sb("b_cols", [128, 16], F32)
        ihi16 = sb("ihi16", [128, 16], BF16)
        ilo16 = sb("ilo16", [128, 16], BF16)
        itr = sb("itr", [16, 256], BF16)

        pj = [es.enter_context(nc.psum_tensor(f"pj{i}", [128, 512], F32)) for i in range(2)]
        pkv = es.enter_context(nc.psum_tensor("pkv", [128, HH], F32))
        ptr = es.enter_context(nc.psum_tensor("ptr", [128, 512], F32))
        pxb = es.enter_context(nc.psum_tensor("pxb", [128, 1024], BF16))

        dX = Sem(es, nc, "dX")
        dW = Sem(es, nc, "dW")
        dS = Sem(es, nc, "dS")
        dR = Sem(es, nc, "dR")
        dO = Sem(es, nc, "dO")
        dB = Sem(es, nc, "dB")
        dF = Sem(es, nc, "dF")
        sV = Sem(es, nc, "sV")
        sA = Sem(es, nc, "sA")
        sG = Sem(es, nc, "sG")
        sT = Sem(es, nc, "sT")

        block = es.enter_context(nc.Block())

        def on(name):
            return getattr(block, name)

        def barrier():
            toks = {}

            @on("vector")
            def _(v):
                toks["v"] = sV.mark(v)

            @on("scalar")
            def _(a):
                toks["a"] = sA.mark(a)

            @on("gpsimd")
            def _(g):
                toks["g"] = sG.mark(g)

            @on("tensor")
            def _(t):
                toks["t"] = sT.mark(t)

            for en in ("vector", "scalar", "gpsimd", "tensor", "sync"):
                @on(en)
                def _(e):
                    e.wait_ge(sV.h, toks["v"])
                    e.wait_ge(sA.h, toks["a"])
                    e.wait_ge(sG.h, toks["g"])
                    e.wait_ge(sT.h, toks["t"])

        def ladder(eng, scr, src_ap, dst_ap, scale_ap):
            s2, aa, dd, md = scr
            n = src_ap.shape[-1]
            s2v, aav, ddv, mdv = s2[:, 0:n], aa[:, 0:n], dd[:, 0:n], md[:, 0:n]
            if scale_ap is not None:
                eng.tensor_scalar(out=mdv, in0=src_ap, scalar1=scale_ap,
                                  scalar2=None, op0=ALU.mult)
                t = mdv
            else:
                t = src_ap
            eng.tensor_scalar(out=s2v, in0=t, scalar1=0.0, scalar2=2.0,
                              op0=ALU.is_ge, op1=ALU.mult)
            eng.tensor_tensor(out=aav, in0=t, in1=s2v, op=ALU.mult)
            eng.tensor_tensor(out=aav, in0=aav, in1=t, op=ALU.subtract)
            eng.tensor_copy(out=ddv, in_=aav)
            for k in LADDER:
                eng.tensor_scalar(out=mdv, in0=ddv, scalar1=k - 0.5, scalar2=-k,
                                  op0=ALU.is_ge, op1=ALU.mult)
                eng.tensor_tensor(out=ddv, in0=ddv, in1=mdv, op=ALU.add)
            eng.tensor_tensor(out=aav, in0=aav, in1=ddv, op=ALU.subtract)
            eng.tensor_tensor(out=mdv, in0=aav, in1=s2v, op=ALU.mult)
            return eng.tensor_tensor(out=dst_ap, in0=mdv, in1=aav, op=ALU.subtract)

        def emit_setup(small, smbf, smlo):
            @on("vector")
            def _(v):
                v.memset(onesc[:], 1.0)
                v.memset(SC[:, 6:7], LN_EPS)
                v.memset(ones1[:], 1.0)
                v.memset(ones2[:], 1.0)
                v.memset(kv_acc[:], 0.0)
                v.memset(AR[:], 0.0)

            @on("sync")
            def _(s):
                s.dma_start(out=eyef[:], in_=eyef_h[:]).then_inc(dB.h, 16)
                s.dma_start(out=eyeb[:], in_=eyeb_h[:]).then_inc(dB.h, 16)
                s.dma_start(out=g_cols[:],
                            in_=lng_h.rearrange("(c p) -> p c", p=128),
                            allow_slow_non_contiguous=True).then_inc(dB.h, 16)
                s.dma_start(out=b_cols[:],
                            in_=lnb_h.rearrange("(c p) -> p c", p=128),
                            allow_slow_non_contiguous=True).then_inc(dB.h, 16)
                dB.n += 64

            for src, dst, n in ((bqk_h, bq2_s, HH), (bv_h, bv2_s, HH),
                                (bu_h, bu2_s, HH), (bo_h, bo2_s, D)):
                @on("sync")
                def _(s, src=src, n=n):
                    s.wait_ge(sV.h, sV.n)
                    s.dma_start(out=small[0:1, 0:n], in_=src[None, :]).then_inc(dB.h, 16)
                    dB.n += 16
                tok = dB.n

                @on("vector")
                def _(v, n=n, tok=tok):
                    v.wait_ge(dB.h, tok)
                    v.tensor_copy(out=smbf[0:1, 0:n], in_=small[0:1, 0:n])
                    sV.inc(v.tensor_tensor(out=smlo[0:1, 0:n], in0=small[0:1, 0:n],
                                    in1=smbf[0:1, 0:n], op=ALU.subtract))
                vtok = sV.n

                @on("sync")
                def _(s, dst=dst, n=n, vtok=vtok):
                    s.wait_ge(sV.h, vtok)
                    s.dma_start(out=dst[0:1, 0:n], in_=smbf[0:1, 0:n]).then_inc(dB.h, 16)
                    s.dma_start(out=dst[1:2, 0:n], in_=smlo[0:1, 0:n]).then_inc(dB.h, 16)
                    dB.n += 32
                tok2 = dB.n

                @on("vector")
                def _(v, tok2=tok2):
                    v.wait_ge(dB.h, tok2)
                    sV.mark(v)

        def emit_p0(bw):
            plans = []
            for idx, (wh, R, C) in (((1, (vw_h, HID, D))), ((0, (gw_h, HID, D))),
                                    ((2, (ow_h, D, HID)))):
                tiles = []
                for rt in range(R // 128):
                    for cq in range(C // 2048):
                        tiles.append(wh[rt * 128:(rt + 1) * 128,
                                        cq * 2048:(cq + 1) * 2048])
                plans.append((idx, R * C, tiles))

            for idx, nel, tiles in plans:
                @on("vector")
                def _(v):
                    sV.inc(v.memset(A2[:, 0:1], 0.0))

                red_tok = [sV.n, sV.n]
                for it, tile in enumerate(tiles):
                    buf = bw[it % 2]

                    @on("sync")
                    def _(s, tile=tile, buf=buf, it=it):
                        s.wait_ge(sV.h, red_tok[it % 2])
                        s.dma_start(out=buf[:, 0:2048], in_=tile).then_inc(dX.h, 16)
                        dX.n += 16
                    tok = dX.n

                    @on("vector")
                    def _(v, buf=buf, tok=tok):
                        v.wait_ge(dX.h, tok)
                        v.tensor_reduce(out=A2[:, 1:2], in_=buf[:, 0:2048], axis=AX.X,
                                        op=ALU.add, apply_absolute_value=True)
                        v.drain()
                        sV.inc(v.tensor_tensor(out=A2[:, 0:1], in0=A2[:, 0:1],
                                               in1=A2[:, 1:2], op=ALU.add))
                    red_tok[it % 2] = sV.n

                @on("tensor")
                def _(t):
                    t.wait_ge(sV.h, sV.n)
                    sT.inc(t.matmul(out=ptr[0:1, 0:1], lhsT=onesc[:],
                                    rhs=A2[:, 0:1], start=True, stop=True))
                ttok = sT.n

                @on("vector")
                def _(v, nel=nel, ttok=ttok):
                    v.wait_ge(sT.h, ttok)
                    sV.inc(v.tensor_scalar(out=A2[0:1, 2:3], in0=ptr[0:1, 0:1],
                                    scalar1=1.0 / nel, scalar2=1e-5,
                                    op0=ALU.mult, op1=ALU.max))
                vtok = sV.n

                @on("tensor")
                def _(t, vtok=vtok):
                    t.wait_ge(sV.h, vtok)
                    sT.inc(t.matmul(out=ptr[:, 1:2], lhsT=ones1[:],
                                    rhs=A2[0:1, 2:3], start=True, stop=True))
                ttok2 = sT.n

                @on("vector")
                def _(v, idx=idx, ttok2=ttok2):
                    v.wait_ge(sT.h, ttok2)
                    v.tensor_copy(out=SC[:, idx:idx + 1], in_=ptr[:, 1:2])
                    v.drain()
                    sV.inc(v.reciprocal(out=SC[:, 3 + idx:4 + idx], in_=SC[:, idx:idx + 1]))


        def emit_wsplit(wh, fcol, whi, wlo, wstg, stg_tok, wpar_tok, par):
            for q in range(8):
                sl = q % 2
                stg = wstg[sl]

                @on("sync")
                def _(sy, q=q, sl=sl, stg=stg, wh=wh, fcol=fcol):
                    sy.wait_ge(sA.h, stg_tok[sl][0])
                    sy.wait_ge(sV.h, stg_tok[sl][1])
                    src = wh[q * 256:(q + 1) * 256, fcol * 512:(fcol + 1) * 512]
                    sy.dma_start(out=stg[:].rearrange("p (c n) -> p c n", c=2),
                                 in_=src.rearrange("(c p) n -> p c n", p=128)
                                 ).then_inc(dW.h, 16)
                    dW.n += 16
                wtok = dW.n

                @on("scalar")
                def _(a, q=q, stg=stg, whi=whi, wtok=wtok, par=par):
                    a.wait_ge(dW.h, wtok)
                    a.wait_ge(sT.h, wpar_tok[par])
                    sA.inc(a.copy(out=whi[:, q * 1024:(q + 1) * 1024], in_=stg[:]))
                atok = sA.n

                @on("vector")
                def _(v, q=q, stg=stg, whi=whi, wlo=wlo, atok=atok):
                    v.wait_ge(sA.h, atok)
                    sV.inc(v.tensor_tensor(out=wlo[:, q * 1024:(q + 1) * 1024],
                                           in0=stg[:],
                                           in1=whi[:, q * 1024:(q + 1) * 1024],
                                           op=ALU.subtract))
                stg_tok[sl] = (atok, sV.n)
            return (sA.n, sV.n)

        def emit_p1(P):
            (x_t, xn_tm, nx_tm, xnf_hi, xnf_lo, wstg, wc_hi, wc_lo,
             kh_hi, kh_lo, vh_hi, vh_lo, ef32, lad_v, b2qk, b2v) = P
            nsub = B1 // 128
            xn_img_hi = xnf_hi[:].rearrange("p (c n) -> p c n", c=16)
            xn_img_lo = xnf_lo[:].rearrange("p (c n) -> p c n", c=16)
            nxs_img = nx_s.rearrange("p (c n) -> p c n", c=16)
            xnhs_img = xn_hi_s.rearrange("p (c n) -> p c n", c=16)
            xnls_img = xn_lo_s.rearrange("p (c n) -> p c n", c=16)
            ptr_img = ptr[:].rearrange("p (c n) -> p c n", c=4)
            x_tok = [sV.n, sV.n]
            q_spill_tok = [0, 0, 0, 0]
            xn_spill_tok = 0
            nx_spill_tok = 0
            exp_tok = [0]
            sil_tok = [0]
            bank_tok = [(0, 0), (0, 0)]
            wpar_tok = [0, 0]
            stg_tok = [(0, 0), (0, 0)]
            pend_split = [None, None]
            pend_split[0] = emit_wsplit(wqk_h, 0, wc_hi[0], wc_lo[0], wstg,
                                        stg_tok, wpar_tok, 0)

            for b in range(NB1):
                own = b < OWN // B1
                for s in range(nsub):
                    xb = x_t[s % 2]
                    row0 = b * B1 + s * 128
                    tcol = row0 // 128 if own else 0

                    @on("sync")
                    def _(sy, xb=xb, row0=row0, s=s):
                        sy.wait_ge(sV.h, x_tok[s % 2])
                        sy.dma_start(out=xb[:], in_=x_h[row0:row0 + 128, :]).then_inc(dX.h, 16)
                        dX.n += 16
                    xtok = dX.n

                    @on("vector")
                    def _(v, xb=xb, xtok=xtok):
                        v.wait_ge(dX.h, xtok)
                        v.wait_ge(sT.h, sT.n)
                        v.wait_ge(sG.h, sG.n)
                        v.reduce_sum(out=A2[:, 0:1], in_=xb[:], axis=AX.X)
                        v.tensor_tensor(out=xn_tm[:], in0=xb[:], in1=xb[:], op=ALU.mult)
                        v.reduce_sum(out=A2[:, 1:2], in_=xn_tm[:], axis=AX.X)
                        v.drain()
                        v.tensor_scalar(out=A2[:, 2:3], in0=A2[:, 0:1],
                                        scalar1=1.0 / D, scalar2=None, op0=ALU.mult)
                        v.drain()
                        v.tensor_tensor(out=A2[:, 3:4], in0=A2[:, 2:3],
                                        in1=A2[:, 2:3], op=ALU.mult)
                        v.tensor_scalar(out=A2[:, 4:5], in0=A2[:, 1:2],
                                        scalar1=1.0 / D, scalar2=None, op0=ALU.mult)
                        v.drain()
                        sV.inc(v.tensor_tensor(out=A2[:, 4:5], in0=A2[:, 4:5],
                                        in1=A2[:, 3:4], op=ALU.subtract))
                    vtok = sV.n

                    @on("scalar")
                    def _(a, vtok=vtok):
                        a.wait_ge(sV.h, vtok)
                        sA.inc(a.activation(out=A2[:, 5:6], in_=A2[:, 4:5], func=AF.Sqrt,
                                     bias=SC[:, 6:7]))
                    atok = sA.n

                    @on("vector")
                    def _(v, xb=xb, atok=atok, tcol=tcol, own=own):
                        v.wait_ge(sA.h, atok)
                        if own:
                            v.wait_ge(dS.h, nx_spill_tok)
                        v.reciprocal(out=A2[:, 5:6], in_=A2[:, 5:6])
                        v.drain()
                        li = v.tensor_scalar(out=xn_tm[:], in0=xb[:], scalar1=A2[:, 2:3],
                                             scalar2=A2[:, 5:6], op0=ALU.subtract,
                                             op1=ALU.mult)
                        if own:
                            v.tensor_reduce(out=A2[:, 6:7], in_=xn_tm[:], axis=AX.X,
                                            op=ALU.max, apply_absolute_value=True)
                            v.drain()
                            v.tensor_scalar(out=AR[:, tcol:tcol + 1], in0=A2[:, 6:7],
                                            scalar1=1e-5, scalar2=None, op0=ALU.max)
                            v.drain()
                            v.reciprocal(out=A2[:, 7:8], in_=AR[:, tcol:tcol + 1])
                            v.drain()
                            v.tensor_scalar(out=AR[:, 16 + tcol:17 + tcol],
                                            in0=A2[:, 7:8], scalar1=127.0,
                                            scalar2=None, op0=ALU.mult)
                            v.tensor_scalar(out=AR[:, 32 + tcol:33 + tcol],
                                            in0=AR[:, tcol:tcol + 1],
                                            scalar1=1.0 / 127.0, scalar2=None,
                                            op0=ALU.mult)
                            v.drain()
                            for j in range(4):
                                li = ladder(v, lad_v, xn_tm[:, j * 512:(j + 1) * 512],
                                            nx_tm[:, j * 512:(j + 1) * 512],
                                            AR[:, 16 + tcol:17 + tcol])
                        sV.inc(li)
                    lntok = sV.n
                    x_tok[s % 2] = lntok

                    for grp in range(4):
                        @on("tensor")
                        def _(t, grp=grp, lntok=lntok):
                            t.wait_ge(sV.h, lntok)
                            t.wait_ge(sA.h, sA.n)
                            t.wait_ge(sV.h, sV.n)
                            for i in range(4):
                                c = grp * 4 + i
                                li = t.transpose(out=ptr[:, i * 128:(i + 1) * 128],
                                            in_=xn_tm[:, c * 128:(c + 1) * 128],
                                            identity=eyef[:])
                            sT.inc(li)
                        ttok = sT.n

                        @on("scalar")
                        def _(a, grp=grp, s=s, ttok=ttok):
                            a.wait_ge(sT.h, ttok)
                            a.wait_ge(dS.h, xn_spill_tok)
                            sA.inc(a.copy(out=xn_img_hi[:, grp * 4:(grp + 1) * 4,
                                                 s * 128:(s + 1) * 128],
                                   in_=ptr_img))
                        atok2 = sA.n

                        @on("vector")
                        def _(v, grp=grp, s=s, ttok=ttok, atok2=atok2):
                            v.wait_ge(sT.h, ttok)
                            v.wait_ge(sA.h, atok2)
                            sV.inc(v.tensor_tensor(out=xn_img_lo[:, grp * 4:(grp + 1) * 4,
                                                          s * 128:(s + 1) * 128],
                                            in0=ptr_img,
                                            in1=xn_img_hi[:, grp * 4:(grp + 1) * 4,
                                                          s * 128:(s + 1) * 128],
                                            op=ALU.subtract))

                    if own:
                        for grp in range(4):
                            @on("tensor")
                            def _(t, grp=grp, lntok=lntok):
                                t.wait_ge(sV.h, lntok)
                                t.wait_ge(sA.h, sA.n)
                                for i in range(4):
                                    c = grp * 4 + i
                                    li = t.transpose(out=pxb[:, i * 128:(i + 1) * 128],
                                                in_=nx_tm[:, c * 128:(c + 1) * 128],
                                                identity=eyeb[:])
                                sT.inc(li)
                            ttok = sT.n

                            @on("scalar")
                            def _(a, grp=grp, ttok=ttok):
                                a.wait_ge(sT.h, ttok)
                                sA.inc(a.copy(out=nx_tm[:, grp * 512:(grp + 1) * 512],
                                       in_=pxb[:, 0:512]))
                        gtok = sA.n

                        @on("sync")
                        def _(sy, row0=row0, gtok=gtok):
                            sy.wait_ge(sA.h, gtok)
                            sy.dma_start(out=nxs_img[:, :, row0:row0 + 128],
                                         in_=nx_tm[:].rearrange("p (c n) -> p c n",
                                                                c=16)).then_inc(dS.h, 16)
                            dS.n += 16
                        nx_spill_tok = dS.n

                        @on("sync")
                        def _(sy, s=s, row0=row0):
                            sy.wait_ge(sV.h, sV.n)
                            sy.wait_ge(sA.h, sA.n)
                            sy.dma_start(out=xnhs_img[:, :, row0:row0 + 128],
                                         in_=xn_img_hi[:, :, s * 128:(s + 1) * 128]
                                         ).then_inc(dS.h, 16)
                            sy.dma_start(out=xnls_img[:, :, row0:row0 + 128],
                                         in_=xn_img_lo[:, :, s * 128:(s + 1) * 128]
                                         ).then_inc(dS.h, 16)
                            dS.n += 32
                        xn_spill_tok = dS.n

                for fcol in range(4):
                    for wi, (wh, b2w) in enumerate(((wqk_h, b2qk), (wv_h, b2v))):
                        gidx = b * 8 + fcol * 2 + wi
                        par = gidx % 2
                        whi, wlo = wc_hi[par], wc_lo[par]
                        splitok = pend_split[par]

                        for tt in range(4):
                            bi = tt % 2
                            pb = pj[bi]

                            @on("tensor")
                            def _(t, tt=tt, fcol=fcol, b2w=b2w, whi=whi, wlo=wlo,
                                  pb=pb, splitok=splitok, bi=bi):
                                t.wait_ge(sA.h, splitok[0])
                                t.wait_ge(sV.h, splitok[1])
                                t.wait_ge(sV.h, bank_tok[bi][0])
                                t.wait_ge(sA.h, bank_tok[bi][1])
                                t.matmul(out=pb[:], lhsT=ones2[:],
                                         rhs=b2w[:, fcol * 512:(fcol + 1) * 512],
                                         start=True, stop=False)
                                for dc in range(16):
                                    t.matmul(out=pb[:],
                                             lhsT=xn_img_hi[:, dc, tt * 128:(tt + 1) * 128],
                                             rhs=whi[:, dc * 512:(dc + 1) * 512],
                                             start=False, stop=False)
                                for dc in range(16):
                                    t.matmul(out=pb[:],
                                             lhsT=xn_img_hi[:, dc, tt * 128:(tt + 1) * 128],
                                             rhs=wlo[:, dc * 512:(dc + 1) * 512],
                                             start=False, stop=False)
                                for dc in range(16):
                                    li = t.matmul(out=pb[:],
                                             lhsT=xn_img_lo[:, dc, tt * 128:(tt + 1) * 128],
                                             rhs=whi[:, dc * 512:(dc + 1) * 512],
                                             start=False, stop=(dc == 15))
                                sT.inc(li)
                            mmtok = sT.n
                            wpar_tok[par] = mmtok

                            if tt == 0 and gidx + 1 < NB1 * 8:
                                nb, nrem = divmod(gidx + 1, 8)
                                nfc, nwi = divmod(nrem, 2)
                                nwh = wqk_h if nwi == 0 else wv_h
                                pend_split[(gidx + 1) % 2] = emit_wsplit(
                                    nwh, nfc, wc_hi[(gidx + 1) % 2],
                                    wc_lo[(gidx + 1) % 2], wstg, stg_tok,
                                    wpar_tok, (gidx + 1) % 2)

                            if wi == 0:
                                @on("vector")
                                def _(v, pb=pb, mmtok=mmtok):
                                    v.wait_ge(sT.h, mmtok)
                                    v.wait_ge(sA.h, exp_tok[0])
                                    v.tensor_scalar(out=ef32[0][:], in0=pb[:],
                                                    scalar1=0.0, scalar2=None,
                                                    op0=ALU.min)
                                    sV.inc(v.tensor_scalar(out=ef32[1][:], in0=pb[:],
                                                           scalar1=0.0, scalar2=None,
                                                           op0=ALU.max))
                                vtok = sV.n
                                bank_tok[bi] = (vtok, 0)

                                @on("scalar")
                                def _(a, vtok=vtok):
                                    a.wait_ge(sV.h, vtok)
                                    sA.inc(a.activation(out=ef32[0][:], in_=ef32[0][:],
                                                        func=AF.Exp))
                                exp_tok[0] = sA.n
                                atok = sA.n

                                @on("vector")
                                def _(v, tt=tt, atok=atok):
                                    v.wait_ge(sA.h, atok)
                                    v.wait_ge(dS.h, q_spill_tok[tt])
                                    v.scalar_tensor_tensor(out=ef32[1][:], in0=ef32[0][:],
                                                           scalar=-1.0, in1=ef32[1][:],
                                                           op0=ALU.add, op1=ALU.add)
                                    v.tensor_copy(out=kh_hi[:, tt * 512:(tt + 1) * 512],
                                                  in_=ef32[1][:])
                                    sV.inc(v.tensor_tensor(out=kh_lo[:, tt * 512:(tt + 1) * 512],
                                                    in0=ef32[1][:],
                                                    in1=kh_hi[:, tt * 512:(tt + 1) * 512],
                                                    op=ALU.subtract))
                            else:
                                @on("scalar")
                                def _(a, pb=pb, tt=tt, mmtok=mmtok):
                                    a.wait_ge(sT.h, mmtok)
                                    a.wait_ge(sV.h, sil_tok[0])
                                    sA.inc(a.activation(out=ef32[2][:], in_=pb[:],
                                                        func=AF.Silu))
                                atok = sA.n
                                bank_tok[bi] = (0, atok)

                                @on("vector")
                                def _(v, tt=tt, atok=atok):
                                    v.wait_ge(sA.h, atok)
                                    v.tensor_copy(out=vh_hi[:, tt * 512:(tt + 1) * 512],
                                                  in_=ef32[2][:])
                                    sV.inc(v.tensor_tensor(out=vh_lo[:, tt * 512:(tt + 1) * 512],
                                                    in0=ef32[2][:],
                                                    in1=vh_hi[:, tt * 512:(tt + 1) * 512],
                                                    op=ALU.subtract))
                                sil_tok[0] = sV.n

                            if wi == 0 and own:
                                @on("sync")
                                def _(sy, b=b, tt=tt, fcol=fcol):
                                    sy.wait_ge(sV.h, sV.n)
                                    sy.wait_ge(sA.h, sA.n)
                                    r0 = b * B1 + tt * 128
                                    sy.dma_start(
                                        out=q_hi_s[r0:r0 + 128,
                                                   fcol * 512:(fcol + 1) * 512],
                                        in_=kh_hi[:, tt * 512:(tt + 1) * 512]
                                    ).then_inc(dS.h, 16)
                                    sy.dma_start(
                                        out=q_lo_s[r0:r0 + 128,
                                                   fcol * 512:(fcol + 1) * 512],
                                        in_=kh_lo[:, tt * 512:(tt + 1) * 512]
                                    ).then_inc(dS.h, 16)
                                    dS.n += 32
                                q_spill_tok[tt] = dS.n

                    evactok = (sV.n, sA.n)

                    @on("tensor")
                    def _(t, b=b, fcol=fcol, evactok=evactok):
                        t.wait_ge(sV.h, evactok[0])
                        t.wait_ge(sA.h, evactok[1])
                        first = True
                        for hd in range(4):
                            col = fcol * 512 + hd * 128
                            for tt in range(4):
                                for pi, (lh, rh) in enumerate(((kh_hi, vh_hi),
                                                               (kh_hi, vh_lo),
                                                               (kh_lo, vh_hi))):
                                    last = (hd == 3 and tt == 3 and pi == 2)
                                    sl = slice(tt * 512 + hd * 128,
                                               tt * 512 + (hd + 1) * 128)
                                    li = t.matmul(out=pkv[:, col:col + 128],
                                                  lhsT=lh[:, sl], rhs=rh[:, sl],
                                                  start=first, stop=last)
                                    first = False
                        sT.inc(li)

                kvtok = sT.n

                @on("vector")
                def _(v, kvtok=kvtok):
                    v.wait_ge(sT.h, kvtok)
                    sV.inc(v.tensor_tensor(out=kv_acc[:], in0=pkv[:], in1=kv_acc[:],
                                    op=ALU.add))

            @on("vector")
            def _(v):
                v.tensor_copy(out=kv_hi[:], in_=kv_acc[:])
                v.tensor_tensor(out=kv_lo[:], in0=kv_acc[:], in1=kv_hi[:],
                                op=ALU.subtract)
                v.tensor_copy(out=ihi16[:], in_=AR[:, 16:32])
                v.drain()
                sV.inc(v.tensor_tensor(out=ilo16[:], in0=AR[:, 16:32], in1=ihi16[:],
                                op=ALU.subtract))
            vtok = sV.n

            @on("tensor")
            def _(t, vtok=vtok):
                t.wait_ge(sV.h, vtok)
                t.wait_ge(sA.h, sA.n)
                t.transpose(out=pxb[0:16, 0:128], in_=ihi16[:], identity=eyeb[:])
                sT.inc(t.transpose(out=pxb[0:16, 128:256], in_=ilo16[:], identity=eyeb[:]))
            ttok = sT.n

            @on("scalar")
            def _(a, ttok=ttok):
                a.wait_ge(sT.h, ttok)
                sA.inc(a.copy(out=itr[:, 0:256], in_=pxb[0:16, 0:256]))
            atok = sA.n

            @on("sync")
            def _(sy, atok=atok):
                sy.wait_ge(sA.h, atok)
                for dst, c0s in ((la, (0, 128)), (lb, (128, 0))):
                    for r, c0 in enumerate(c0s):
                        sy.dma_start(
                            out=dst[r:r + 1, :].rearrange("o (c p) -> o c p", p=128),
                            in_=itr[:, c0:c0 + 128]).then_inc(dB.h, 16)
                        dB.n += 16

            if debug:
                @on("sync")
                def _(sy):
                    sy.wait_ge(sV.h, sV.n)
                    sy.dma_start(out=kv_dbg, in_=kv_acc[:]).then_inc(dS.h, 16)
                    sy.dma_start(out=ar_dbg, in_=AR[:]).then_inc(dS.h, 16)
                    sy.dma_start(out=sc_dbg, in_=SC[:]).then_inc(dS.h, 16)
                    dS.n += 48

        def emit_tern(P):
            tstr, tp1, tern2, ttmp, smallF, smbfF, smloF = P
            plans = [(gw_h, gwt_s, HID, D, 0), (vw_h, vwt_s, HID, D, 1),
                     (ow_h, owt_s, D, HID, 2)]
            str_tok = [(sV.n, sG.n), (sV.n, sG.n)]
            spill_tok = [0, 0, 0, 0]
            ter_tok = [0, 0]
            pxb_tok = [0, 0]
            for wh, dst_s, R, C, idx in plans:
                dst_img = dst_s.rearrange("p (c n) -> p c n", c=16 if idx < 2 else 64)

                @on("vector")
                def _(v, idx=idx):
                    v.tensor_scalar(out=A2[:, 3:4], in0=SC[:, idx:idx + 1],
                                    scalar1=0.5, scalar2=None, op0=ALU.mult)
                    sV.inc(v.tensor_scalar(out=A2[:, 4:5], in0=SC[:, idx:idx + 1],
                                    scalar1=-0.5, scalar2=None, op0=ALU.mult))
                thr_tok = sV.n


                ntile = (R // 128) * (C // 2048)
                for it in range(ntile):
                    rt, cq = it // (C // 2048), it % (C // 2048)
                    sl = it % 2
                    buf = tstr[sl]
                    ter = tern2[sl]

                    @on("sync")
                    def _(sy, wh=wh, rt=rt, cq=cq, buf=buf, sl=sl):
                        sy.wait_ge(sV.h, str_tok[sl][0])
                        sy.wait_ge(sG.h, str_tok[sl][1])
                        sy.dma_start(out=buf[:],
                                     in_=wh[rt * 128:(rt + 1) * 128,
                                            cq * 2048:(cq + 1) * 2048]).then_inc(dW.h, 16)
                        dW.n += 16
                    wtok = dW.n

                    @on("vector")
                    def _(e, buf=buf, ter=ter, wtok=wtok, sl=sl):
                        e.wait_ge(dW.h, wtok)
                        e.wait_ge(sT.h, ter_tok[sl])
                        e.tensor_scalar(out=tp1[:], in0=buf[:], scalar1=A2[:, 3:4],
                                        scalar2=None, op0=ALU.is_ge)
                        e.tensor_scalar(out=ter[:], in0=buf[:], scalar1=A2[:, 4:5],
                                        scalar2=-1.0, op0=ALU.is_le, op1=ALU.mult)
                        sV.inc(e.tensor_tensor(out=ter[:], in0=ter[:], in1=tp1[:],
                                               op=ALU.add))
                    str_tok[sl] = (sV.n, sG.n)
                    etok = (sV.n, sG.n)
                    tsl4 = it % 4
                    tstage = ttmp[tsl4]

                    for grp in range(4):
                        ph_ = grp % 2

                        @on("tensor")
                        def _(t, ter=ter, grp=grp, etok=etok, ph_=ph_):
                            t.wait_ge(sV.h, etok[0])
                            t.wait_ge(sA.h, pxb_tok[ph_])
                            for i in range(4):
                                c = grp * 4 + i
                                li = t.transpose(
                                    out=pxb[:, (ph_ * 4 + i) * 128:(ph_ * 4 + i + 1) * 128],
                                    in_=ter[:, c * 128:(c + 1) * 128],
                                    identity=eyeb[:])
                            sT.inc(li)
                        ttok = sT.n

                        @on("scalar")
                        def _(a, grp=grp, ttok=ttok, tstage=tstage, tsl4=tsl4, ph_=ph_):
                            a.wait_ge(sT.h, ttok)
                            a.wait_ge(dS.h, spill_tok[tsl4])
                            sA.inc(a.copy(out=tstage[:, grp * 512:(grp + 1) * 512],
                                          in_=pxb[:, ph_ * 512:(ph_ + 1) * 512]))
                        pxb_tok[ph_] = sA.n
                    ter_tok[sl] = sT.n
                    atok = sA.n

                    @on("sync")
                    def _(sy, rt=rt, cq=cq, idx=idx, atok=atok, dst_img=dst_img,
                          tstage=tstage):
                        sy.wait_ge(sA.h, atok)
                        c0 = cq * 16
                        dst = dst_img[:, c0:c0 + 16, rt * 128:(rt + 1) * 128]
                        sy.dma_start(out=dst,
                                     in_=tstage[:].rearrange("p (c n) -> p c n", c=16)
                                     ).then_inc(dS.h, 16)
                        dS.n += 16
                    spill_tok[tsl4] = dS.n

            for src, dst, n, idx in ((gb_h, rg_s, HID, 0), (vb_h, rv_s, HID, 1),
                                     (ob_h, ro_s, D, 2)):
                @on("sync")
                def _(sy, src=src, n=n):
                    sy.wait_ge(sV.h, sV.n)
                    sy.dma_start(out=smallF[0:1, 0:n], in_=src[None, :]).then_inc(dB.h, 16)
                    dB.n += 16
                tok = dB.n

                @on("vector")
                def _(v, n=n, idx=idx, tok=tok):
                    v.wait_ge(dB.h, tok)
                    v.tensor_scalar(out=smallF[0:1, 0:n], in0=smallF[0:1, 0:n],
                                    scalar1=SC[0:1, 3 + idx:4 + idx], scalar2=None,
                                    op0=ALU.mult)
                    v.drain()
                    v.tensor_copy(out=smbfF[0:1, 0:n], in_=smallF[0:1, 0:n])
                    sV.inc(v.tensor_tensor(out=smloF[0:1, 0:n], in0=smallF[0:1, 0:n],
                                    in1=smbfF[0:1, 0:n], op=ALU.subtract))
                vtok = sV.n

                @on("sync")
                def _(sy, dst=dst, n=n, vtok=vtok):
                    sy.wait_ge(sV.h, vtok)
                    sy.dma_start(out=dst[0:1, 0:n], in_=smbfF[0:1, 0:n]).then_inc(dB.h, 16)
                    sy.dma_start(out=dst[1:2, 0:n], in_=smloF[0:1, 0:n]).then_inc(dB.h, 16)
                    dB.n += 32
                tok2 = dB.n

                @on("vector")
                def _(v, tok2=tok2):
                    v.wait_ge(dB.h, tok2)
                    sV.mark(v)

        def emit_p3b(P):
            (nxl, gwc, vwc, hq, nq, n_hf, owc, ffn_acc, AR2, ef, lad_v, lad_g,
             lao, lbo, b2o3, hgv) = P
            nxs_img = nx_s.rearrange("p (c n) -> p c n", c=16)
            gwt_img = gwt_s.rearrange("p (c n) -> p c n", c=16)
            vwt_img = vwt_s.rearrange("p (c n) -> p c n", c=16)
            owt_img = owt_s.rearrange("p (c n) -> p c n", c=64)
            nxl_img = nxl[:].rearrange("p (c n) -> p c n", c=16)
            gwc_img = gwc[:].rearrange("p (c n) -> p c n", c=16)
            vwc_img = vwc[:].rearrange("p (c n) -> p c n", c=16)
            nhf_img = n_hf[:].rearrange("p (c n) -> p c n", c=32)
            owc_img = owc[:].rearrange("p (c n) -> p c n", c=32)
            facc_img = ffn_acc[:].rearrange("p (c n) -> p c n", c=4)

            gv_tok = [0, 0]
            out_pkv_tok = [0]
            owc_tok = [0]
            ef_tok = [0, 0]
            h_spill_tok = [0, 0]
            hscl_tok = [0, 0]
            lad_tok = [0, 0]
            nq_tok = [0, 0]

            @on("vector")
            def _(v):
                v.tensor_scalar(out=AR2[:, 0:16], in0=AR[:, 32:48],
                                scalar1=SC[:, 0:1], scalar2=None, op0=ALU.mult)
                v.tensor_scalar(out=AR2[:, 16:32], in0=AR[:, 32:48],
                                scalar1=SC[:, 1:2], scalar2=None, op0=ALU.mult)
                sV.inc(v.memset(AR2[:, 32:48], 0.0))

            def gv_block(b3):
                r0 = b3 * B3B

                @on("sync")
                def _(sy, r0=r0):
                    sy.wait_ge(sT.h, sT.n)
                    sy.dma_start(out=nxl[:], in_=nxs_img[:, :, r0:r0 + B3B]
                                 ).then_inc(dR.h, 16)
                    dR.n += 16
                nxtok = dR.n

                for hcol in range(16):
                    @on("sync")
                    def _(sy, hcol=hcol):
                        sy.wait_ge(sT.h, sT.n)
                        sy.dma_start(out=gwc[:], in_=gwt_img[:, :, hcol * 512:
                                                           (hcol + 1) * 512]
                                     ).then_inc(dR.h, 16)
                        sy.dma_start(out=vwc[:], in_=vwt_img[:, :, hcol * 512:
                                                           (hcol + 1) * 512]
                                     ).then_inc(dR.h, 16)
                        sy.dma_start(out=b2t[:, 0:512],
                                     in_=rg_s[:, hcol * 512:(hcol + 1) * 512]
                                     ).then_inc(dB.h, 16)
                        sy.dma_start(out=b2t[:, 512:1024],
                                     in_=rv_s[:, hcol * 512:(hcol + 1) * 512]
                                     ).then_inc(dB.h, 16)
                        dR.n += 32
                        dB.n += 32
                    wtok = (dR.n, dB.n)

                    for tt4 in range(4):
                        tcol = b3 * 4 + tt4
                        tsl = slice(r0 + tt4 * 128, r0 + (tt4 + 1) * 128)
                        kpar = tt4 % 2
                        pgate = pj[0][:] if kpar == 0 else pkv[:, 0:512]
                        pval = pj[1][:] if kpar == 0 else pkv[:, 512:1024]

                        @on("tensor")
                        def _(t, tt4=tt4, tsl=tsl, wtok=wtok, nxtok=nxtok,
                              kpar=kpar, pgate=pgate, pval=pval):
                            t.wait_ge(dR.h, wtok[0])
                            t.wait_ge(dB.h, wtok[1])
                            t.wait_ge(dR.h, nxtok)
                            t.wait_ge(sV.h, max(gv_tok[kpar], out_pkv_tok[0]))
                            for pb, wimg, bslc in ((pgate, gwc_img, b2t[:, 0:512]),
                                                   (pval, vwc_img, b2t[:, 512:1024])):
                                t.matmul(out=pb, lhsT=la[:, tsl], rhs=bslc,
                                         start=True, stop=False)
                                t.matmul(out=pb, lhsT=lb[:, tsl], rhs=bslc,
                                         start=False, stop=False)
                                for dc in range(16):
                                    li = t.matmul(out=pb,
                                                  lhsT=nxl_img[:, dc,
                                                       tt4 * 128:(tt4 + 1) * 128],
                                                  rhs=wimg[:, dc, :],
                                                  start=False, stop=(dc == 15))
                            sT.inc(li)
                        mmtok = sT.n

                        efg = ef[0] if kpar == 0 else ef[2]
                        efv = ef[1] if kpar == 0 else ef[3]

                        @on("vector")
                        def _(v, tt4=tt4, tcol=tcol, mmtok=mmtok, kpar=kpar,
                              pgate=pgate, pval=pval, efg=efg, efv=efv):
                            v.wait_ge(sT.h, mmtok)
                            v.wait_ge(sA.h, ef_tok[kpar])
                            v.tensor_scalar(out=efg[:], in0=pgate,
                                            scalar1=AR2[:, tcol:tcol + 1],
                                            scalar2=None, op0=ALU.mult)
                            sV.inc(v.tensor_scalar(out=efv[:], in0=pval,
                                            scalar1=AR2[:, 16 + tcol:17 + tcol],
                                            scalar2=None, op0=ALU.mult))
                        vtok = sV.n
                        gv_tok[kpar] = vtok

                        @on("scalar")
                        def _(a, vtok=vtok, efg=efg):
                            a.wait_ge(sV.h, vtok)
                            sA.inc(a.activation(out=efg[:], in_=efg[:], func=AF.Silu))
                        atok = sA.n

                        @on("vector")
                        def _(v, tcol=tcol, atok=atok, kpar=kpar, efg=efg, efv=efv):
                            v.wait_ge(sA.h, atok)
                            v.wait_ge(dS.h, h_spill_tok[kpar])
                            sV.inc(v.tensor_tensor(out=hgv[kpar][:], in0=efg[:],
                                                   in1=efv[:], op=ALU.mult))
                        vtok2 = sV.n

                        @on("scalar")
                        def _(a, vtok2=vtok2, kpar=kpar, efg=efg):
                            a.wait_ge(sV.h, vtok2)
                            sA.inc(a.activation(out=efg[:], in_=hgv[kpar][:],
                                                func=AF.Abs))
                        atok2 = sA.n
                        ef_tok[kpar] = atok2

                        @on("vector")
                        def _(v, tcol=tcol, atok2=atok2, efg=efg):
                            v.wait_ge(sA.h, atok2)
                            v.tensor_reduce(out=A2[:, 0:1], in_=efg[:], axis=AX.X,
                                            op=ALU.max)
                            v.drain()
                            sV.inc(v.tensor_tensor(out=AR2[:, 32 + tcol:33 + tcol],
                                            in0=AR2[:, 32 + tcol:33 + tcol],
                                            in1=A2[:, 0:1], op=ALU.max))
                        vtok3 = sV.n

                        @on("sync")
                        def _(sy, tsl=tsl, hcol=hcol, vtok3=vtok3, kpar=kpar):
                            sy.wait_ge(sV.h, vtok3)
                            sy.dma_start(out=h_sc[tsl, hcol * 512:(hcol + 1) * 512],
                                         in_=hgv[kpar][:]).then_inc(dS.h, 16)
                            dS.n += 16
                        h_spill_tok[kpar] = dS.n

                csl = slice(32 + b3 * 4, 32 + b3 * 4 + 4)
                ssl = slice(48 + b3 * 4, 48 + b3 * 4 + 4)
                osl = slice(0 + b3 * 4, 4 + b3 * 4)

                @on("vector")
                def _(v):
                    v.wait_ge(dS.h, dS.n)
                    v.tensor_scalar(out=AR2[:, csl], in0=AR2[:, csl], scalar1=1e-5,
                                    scalar2=None, op0=ALU.max)
                    v.drain()
                    v.reciprocal(out=AR2[:, ssl], in_=AR2[:, csl])
                    v.drain()
                    v.tensor_scalar(out=AR2[:, ssl], in0=AR2[:, ssl], scalar1=127.0,
                                    scalar2=None, op0=ALU.mult)
                    v.tensor_scalar(out=AR2[:, osl], in0=AR2[:, csl],
                                    scalar1=1.0 / 127.0, scalar2=None, op0=ALU.mult)
                    v.drain()
                    v.tensor_scalar(out=AR2[:, osl], in0=AR2[:, osl],
                                    scalar1=SC[:, 2:3], scalar2=None, op0=ALU.mult)
                    v.drain()
                    v.reciprocal(out=A2[:, 4:8], in_=AR2[:, osl])
                    v.drain()
                    v.tensor_copy(out=ihi16[:, b3 * 4:b3 * 4 + 4], in_=A2[:, 4:8])
                    v.drain()
                    sV.inc(v.tensor_tensor(out=ilo16[:, b3 * 4:b3 * 4 + 4], in0=A2[:, 4:8],
                                    in1=ihi16[:, b3 * 4:b3 * 4 + 4], op=ALU.subtract))
                gam_tok = sV.n

                @on("tensor")
                def _(t, gam_tok=gam_tok):
                    t.wait_ge(sV.h, gam_tok)
                    t.wait_ge(sA.h, sA.n)
                    t.transpose(out=pxb[0:16, 0:128], in_=ihi16[:], identity=eyeb[:])
                    sT.inc(t.transpose(out=pxb[0:16, 128:256], in_=ilo16[:], identity=eyeb[:]))
                ttok = sT.n

                @on("scalar")
                def _(a, ttok=ttok):
                    a.wait_ge(sT.h, ttok)
                    a.wait_ge(dB.h, dB.n)
                    sA.inc(a.copy(out=itr[:, 0:256], in_=pxb[0:16, 0:256]))
                atok = sA.n

                @on("sync")
                def _(sy, b3=b3, r0=r0, atok=atok):
                    sy.wait_ge(sA.h, atok)
                    sy.wait_ge(sT.h, sT.n)
                    for dst, c0 in ((lao[b3 % 2], 0), (lbo[b3 % 2], 128)):
                        sy.dma_start(
                            out=dst[0:1, :].rearrange("o (c p) -> o c p", p=128),
                            in_=itr[b3 * 4:b3 * 4 + 4, c0:c0 + 128]).then_inc(dB.h, 16)
                        sy.dma_start(
                            out=dst[1:2, :].rearrange("o (c p) -> o c p", p=128),
                            in_=itr[b3 * 4:b3 * 4 + 4, (c0 + 128) % 256:(c0 + 128) % 256 + 128]
                            ).then_inc(dB.h, 16)
                        dB.n += 32

            def qout_block(b3):
                r0 = b3 * B3B
                for half in range(2):
                    for tt4 in range(4):
                        tcol = b3 * 4 + tt4
                        tsl = slice(r0 + tt4 * 128, r0 + (tt4 + 1) * 128)
                        for hp in range(4):
                            hcol = half * 8 + hp * 2
                            sl = hp % 2

                            @on("sync")
                            def _(sy, tsl=tsl, hcol=hcol, sl=sl):
                                sy.wait_ge(sV.h, lad_tok[sl])
                                sy.dma_start(out=hq[sl][:],
                                             in_=h_sc[tsl, hcol * 512:(hcol + 2) * 512]
                                             ).then_inc(dR.h, 16)
                                dR.n += 16
                            htok = dR.n

                            @on("scalar")
                            def _(a, sl=sl, tcol=tcol, htok=htok):
                                a.wait_ge(dR.h, htok)
                                a.wait_ge(sV.h, lad_tok[sl])
                                sA.inc(a.activation(out=hq[sl][:], in_=hq[sl][:],
                                                    func=AF.Identity,
                                                    scale=AR2[:, 48 + tcol:49 + tcol]))
                            hscl_tok[sl] = sA.n
                            stok = sA.n

                            @on("vector")
                            def _(e, sl=sl, stok=stok):
                                e.wait_ge(sA.h, stok)
                                e.wait_ge(sT.h, nq_tok[0])
                                e.wait_ge(dF.h, dF.n)
                                sV.inc(ladder(e, lad_v, hq[sl][:], nq[sl][:], None))
                            lad_tok[sl] = sV.n
                            etok = sV.n

                            @on("tensor")
                            def _(t, etok=etok, sl=sl):
                                t.wait_ge(sV.h, etok)
                                t.wait_ge(sA.h, sA.n)
                                for i in range(8):
                                    li = t.transpose(out=pxb[:, i * 128:(i + 1) * 128],
                                                in_=nq[sl][:, i * 128:(i + 1) * 128],
                                                identity=eyeb[:])
                                sT.inc(li)
                            nq_tok[0] = sT.n
                            ttok2 = sT.n

                            @on("scalar")
                            def _(a, hp=hp, tt4=tt4, ttok2=ttok2):
                                a.wait_ge(sT.h, ttok2)
                                dst = nhf_img[:, hp * 8:(hp + 1) * 8,
                                              tt4 * 128:(tt4 + 1) * 128]
                                sA.inc(a.copy(out=dst,
                                       in_=pxb[:].rearrange("p (c n) -> p c n", c=8)))
                    nh_tok = (sA.n, sV.n, sG.n)

                    for dcol in range(8):
                        @on("sync")
                        def _(sy, half=half, dcol=dcol):
                            sy.wait_ge(sT.h, sT.n)
                            sy.dma_start(out=owc[:],
                                         in_=owt_img[:, half * 32:(half + 1) * 32,
                                                     dcol * 256:(dcol + 1) * 256]
                                         ).then_inc(dR.h, 16)
                            sy.dma_start(out=b2o3[:],
                                         in_=ro_s[:, dcol * 256:(dcol + 1) * 256]
                                         ).then_inc(dB.h, 16)
                            dR.n += 16
                            dB.n += 16
                        otok = (dR.n, dB.n)

                        for tt4 in range(4):
                            tcol = b3 * 4 + tt4
                            tsl = slice(r0 + tt4 * 128, r0 + (tt4 + 1) * 128)
                            pb = pkv[:, tt4 * 512:tt4 * 512 + 256]

                            @on("tensor")
                            def _(t, half=half, tt4=tt4, tsl=tsl, pb=pb, otok=otok,
                                  nh_tok=nh_tok):
                                t.wait_ge(dR.h, otok[0])
                                t.wait_ge(dB.h, otok[1])
                                t.wait_ge(sA.h, nh_tok[0])
                                t.wait_ge(sV.h, sV.n)
                                first = True
                                if half == 0:
                                    lsl = slice(tt4 * 128, (tt4 + 1) * 128)
                                    t.matmul(out=pb, lhsT=lao[b3 % 2][:, lsl],
                                             rhs=b2o3[:, 0:256], start=True, stop=False)
                                    t.matmul(out=pb, lhsT=lbo[b3 % 2][:, lsl],
                                             rhs=b2o3[:, 0:256], start=False, stop=False)
                                    first = False
                                for hc in range(32):
                                    li = t.matmul(out=pb,
                                             lhsT=nhf_img[:, hc, tt4 * 128:(tt4 + 1) * 128],
                                             rhs=owc_img[:, hc, :],
                                             start=first, stop=(hc == 31))
                                    first = False
                                sT.inc(li)
                            mmtok = sT.n

                            @on("vector")
                            def _(v, half=half, tt4=tt4, dcol=dcol, tcol=tcol,
                                  pb=pb, mmtok=mmtok):
                                v.wait_ge(sT.h, mmtok)
                                dsl = facc_img[:, tt4, dcol * 256:(dcol + 1) * 256]
                                if half == 0:
                                    sV.inc(v.tensor_copy(out=dsl, in_=pb))
                                else:
                                    v.wait_ge(dF.h, dF.n)
                                    v.tensor_tensor(out=lad_v[2][:, 0:256], in0=pb,
                                                    in1=dsl, op=ALU.add)
                                    sV.inc(v.tensor_scalar(out=lad_v[3][:, 0:256],
                                                    in0=lad_v[2][:, 0:256],
                                                    scalar1=AR2[:, tcol:tcol + 1],
                                                    scalar2=None, op0=ALU.mult))
                            vtok = sV.n

                            out_pkv_tok[0] = sV.n
                            if half == 1:
                                @on("sync")
                                def _(sy, tsl=tsl, dcol=dcol, vtok=vtok):
                                    sy.wait_ge(sV.h, vtok)
                                    sy.dma_start(out=ffn_s[tsl, dcol * 256:(dcol + 1) * 256],
                                                 in_=lad_v[3][:, 0:256]).then_inc(dF.h, 16)
                                    dF.n += 16


            for b3 in range(NB3B):
                gv_block(b3)
                if b3 > 0:
                    qout_block(b3 - 1)
            qout_block(NB3B - 1)

        def emit_p3a(P):
            (q_st, qfh, qfl, xfh, xfl, at1, u_tm, mfh, mfl, wstg, wc_hi, wc_lo,
             ef32, ytile, xtile, ftile, b2u, b2o) = P
            xnhs_img = xn_hi_s.rearrange("p (c n) -> p c n", c=16)
            xnls_img = xn_lo_s.rearrange("p (c n) -> p c n", c=16)
            bank_tok = [(0, 0), (0, 0)]
            mb_tok = [0, 0]
            qpx_tok = [0, 0]
            wpar_tok = [0, 0]
            stg_tok = [(0, 0), (0, 0)]
            for b4 in range(NB3A):
                r0 = b4 * B3A
                t1_toks = []
                for tl in range(2):
                    rr = r0 + tl * 128

                    @on("sync")
                    def _(sy, rr=rr, tl=tl):
                        sy.wait_ge(sT.h, sT.n)
                        sy.dma_start(out=xfh[tl][:], in_=xnhs_img[:, :, rr:rr + 128]
                                     ).then_inc(dR.h, 16)
                        sy.dma_start(out=xfl[tl][:], in_=xnls_img[:, :, rr:rr + 128]
                                     ).then_inc(dR.h, 16)
                        dR.n += 32

                    for qsrc, dst in ((q_hi_s, qfh[tl]), (q_lo_s, qfl[tl])):
                        @on("sync")
                        def _(sy, qsrc=qsrc, rr=rr):
                            sy.wait_ge(sT.h, sT.n)
                            sy.dma_start(out=q_st[:], in_=qsrc[rr:rr + 128, :]
                                         ).then_inc(dR.h, 16)
                            dR.n += 16
                        qtok = dR.n
                        for grp in range(4):
                            ph_ = grp % 2

                            @on("tensor")
                            def _(t, grp=grp, qtok=qtok, ph_=ph_):
                                t.wait_ge(dR.h, qtok)
                                t.wait_ge(sA.h, qpx_tok[ph_])
                                for i in range(4):
                                    c = grp * 4 + i
                                    li = t.transpose(
                                        out=pxb[:, (ph_ * 4 + i) * 128:(ph_ * 4 + i + 1) * 128],
                                        in_=q_st[:, c * 128:(c + 1) * 128],
                                        identity=eyeb[:])
                                sT.inc(li)
                            ttok = sT.n

                            @on("scalar")
                            def _(a, dst=dst, grp=grp, ttok=ttok, ph_=ph_):
                                a.wait_ge(sT.h, ttok)
                                sA.inc(a.copy(out=dst[:, grp * 512:(grp + 1) * 512],
                                       in_=pxb[:, ph_ * 512:(ph_ + 1) * 512]))
                            qpx_tok[ph_] = sA.n
                    qf_tok = sA.n

                    @on("tensor")
                    def _(t, tl=tl, qf_tok=qf_tok):
                        t.wait_ge(sA.h, qf_tok)
                        t.wait_ge(sV.h, sV.n)
                        for hd in range(16):
                            c = hd * 128
                            for pi, (lh, rh) in enumerate(((qfh[tl], kv_hi),
                                                           (qfh[tl], kv_lo),
                                                           (qfl[tl], kv_hi))):
                                li = t.matmul(out=pkv[:, c:c + 128],
                                              lhsT=lh[:, c:c + 128],
                                              rhs=rh[:, c:c + 128],
                                              start=(pi == 0 and hd % 4 == 0),
                                              stop=(pi == 2 and hd % 4 == 3))
                        sT.inc(li)
                    qkv_tok = sT.n

                    @on("vector")
                    def _(v, tl=tl, qkv_tok=qkv_tok):
                        v.wait_ge(sT.h, qkv_tok)
                        v.tensor_copy(out=at1[tl][:], in_=pkv[:])
                        v.reduce_sum(out=A2[:, 0:1], in_=at1[tl][:], axis=AX.X)
                        v.tensor_tensor(out=at1[tl][:], in0=at1[tl][:], in1=at1[tl][:],
                                        op=ALU.mult)
                        v.reduce_sum(out=A2[:, 1:2], in_=at1[tl][:], axis=AX.X)
                        v.drain()
                        v.tensor_scalar(out=A2[:, 2:3], in0=A2[:, 0:1],
                                        scalar1=1.0 / HH, scalar2=None, op0=ALU.mult)
                        v.drain()
                        v.tensor_tensor(out=A2[:, 3:4], in0=A2[:, 2:3],
                                        in1=A2[:, 2:3], op=ALU.mult)
                        v.tensor_scalar(out=A2[:, 4:5], in0=A2[:, 1:2],
                                        scalar1=1.0 / HH, scalar2=None, op0=ALU.mult)
                        v.drain()
                        sV.inc(v.tensor_tensor(out=A2[:, 4:5], in0=A2[:, 4:5],
                                        in1=A2[:, 3:4], op=ALU.subtract))
                    vtok = sV.n

                    @on("scalar")
                    def _(a, vtok=vtok):
                        a.wait_ge(sV.h, vtok)
                        sA.inc(a.activation(out=A2[:, 5:6], in_=A2[:, 4:5], func=AF.Sqrt,
                                     bias=SC[:, 6:7]))
                    atok = sA.n

                    @on("vector")
                    def _(v, tl=tl, atok=atok):
                        v.wait_ge(sA.h, atok)
                        v.reciprocal(out=A2[:, 5:6], in_=A2[:, 5:6])
                        v.drain()
                        sV.inc(v.tensor_scalar(out=at1[tl][:], in0=pkv[:], scalar1=A2[:, 2:3],
                                        scalar2=A2[:, 5:6], op0=ALU.subtract,
                                        op1=ALU.mult))
                    t1_toks.append(sV.n)

                for fcol in range(4):
                    par = fcol % 2
                    splitok = emit_wsplit(wu_h, fcol, wc_hi[par], wc_lo[par],
                                          wstg, stg_tok, wpar_tok, par)

                    for tl in range(2):
                        bi = tl

                        @on("tensor")
                        def _(t, tl=tl, fcol=fcol, par=par, splitok=splitok, bi=bi):
                            t.wait_ge(sA.h, splitok[0])
                            t.wait_ge(sV.h, splitok[1])
                            t.wait_ge(sA.h, bank_tok[bi][0])
                            t.wait_ge(sV.h, bank_tok[bi][1])
                            t.matmul(out=pj[bi][:], lhsT=ones2[:],
                                     rhs=b2u[:, fcol * 512:(fcol + 1) * 512],
                                     start=True, stop=False)
                            xfh_img = xfh[tl][:].rearrange("p (c n) -> p c n", c=16)
                            xfl_img = xfl[tl][:].rearrange("p (c n) -> p c n", c=16)
                            for dc in range(16):
                                t.matmul(out=pj[bi][:], lhsT=xfh_img[:, dc, :],
                                         rhs=wc_hi[par][:, dc * 512:(dc + 1) * 512],
                                         start=False, stop=False)
                            for dc in range(16):
                                t.matmul(out=pj[bi][:], lhsT=xfh_img[:, dc, :],
                                         rhs=wc_lo[par][:, dc * 512:(dc + 1) * 512],
                                         start=False, stop=False)
                            for dc in range(16):
                                li = t.matmul(out=pj[bi][:], lhsT=xfl_img[:, dc, :],
                                         rhs=wc_hi[par][:, dc * 512:(dc + 1) * 512],
                                         start=False, stop=(dc == 15))
                            sT.inc(li)
                        mmtok = sT.n
                        wpar_tok[par] = mmtok

                        @on("scalar")
                        def _(a, tl=tl, fcol=fcol, bi=bi, mmtok=mmtok):
                            a.wait_ge(sT.h, mmtok)
                            sA.inc(a.activation(out=u_tm[tl][:, fcol * 512:(fcol + 1) * 512],
                                         in_=pj[bi][:], func=AF.Silu))
                        bank_tok[bi] = (sA.n, bank_tok[bi][1])

                u_tok = sA.n
                for tl in range(2):
                    for grp in range(4):
                        p_ = grp % 2
                        pa = ptr[:] if p_ == 0 else pkv[:, 0:512]
                        pb_ = pj[0][:] if p_ == 0 else pj[1][:]

                        @on("tensor")
                        def _(t, tl=tl, grp=grp, u_tok=u_tok, t1t=t1_toks[tl],
                              p_=p_, pa=pa, pb_=pb_):
                            t.wait_ge(sA.h, u_tok)
                            t.wait_ge(sV.h, max(t1t, mb_tok[p_]))
                            for i in range(4):
                                c = grp * 4 + i
                                t.transpose(out=pa[:, i * 128:(i + 1) * 128],
                                            in_=at1[tl][:, c * 128:(c + 1) * 128],
                                            identity=eyef[:])
                            for i in range(4):
                                c = grp * 4 + i
                                li = t.transpose(out=pb_[:, i * 128:(i + 1) * 128],
                                                 in_=u_tm[tl][:, c * 128:(c + 1) * 128],
                                                 identity=eyef[:])
                            sT.inc(li)
                        ttok = sT.n

                        @on("vector")
                        def _(v, tl=tl, grp=grp, ttok=ttok, pa=pa, pb_=pb_):
                            v.wait_ge(sT.h, ttok)
                            for i in range(4):
                                c = grp * 4 + i
                                v.tensor_scalar(out=ef32[0][:, i * 128:(i + 1) * 128],
                                                in0=pa[:, i * 128:(i + 1) * 128],
                                                scalar1=g_cols[:, c:c + 1],
                                                scalar2=b_cols[:, c:c + 1],
                                                op0=ALU.mult, op1=ALU.add)
                            v.drain()
                            v.tensor_tensor(out=ef32[1][:], in0=ef32[0][:],
                                            in1=pb_[:, 0:512], op=ALU.mult)
                            v.tensor_copy(out=mfh[tl][:, grp * 512:(grp + 1) * 512],
                                          in_=ef32[1][:])
                            sV.inc(v.tensor_tensor(
                                out=mfl[tl][:, grp * 512:(grp + 1) * 512],
                                in0=ef32[1][:],
                                in1=mfh[tl][:, grp * 512:(grp + 1) * 512],
                                op=ALU.subtract))
                        mb_tok[p_] = sV.n
                m_tok = (sV.n, sA.n)

                for dcol in range(4):
                    par = dcol % 2
                    splitok = emit_wsplit(wo_h, dcol, wc_hi[par], wc_lo[par],
                                          wstg, stg_tok, wpar_tok, par)

                    for tl in range(2):
                        rr = r0 + tl * 128
                        bi = tl

                        @on("sync")
                        def _(sy, rr=rr, dcol=dcol, tl=tl):
                            sy.wait_ge(sV.h, sV.n)
                            sy.dma_start(out=xtile[:],
                                         in_=x_h[rr:rr + 128,
                                                 dcol * 512:(dcol + 1) * 512]
                                         ).then_inc(dX.h, 16)
                            sy.dma_start(out=ftile[:],
                                         in_=ffn_s[rr:rr + 128,
                                                   dcol * 512:(dcol + 1) * 512]
                                         ).then_inc(dR.h, 16)
                            dX.n += 16
                            dR.n += 16
                        xftok = (dX.n, dR.n)

                        @on("tensor")
                        def _(t, tl=tl, dcol=dcol, par=par, splitok=splitok, bi=bi,
                              m_tok=m_tok):
                            t.wait_ge(sA.h, splitok[0])
                            t.wait_ge(sV.h, splitok[1])
                            t.wait_ge(sV.h, m_tok[0])
                            t.wait_ge(sA.h, m_tok[1])
                            t.matmul(out=pj[bi][:], lhsT=ones2[:],
                                     rhs=b2o[:, dcol * 512:(dcol + 1) * 512],
                                     start=True, stop=False)
                            for hc in range(16):
                                t.matmul(out=pj[bi][:],
                                         lhsT=mfh[tl][:, hc * 128:(hc + 1) * 128],
                                         rhs=wc_hi[par][:, hc * 512:(hc + 1) * 512],
                                         start=False, stop=False)
                            for hc in range(16):
                                t.matmul(out=pj[bi][:],
                                         lhsT=mfh[tl][:, hc * 128:(hc + 1) * 128],
                                         rhs=wc_lo[par][:, hc * 512:(hc + 1) * 512],
                                         start=False, stop=False)
                            for hc in range(16):
                                li = t.matmul(out=pj[bi][:],
                                         lhsT=mfl[tl][:, hc * 128:(hc + 1) * 128],
                                         rhs=wc_hi[par][:, hc * 512:(hc + 1) * 512],
                                         start=False, stop=(hc == 15))
                            sT.inc(li)
                        mmtok = sT.n
                        wpar_tok[par] = mmtok

                        @on("vector")
                        def _(v, tl=tl, bi=bi, mmtok=mmtok, xftok=xftok):
                            v.wait_ge(sT.h, mmtok)
                            v.wait_ge(dX.h, xftok[0])
                            v.wait_ge(dR.h, xftok[1])
                            v.wait_ge(dO.h, dO.n)
                            v.tensor_tensor(out=ytile[:], in0=pj[bi][:],
                                            in1=xtile[:], op=ALU.add)
                            sV.inc(v.tensor_tensor(out=ytile[:], in0=ytile[:],
                                            in1=ftile[:], op=ALU.add))
                        bank_tok[bi] = (bank_tok[bi][0], sV.n)
                        vtok = sV.n

                        @on("sync")
                        def _(sy, rr=rr, dcol=dcol, tl=tl, vtok=vtok):
                            sy.wait_ge(sV.h, vtok)
                            sy.dma_start(out=y_h[rr:rr + 128,
                                                 dcol * 512:(dcol + 1) * 512],
                                         in_=ytile[:]).then_inc(dO.h, 16)
                            dO.n += 16

        with ExitStack() as s0:
            small2k = s0.enter_context(nc.sbuf_tensor("small2k", [2, 2048], F32))
            smbf2k = s0.enter_context(nc.sbuf_tensor("smbf2k", [2, 2048], BF16))
            smlo2k = s0.enter_context(nc.sbuf_tensor("smlo2k", [1, 2048], BF16))
            emit_setup(small2k, smbf2k, smlo2k)
            barrier()
        with ExitStack() as ph:
            def sb1(name, shape, dt):
                return ph.enter_context(nc.sbuf_tensor(name, shape, dt))
            x_t = [sb1(f"x_t{i}", [128, D], F32) for i in range(2)]
            emit_p0(x_t)
            barrier()
            if upto not in ("p0",):
                xn_tm = sb1("xn_tm", [128, D], F32)
                nx_tm = sb1("nx_tm", [128, D], BF16)
                xnf_hi = sb1("xnf_hi", [128, 16 * B1], BF16)
                xnf_lo = sb1("xnf_lo", [128, 16 * B1], BF16)
                wstg = [sb1(f"wstg{i}", [128, 1024], F32) for i in range(2)]
                wc_hi = [sb1(f"wc_hi{i}", [128, 16 * 512], BF16) for i in range(2)]
                wc_lo = [sb1(f"wc_lo{i}", [128, 16 * 512], BF16) for i in range(2)]
                b2qk = sb1("b2qk", [2, HH], BF16)
                b2v = sb1("b2v", [2, HH], BF16)

                @on("sync")
                def _(sy):
                    sy.dma_start(out=b2qk[:], in_=bq2_s[:]).then_inc(dB.h, 16)
                    sy.dma_start(out=b2v[:], in_=bv2_s[:]).then_inc(dB.h, 16)
                    dB.n += 32
                b2tok = dB.n

                @on("tensor")
                def _(t, b2tok=b2tok):
                    t.wait_ge(dB.h, b2tok)

                kh_hi = sb1("kh_hi", [128, 4 * 512], BF16)
                kh_lo = sb1("kh_lo", [128, 4 * 512], BF16)
                vh_hi = sb1("vh_hi", [128, 4 * 512], BF16)
                vh_lo = sb1("vh_lo", [128, 4 * 512], BF16)
                ef32 = [sb1(f"ef32{i}", [128, 512], F32) for i in range(3)]
                lad_v = [sb1(f"lad_v{i}", [128, 512], F32) for i in range(4)]
                emit_p1((x_t, xn_tm, nx_tm, xnf_hi, xnf_lo, wstg, wc_hi, wc_lo,
                         kh_hi, kh_lo, vh_hi, vh_lo, ef32, lad_v, b2qk, b2v))
                barrier()

        if upto in ("tern", "p3b", "all"):
            with ExitStack() as ph2:
                def sb2(name, shape, dt):
                    return ph2.enter_context(nc.sbuf_tensor(name, shape, dt))
                tstr = [sb2(f"tstr{i}", [128, 2048], F32) for i in range(2)]
                tp1 = sb2("tp1", [128, 2048], F32)
                tern2 = [sb2(f"tern2{i}", [128, 2048], BF16) for i in range(2)]
                ttmp = [sb2(f"ttmp{i}", [128, 2048], BF16) for i in range(4)]
                smallF = sb2("smallF", [1, HID], F32)
                smbfF = sb2("smbfF", [1, HID], BF16)
                smloF = sb2("smloF", [1, HID], BF16)
                emit_tern((tstr, tp1, tern2, ttmp, smallF, smbfF, smloF))
                barrier()

        if upto in ("p3b", "all"):
            with ExitStack() as ph3:
                def sb3(name, shape, dt):
                    return ph3.enter_context(nc.sbuf_tensor(name, shape, dt))
                nxl = sb3("nxl", [128, 16 * 512], BF16)
                gwc = sb3("gwc", [128, 16 * 512], BF16)
                vwc = sb3("vwc", [128, 16 * 512], BF16)
                hq = [sb3(f"hq{i}", [128, 1024], F32) for i in range(2)]
                nq0 = sb3("nq0", [128, 1024], BF16)
                nq = [nq0, nq0]
                n_hf = sb3("n_hf", [128, 32 * 512], BF16)
                owc = sb3("owc", [128, 32 * 256], BF16)
                ffn_acc = sb3("ffn_acc", [128, 4 * 2048], F32)
                AR2 = sb3("AR2", [128, 64], F32)
                ef = [sb3(f"ef{i}", [128, 512], F32) for i in range(4)]
                lad_v3 = [sb3(f"lad_v3{i}", [128, 1024], F32) for i in range(4)]
                lad_g3 = None
                lao = [sb3(f"lao{i}", [2, B3B], BF16) for i in range(2)]
                lbo = [sb3(f"lbo{i}", [2, B3B], BF16) for i in range(2)]
                b2o3 = sb3("b2o3", [2, 256], BF16)
                hgv = [sb3(f"hgv{i}", [128, 512], F32) for i in range(2)]
                emit_p3b((nxl, gwc, vwc, hq, nq, n_hf, owc, ffn_acc, AR2, ef,
                          lad_v3, lad_g3, lao, lbo, b2o3, hgv))
                barrier()

        if upto == "all":
            with ExitStack() as ph4:
                def sb4(name, shape, dt):
                    return ph4.enter_context(nc.sbuf_tensor(name, shape, dt))
                q_st = sb4("q_st", [128, 2048], BF16)
                qfh0 = sb4("qfh0", [128, 2048], BF16)
                qfl0 = sb4("qfl0", [128, 2048], BF16)
                qfh = [qfh0, qfh0]
                qfl = [qfl0, qfl0]
                xfh = [sb4(f"xfh{i}", [128, 2048], BF16) for i in range(2)]
                xfl = [sb4(f"xfl{i}", [128, 2048], BF16) for i in range(2)]
                at1 = [sb4(f"at1{i}", [128, 2048], F32) for i in range(2)]
                u_tm = [sb4(f"u_tm{i}", [128, 2048], F32) for i in range(2)]
                mfh = [sb4(f"mfh{i}", [128, 2048], BF16) for i in range(2)]
                mfl = [sb4(f"mfl{i}", [128, 2048], BF16) for i in range(2)]
                wstg4 = [sb4(f"wstg4{i}", [128, 1024], F32) for i in range(2)]
                wc_hi4 = [sb4(f"wc_hi4{i}", [128, 16 * 512], BF16) for i in range(2)]
                wc_lo4 = [sb4(f"wc_lo4{i}", [128, 16 * 512], BF16) for i in range(2)]
                ef324 = [sb4(f"ef324{i}", [128, 512], F32) for i in range(2)]
                b2u = sb4("b2u", [2, HH], BF16)
                b2o = sb4("b2o", [2, D], BF16)

                @on("sync")
                def _(sy):
                    sy.dma_start(out=b2u[:], in_=bu2_s[:]).then_inc(dB.h, 16)
                    sy.dma_start(out=b2o[:], in_=bo2_s[:]).then_inc(dB.h, 16)
                    dB.n += 32
                b4tok = dB.n

                @on("tensor")
                def _(t, b4tok=b4tok):
                    t.wait_ge(dB.h, b4tok)

                ytile = sb4("ytile", [128, 512], F32)
                xtile = sb4("xtile", [128, 512], F32)
                ftile = sb4("ftile", [128, 512], F32)
                emit_p3a((q_st, qfh, qfl, xfh, xfl, at1, u_tm, mfh, mfl, wstg4,
                          wc_hi4, wc_lo4, ef324, ytile, xtile, ftile, b2u, b2o))

        @on("gpsimd")
        def _(g):
            g.wait_ge(dS.h, dS.n)
            g.wait_ge(dX.h, dX.n)
            g.wait_ge(dW.h, dW.n)
            g.wait_ge(dB.h, dB.n)
            g.wait_ge(dR.h, dR.n)
            g.wait_ge(dF.h, dF.n)
            if dO.n:
                g.wait_ge(dO.h, dO.n)

    return nc


def _in_maps(inputs):
    x = np.ascontiguousarray(np.asarray(inputs["L0_attn_x"], dtype=np.float32))
    eyef = np.eye(128, dtype=np.float32)
    eyeb = np.eye(128, dtype=ml_dtypes.bfloat16)
    shared = {
        "Wqk": np.ascontiguousarray(inputs["L0_attn_Wqk"]),
        "bqk": np.ascontiguousarray(inputs["L0_attn_bqk"]),
        "Wv": np.ascontiguousarray(inputs["L0_attn_Wv"]),
        "bv": np.ascontiguousarray(inputs["L0_attn_bv"]),
        "Wu": np.ascontiguousarray(inputs["L0_attn_Wu"]),
        "bu": np.ascontiguousarray(inputs["L0_attn_bu"]),
        "Wo": np.ascontiguousarray(inputs["L0_attn_Wo"]),
        "bo": np.ascontiguousarray(inputs["L0_attn_bo"]),
        "ln_g": np.ascontiguousarray(inputs["L0_attn_ln_g"]),
        "ln_b": np.ascontiguousarray(inputs["L0_attn_ln_b"]),
        "gate_w": np.ascontiguousarray(inputs["L0_ffn_gate_w"]),
        "gate_b": np.ascontiguousarray(inputs["L0_ffn_gate_b"]),
        "val_w": np.ascontiguousarray(inputs["L0_ffn_val_w"]),
        "val_b": np.ascontiguousarray(inputs["L0_ffn_val_b"]),
        "out_w": np.ascontiguousarray(inputs["L0_ffn_out_w"]),
        "out_b": np.ascontiguousarray(inputs["L0_ffn_out_b"]),
        "eyef": eyef, "eyeb": eyeb,
    }
    maps = []
    for c in range(8):
        b, hf = c // 2, c % 2
        own = x[b, hf * OWN:(hf + 1) * OWN]
        oth = x[b, (1 - hf) * OWN:(2 - hf) * OWN]
        m = dict(shared)
        m["x_seq"] = np.ascontiguousarray(np.concatenate([own, oth], axis=0))
        maps.append(m)
    return maps


def kernel(**inputs):
    nc = build_program()
    maps = _in_maps(inputs)
    res = run_bass_kernel_spmd(nc, maps, core_ids=list(range(8)))
    y = np.empty((4, SEQ, D), np.float32)
    for c in range(8):
        b, hf = c // 2, c % 2
        y[b, hf * OWN:(hf + 1) * OWN] = res.results[c]["y"]
    return y


# version 15
# speedup vs baseline: 1.0230x; 1.0230x over previous
import numpy as np
import ml_dtypes
import concourse.bass as bass
import concourse.mybir as mybir
from concourse.bass_utils import run_bass_kernel_spmd
from contextlib import ExitStack

F32 = mybir.dt.float32
BF16 = mybir.dt.bfloat16
AF = mybir.ActivationFunctionType
ALU = mybir.AluOpType
AX = mybir.AxisListType

SEQ, OWN, D, HH, HID, HEADS = 4096, 2048, 2048, 2048, 8192, 16
LN_EPS = 1e-5
B1 = 512
NB1 = SEQ // B1
B3B = 512
NB3B = OWN // B3B
B3A = 256
NB3A = OWN // B3A
LADDER = (128.0, 64.0, 32.0, 16.0, 8.0, 4.0, 2.0, 1.0)


class Sem:
    def __init__(self, es, nc, name):
        self.h = es.enter_context(nc.semaphore(name))
        self.n = 0

    def inc(self, inst, k=1):
        inst.then_inc(self.h, k)
        self.n += k
        return self.n

    def mark(self, eng):
        eng.drain()
        eng.sem_inc(self.h, 1)
        self.n += 1
        return self.n


def build_program(upto="all", debug=False):
    nc = bass.Bass("TRN2", target_bir_lowering=False)
    dk = "ExternalOutput" if debug else "Internal"

    x_h = nc.dram_tensor("x_seq", [SEQ, D], F32, kind="ExternalInput").ap()
    wqk_h = nc.dram_tensor("Wqk", [D, HH], F32, kind="ExternalInput").ap()
    bqk_h = nc.dram_tensor("bqk", [HH], F32, kind="ExternalInput").ap()
    wv_h = nc.dram_tensor("Wv", [D, HH], F32, kind="ExternalInput").ap()
    bv_h = nc.dram_tensor("bv", [HH], F32, kind="ExternalInput").ap()
    wu_h = nc.dram_tensor("Wu", [D, HH], F32, kind="ExternalInput").ap()
    bu_h = nc.dram_tensor("bu", [HH], F32, kind="ExternalInput").ap()
    wo_h = nc.dram_tensor("Wo", [HH, D], F32, kind="ExternalInput").ap()
    bo_h = nc.dram_tensor("bo", [D], F32, kind="ExternalInput").ap()
    lng_h = nc.dram_tensor("ln_g", [HH], F32, kind="ExternalInput").ap()
    lnb_h = nc.dram_tensor("ln_b", [HH], F32, kind="ExternalInput").ap()
    gw_h = nc.dram_tensor("gate_w", [HID, D], F32, kind="ExternalInput").ap()
    gb_h = nc.dram_tensor("gate_b", [HID], F32, kind="ExternalInput").ap()
    vw_h = nc.dram_tensor("val_w", [HID, D], F32, kind="ExternalInput").ap()
    vb_h = nc.dram_tensor("val_b", [HID], F32, kind="ExternalInput").ap()
    ow_h = nc.dram_tensor("out_w", [D, HID], F32, kind="ExternalInput").ap()
    ob_h = nc.dram_tensor("out_b", [D], F32, kind="ExternalInput").ap()
    eyef_h = nc.dram_tensor("eyef", [128, 128], F32, kind="ExternalInput").ap()
    eyeb_h = nc.dram_tensor("eyeb", [128, 128], BF16, kind="ExternalInput").ap()
    y_h = nc.dram_tensor("y", [OWN, D], F32, kind="ExternalOutput").ap()

    q_hi_s = nc.dram_tensor("q_hi_s", [OWN, HH], BF16, kind=dk).ap()
    q_lo_s = nc.dram_tensor("q_lo_s", [OWN, HH], BF16, kind=dk).ap()
    xn_hi_s = nc.dram_tensor("xn_hi_s", [128, 16 * OWN], BF16, kind=dk).ap()
    xn_lo_s = nc.dram_tensor("xn_lo_s", [128, 16 * OWN], BF16, kind=dk).ap()
    nx_s = nc.dram_tensor("nx_s", [128, 16 * OWN], BF16, kind=dk).ap()
    gwt_s = nc.dram_tensor("gwt_s", [128, 16 * HID], BF16, kind=dk).ap()
    vwt_s = nc.dram_tensor("vwt_s", [128, 16 * HID], BF16, kind=dk).ap()
    owt_s = nc.dram_tensor("owt_s", [128, 64 * D], BF16, kind=dk).ap()
    h_sc = nc.dram_tensor("h_sc", [OWN, HID], F32, kind=dk).ap()
    ffn_s = nc.dram_tensor("ffn_s", [OWN, D], F32, kind=dk).ap()
    bq2_s = nc.dram_tensor("bq2_s", [2, HH], BF16, kind=dk).ap()
    bv2_s = nc.dram_tensor("bv2_s", [2, HH], BF16, kind=dk).ap()
    bu2_s = nc.dram_tensor("bu2_s", [2, HH], BF16, kind=dk).ap()
    bo2_s = nc.dram_tensor("bo2_s", [2, D], BF16, kind=dk).ap()
    rg_s = nc.dram_tensor("rg_s", [2, HID], BF16, kind=dk).ap()
    rv_s = nc.dram_tensor("rv_s", [2, HID], BF16, kind=dk).ap()
    ro_s = nc.dram_tensor("ro_s", [2, D], BF16, kind=dk).ap()
    if debug:
        kv_dbg = nc.dram_tensor("kv_dbg", [128, HH], F32, kind="ExternalOutput").ap()
        ar_dbg = nc.dram_tensor("ar_dbg", [128, 64], F32, kind="ExternalOutput").ap()
        sc_dbg = nc.dram_tensor("sc_dbg", [128, 8], F32, kind="ExternalOutput").ap()

    es = ExitStack()
    with es:
        def sb(name, shape, dt):
            return es.enter_context(nc.sbuf_tensor(name, shape, dt))

        kv_acc = sb("kv_acc", [128, HH], F32)
        kv_hi = sb("kv_hi", [128, HH], BF16)
        kv_lo = sb("kv_lo", [128, HH], BF16)
        AR = sb("AR", [128, 64], F32)
        A2 = sb("A2", [128, 8], F32)
        SC = sb("SC", [128, 8], F32)
        onesc = sb("onesc", [128, 1], F32)
        ones1 = sb("ones1", [1, 128], F32)
        ones2 = sb("ones2", [2, 128], BF16)
        eyef = sb("eyef_t", [128, 128], F32)
        eyeb = sb("eyeb_t", [128, 128], BF16)
        b2t = sb("b2t", [2, 1024], BF16)
        la = sb("la", [2, OWN], BF16)
        lb = sb("lb", [2, OWN], BF16)
        g_cols = sb("g_cols", [128, 16], F32)
        b_cols = sb("b_cols", [128, 16], F32)
        ihi16 = sb("ihi16", [128, 16], BF16)
        ilo16 = sb("ilo16", [128, 16], BF16)
        itr = sb("itr", [16, 256], BF16)

        pj = [es.enter_context(nc.psum_tensor(f"pj{i}", [128, 512], F32)) for i in range(2)]
        pkv = es.enter_context(nc.psum_tensor("pkv", [128, HH], F32))
        ptr = es.enter_context(nc.psum_tensor("ptr", [128, 512], F32))
        pxb = es.enter_context(nc.psum_tensor("pxb", [128, 1024], BF16))

        dX = Sem(es, nc, "dX")
        dW = Sem(es, nc, "dW")
        dS = Sem(es, nc, "dS")
        dR = Sem(es, nc, "dR")
        dO = Sem(es, nc, "dO")
        dB = Sem(es, nc, "dB")
        dF = Sem(es, nc, "dF")
        sV = Sem(es, nc, "sV")
        sA = Sem(es, nc, "sA")
        sG = Sem(es, nc, "sG")
        sT = Sem(es, nc, "sT")

        block = es.enter_context(nc.Block())

        def on(name):
            return getattr(block, name)

        def barrier():
            toks = {}

            @on("vector")
            def _(v):
                toks["v"] = sV.mark(v)

            @on("scalar")
            def _(a):
                toks["a"] = sA.mark(a)

            @on("gpsimd")
            def _(g):
                toks["g"] = sG.mark(g)

            @on("tensor")
            def _(t):
                toks["t"] = sT.mark(t)

            for en in ("vector", "scalar", "gpsimd", "tensor", "sync"):
                @on(en)
                def _(e):
                    e.wait_ge(sV.h, toks["v"])
                    e.wait_ge(sA.h, toks["a"])
                    e.wait_ge(sG.h, toks["g"])
                    e.wait_ge(sT.h, toks["t"])

        def ladder(eng, scr, src_ap, dst_ap, scale_ap):
            s2, aa, dd, md = scr
            n = src_ap.shape[-1]
            s2v, aav, ddv, mdv = s2[:, 0:n], aa[:, 0:n], dd[:, 0:n], md[:, 0:n]
            if scale_ap is not None:
                eng.tensor_scalar(out=mdv, in0=src_ap, scalar1=scale_ap,
                                  scalar2=None, op0=ALU.mult)
                t = mdv
            else:
                t = src_ap
            eng.tensor_scalar(out=s2v, in0=t, scalar1=0.0, scalar2=2.0,
                              op0=ALU.is_ge, op1=ALU.mult)
            eng.tensor_tensor(out=aav, in0=t, in1=s2v, op=ALU.mult)
            eng.tensor_tensor(out=aav, in0=aav, in1=t, op=ALU.subtract)
            eng.tensor_copy(out=ddv, in_=aav)
            for k in LADDER:
                eng.tensor_scalar(out=mdv, in0=ddv, scalar1=k - 0.5, scalar2=-k,
                                  op0=ALU.is_ge, op1=ALU.mult)
                eng.tensor_tensor(out=ddv, in0=ddv, in1=mdv, op=ALU.add)
            eng.tensor_tensor(out=aav, in0=aav, in1=ddv, op=ALU.subtract)
            eng.tensor_tensor(out=mdv, in0=aav, in1=s2v, op=ALU.mult)
            return eng.tensor_tensor(out=dst_ap, in0=mdv, in1=aav, op=ALU.subtract)

        def emit_setup(small, smbf, smlo):
            @on("vector")
            def _(v):
                v.memset(onesc[:], 1.0)
                v.memset(SC[:, 6:7], LN_EPS)
                v.memset(ones1[:], 1.0)
                v.memset(ones2[:], 1.0)
                v.memset(kv_acc[:], 0.0)
                v.memset(AR[:], 0.0)

            @on("sync")
            def _(s):
                s.dma_start(out=eyef[:], in_=eyef_h[:]).then_inc(dB.h, 16)
                s.dma_start(out=eyeb[:], in_=eyeb_h[:]).then_inc(dB.h, 16)
                s.dma_start(out=g_cols[:],
                            in_=lng_h.rearrange("(c p) -> p c", p=128),
                            allow_slow_non_contiguous=True).then_inc(dB.h, 16)
                s.dma_start(out=b_cols[:],
                            in_=lnb_h.rearrange("(c p) -> p c", p=128),
                            allow_slow_non_contiguous=True).then_inc(dB.h, 16)
                dB.n += 64

            for src, dst, n in ((bqk_h, bq2_s, HH), (bv_h, bv2_s, HH),
                                (bu_h, bu2_s, HH), (bo_h, bo2_s, D)):
                @on("sync")
                def _(s, src=src, n=n):
                    s.wait_ge(sV.h, sV.n)
                    s.dma_start(out=small[0:1, 0:n], in_=src[None, :]).then_inc(dB.h, 16)
                    dB.n += 16
                tok = dB.n

                @on("vector")
                def _(v, n=n, tok=tok):
                    v.wait_ge(dB.h, tok)
                    v.tensor_copy(out=smbf[0:1, 0:n], in_=small[0:1, 0:n])
                    sV.inc(v.tensor_tensor(out=smlo[0:1, 0:n], in0=small[0:1, 0:n],
                                    in1=smbf[0:1, 0:n], op=ALU.subtract))
                vtok = sV.n

                @on("sync")
                def _(s, dst=dst, n=n, vtok=vtok):
                    s.wait_ge(sV.h, vtok)
                    s.dma_start(out=dst[0:1, 0:n], in_=smbf[0:1, 0:n]).then_inc(dB.h, 16)
                    s.dma_start(out=dst[1:2, 0:n], in_=smlo[0:1, 0:n]).then_inc(dB.h, 16)
                    dB.n += 32
                tok2 = dB.n

                @on("vector")
                def _(v, tok2=tok2):
                    v.wait_ge(dB.h, tok2)
                    sV.mark(v)

        def emit_p0(bw):
            plans = []
            for idx, (wh, R, C) in (((1, (vw_h, HID, D))), ((0, (gw_h, HID, D))),
                                    ((2, (ow_h, D, HID)))):
                tiles = []
                for rt in range(R // 128):
                    for cq in range(C // 2048):
                        tiles.append(wh[rt * 128:(rt + 1) * 128,
                                        cq * 2048:(cq + 1) * 2048])
                plans.append((idx, R * C, tiles))

            for idx, nel, tiles in plans:
                @on("vector")
                def _(v):
                    sV.inc(v.memset(A2[:, 0:1], 0.0))

                red_tok = [sV.n, sV.n]
                for it, tile in enumerate(tiles):
                    buf = bw[it % 2]

                    @on("sync")
                    def _(s, tile=tile, buf=buf, it=it):
                        s.wait_ge(sV.h, red_tok[it % 2])
                        s.dma_start(out=buf[:, 0:2048], in_=tile).then_inc(dX.h, 16)
                        dX.n += 16
                    tok = dX.n

                    @on("vector")
                    def _(v, buf=buf, tok=tok):
                        v.wait_ge(dX.h, tok)
                        v.tensor_reduce(out=A2[:, 1:2], in_=buf[:, 0:2048], axis=AX.X,
                                        op=ALU.add, apply_absolute_value=True)
                        v.drain()
                        sV.inc(v.tensor_tensor(out=A2[:, 0:1], in0=A2[:, 0:1],
                                               in1=A2[:, 1:2], op=ALU.add))
                    red_tok[it % 2] = sV.n

                @on("tensor")
                def _(t):
                    t.wait_ge(sV.h, sV.n)
                    sT.inc(t.matmul(out=ptr[0:1, 0:1], lhsT=onesc[:],
                                    rhs=A2[:, 0:1], start=True, stop=True))
                ttok = sT.n

                @on("vector")
                def _(v, nel=nel, ttok=ttok):
                    v.wait_ge(sT.h, ttok)
                    sV.inc(v.tensor_scalar(out=A2[0:1, 2:3], in0=ptr[0:1, 0:1],
                                    scalar1=1.0 / nel, scalar2=1e-5,
                                    op0=ALU.mult, op1=ALU.max))
                vtok = sV.n

                @on("tensor")
                def _(t, vtok=vtok):
                    t.wait_ge(sV.h, vtok)
                    sT.inc(t.matmul(out=ptr[:, 1:2], lhsT=ones1[:],
                                    rhs=A2[0:1, 2:3], start=True, stop=True))
                ttok2 = sT.n

                @on("vector")
                def _(v, idx=idx, ttok2=ttok2):
                    v.wait_ge(sT.h, ttok2)
                    v.tensor_copy(out=SC[:, idx:idx + 1], in_=ptr[:, 1:2])
                    v.drain()
                    sV.inc(v.reciprocal(out=SC[:, 3 + idx:4 + idx], in_=SC[:, idx:idx + 1]))


        def emit_wsplit(wh, fcol, whi, wlo, wstg, stg_tok, wpar_tok, par):
            for q in range(8):
                sl = q % 2
                stg = wstg[sl]

                @on("sync")
                def _(sy, q=q, sl=sl, stg=stg, wh=wh, fcol=fcol):
                    sy.wait_ge(sA.h, stg_tok[sl][0])
                    sy.wait_ge(sV.h, stg_tok[sl][1])
                    src = wh[q * 256:(q + 1) * 256, fcol * 512:(fcol + 1) * 512]
                    sy.dma_start(out=stg[:].rearrange("p (c n) -> p c n", c=2),
                                 in_=src.rearrange("(c p) n -> p c n", p=128)
                                 ).then_inc(dW.h, 16)
                    dW.n += 16
                wtok = dW.n

                @on("scalar")
                def _(a, q=q, stg=stg, whi=whi, wtok=wtok, par=par):
                    a.wait_ge(dW.h, wtok)
                    a.wait_ge(sT.h, wpar_tok[par])
                    sA.inc(a.copy(out=whi[:, q * 1024:(q + 1) * 1024], in_=stg[:]))
                atok = sA.n

                @on("vector")
                def _(v, q=q, stg=stg, whi=whi, wlo=wlo, atok=atok):
                    v.wait_ge(sA.h, atok)
                    sV.inc(v.tensor_tensor(out=wlo[:, q * 1024:(q + 1) * 1024],
                                           in0=stg[:],
                                           in1=whi[:, q * 1024:(q + 1) * 1024],
                                           op=ALU.subtract))
                stg_tok[sl] = (atok, sV.n)
            return (sA.n, sV.n)

        def emit_p1(P):
            (x_t, xn_tm, nx_tm, xnf_hi, xnf_lo, wstg, wc_hi, wc_lo,
             kh_hi, kh_lo, vh_hi, vh_lo, ef32, lad_v, b2qk, b2v) = P
            nsub = B1 // 128
            xn_img_hi = xnf_hi[:].rearrange("p (c n) -> p c n", c=16)
            xn_img_lo = xnf_lo[:].rearrange("p (c n) -> p c n", c=16)
            nxs_img = nx_s.rearrange("p (c n) -> p c n", c=16)
            xnhs_img = xn_hi_s.rearrange("p (c n) -> p c n", c=16)
            xnls_img = xn_lo_s.rearrange("p (c n) -> p c n", c=16)
            ptr_img = ptr[:].rearrange("p (c n) -> p c n", c=4)
            x_tok = [sV.n, sV.n]
            q_spill_tok = [0, 0, 0, 0]
            xn_spill_tok = 0
            nx_spill_tok = 0
            exp_tok = [0]
            sil_tok = [0]
            bank_tok = [(0, 0), (0, 0)]
            wpar_tok = [0, 0]
            stg_tok = [(0, 0), (0, 0)]
            pend_split = [None, None]
            pend_split[0] = emit_wsplit(wqk_h, 0, wc_hi[0], wc_lo[0], wstg,
                                        stg_tok, wpar_tok, 0)

            for b in range(NB1):
                own = b < OWN // B1
                for s in range(nsub):
                    xb = x_t[s % 2]
                    row0 = b * B1 + s * 128
                    tcol = row0 // 128 if own else 0

                    @on("sync")
                    def _(sy, xb=xb, row0=row0, s=s):
                        sy.wait_ge(sV.h, x_tok[s % 2])
                        sy.dma_start(out=xb[:], in_=x_h[row0:row0 + 128, :]).then_inc(dX.h, 16)
                        dX.n += 16
                    xtok = dX.n

                    @on("vector")
                    def _(v, xb=xb, xtok=xtok):
                        v.wait_ge(dX.h, xtok)
                        v.wait_ge(sT.h, sT.n)
                        v.wait_ge(sG.h, sG.n)
                        v.reduce_sum(out=A2[:, 0:1], in_=xb[:], axis=AX.X)
                        v.tensor_tensor(out=xn_tm[:], in0=xb[:], in1=xb[:], op=ALU.mult)
                        v.reduce_sum(out=A2[:, 1:2], in_=xn_tm[:], axis=AX.X)
                        v.drain()
                        v.tensor_scalar(out=A2[:, 2:3], in0=A2[:, 0:1],
                                        scalar1=1.0 / D, scalar2=None, op0=ALU.mult)
                        v.drain()
                        v.tensor_tensor(out=A2[:, 3:4], in0=A2[:, 2:3],
                                        in1=A2[:, 2:3], op=ALU.mult)
                        v.tensor_scalar(out=A2[:, 4:5], in0=A2[:, 1:2],
                                        scalar1=1.0 / D, scalar2=None, op0=ALU.mult)
                        v.drain()
                        sV.inc(v.tensor_tensor(out=A2[:, 4:5], in0=A2[:, 4:5],
                                        in1=A2[:, 3:4], op=ALU.subtract))
                    vtok = sV.n

                    @on("scalar")
                    def _(a, vtok=vtok):
                        a.wait_ge(sV.h, vtok)
                        sA.inc(a.activation(out=A2[:, 5:6], in_=A2[:, 4:5], func=AF.Sqrt,
                                     bias=SC[:, 6:7]))
                    atok = sA.n

                    @on("vector")
                    def _(v, xb=xb, atok=atok, tcol=tcol, own=own):
                        v.wait_ge(sA.h, atok)
                        if own:
                            v.wait_ge(dS.h, nx_spill_tok)
                        v.reciprocal(out=A2[:, 5:6], in_=A2[:, 5:6])
                        v.drain()
                        li = v.tensor_scalar(out=xn_tm[:], in0=xb[:], scalar1=A2[:, 2:3],
                                             scalar2=A2[:, 5:6], op0=ALU.subtract,
                                             op1=ALU.mult)
                        if own:
                            v.tensor_reduce(out=A2[:, 6:7], in_=xn_tm[:], axis=AX.X,
                                            op=ALU.max, apply_absolute_value=True)
                            v.drain()
                            v.tensor_scalar(out=AR[:, tcol:tcol + 1], in0=A2[:, 6:7],
                                            scalar1=1e-5, scalar2=None, op0=ALU.max)
                            v.drain()
                            v.reciprocal(out=A2[:, 7:8], in_=AR[:, tcol:tcol + 1])
                            v.drain()
                            v.tensor_scalar(out=AR[:, 16 + tcol:17 + tcol],
                                            in0=A2[:, 7:8], scalar1=127.0,
                                            scalar2=None, op0=ALU.mult)
                            v.tensor_scalar(out=AR[:, 32 + tcol:33 + tcol],
                                            in0=AR[:, tcol:tcol + 1],
                                            scalar1=1.0 / 127.0, scalar2=None,
                                            op0=ALU.mult)
                            v.drain()
                            for j in range(4):
                                li = ladder(v, lad_v, xn_tm[:, j * 512:(j + 1) * 512],
                                            nx_tm[:, j * 512:(j + 1) * 512],
                                            AR[:, 16 + tcol:17 + tcol])
                        sV.inc(li)
                    lntok = sV.n
                    x_tok[s % 2] = lntok

                    for grp in range(4):
                        @on("tensor")
                        def _(t, grp=grp, lntok=lntok):
                            t.wait_ge(sV.h, lntok)
                            t.wait_ge(sA.h, sA.n)
                            t.wait_ge(sV.h, sV.n)
                            for i in range(4):
                                c = grp * 4 + i
                                li = t.transpose(out=ptr[:, i * 128:(i + 1) * 128],
                                            in_=xn_tm[:, c * 128:(c + 1) * 128],
                                            identity=eyef[:])
                            sT.inc(li)
                        ttok = sT.n

                        @on("scalar")
                        def _(a, grp=grp, s=s, ttok=ttok):
                            a.wait_ge(sT.h, ttok)
                            a.wait_ge(dS.h, xn_spill_tok)
                            sA.inc(a.copy(out=xn_img_hi[:, grp * 4:(grp + 1) * 4,
                                                 s * 128:(s + 1) * 128],
                                   in_=ptr_img))
                        atok2 = sA.n

                        @on("vector")
                        def _(v, grp=grp, s=s, ttok=ttok, atok2=atok2):
                            v.wait_ge(sT.h, ttok)
                            v.wait_ge(sA.h, atok2)
                            sV.inc(v.tensor_tensor(out=xn_img_lo[:, grp * 4:(grp + 1) * 4,
                                                          s * 128:(s + 1) * 128],
                                            in0=ptr_img,
                                            in1=xn_img_hi[:, grp * 4:(grp + 1) * 4,
                                                          s * 128:(s + 1) * 128],
                                            op=ALU.subtract))

                    if own:
                        for grp in range(4):
                            @on("tensor")
                            def _(t, grp=grp, lntok=lntok):
                                t.wait_ge(sV.h, lntok)
                                t.wait_ge(sA.h, sA.n)
                                for i in range(4):
                                    c = grp * 4 + i
                                    li = t.transpose(out=pxb[:, i * 128:(i + 1) * 128],
                                                in_=nx_tm[:, c * 128:(c + 1) * 128],
                                                identity=eyeb[:])
                                sT.inc(li)
                            ttok = sT.n

                            @on("scalar")
                            def _(a, grp=grp, ttok=ttok):
                                a.wait_ge(sT.h, ttok)
                                sA.inc(a.copy(out=nx_tm[:, grp * 512:(grp + 1) * 512],
                                       in_=pxb[:, 0:512]))
                        gtok = sA.n

                        @on("sync")
                        def _(sy, row0=row0, gtok=gtok):
                            sy.wait_ge(sA.h, gtok)
                            sy.dma_start(out=nxs_img[:, :, row0:row0 + 128],
                                         in_=nx_tm[:].rearrange("p (c n) -> p c n",
                                                                c=16)).then_inc(dS.h, 16)
                            dS.n += 16
                        nx_spill_tok = dS.n

                        @on("sync")
                        def _(sy, s=s, row0=row0):
                            sy.wait_ge(sV.h, sV.n)
                            sy.wait_ge(sA.h, sA.n)
                            sy.dma_start(out=xnhs_img[:, :, row0:row0 + 128],
                                         in_=xn_img_hi[:, :, s * 128:(s + 1) * 128]
                                         ).then_inc(dS.h, 16)
                            sy.dma_start(out=xnls_img[:, :, row0:row0 + 128],
                                         in_=xn_img_lo[:, :, s * 128:(s + 1) * 128]
                                         ).then_inc(dS.h, 16)
                            dS.n += 32
                        xn_spill_tok = dS.n

                for fcol in range(4):
                    for wi, (wh, b2w) in enumerate(((wqk_h, b2qk), (wv_h, b2v))):
                        gidx = b * 8 + fcol * 2 + wi
                        par = gidx % 2
                        whi, wlo = wc_hi[par], wc_lo[par]
                        splitok = pend_split[par]

                        for tt in range(4):
                            bi = tt % 2
                            pb = pj[bi]

                            @on("tensor")
                            def _(t, tt=tt, fcol=fcol, b2w=b2w, whi=whi, wlo=wlo,
                                  pb=pb, splitok=splitok, bi=bi):
                                t.wait_ge(sA.h, splitok[0])
                                t.wait_ge(sV.h, splitok[1])
                                t.wait_ge(sV.h, bank_tok[bi][0])
                                t.wait_ge(sA.h, bank_tok[bi][1])
                                t.matmul(out=pb[:], lhsT=ones2[:],
                                         rhs=b2w[:, fcol * 512:(fcol + 1) * 512],
                                         start=True, stop=False)
                                for dc in range(16):
                                    t.matmul(out=pb[:],
                                             lhsT=xn_img_hi[:, dc, tt * 128:(tt + 1) * 128],
                                             rhs=whi[:, dc * 512:(dc + 1) * 512],
                                             start=False, stop=False)
                                for dc in range(16):
                                    t.matmul(out=pb[:],
                                             lhsT=xn_img_hi[:, dc, tt * 128:(tt + 1) * 128],
                                             rhs=wlo[:, dc * 512:(dc + 1) * 512],
                                             start=False, stop=False)
                                for dc in range(16):
                                    li = t.matmul(out=pb[:],
                                             lhsT=xn_img_lo[:, dc, tt * 128:(tt + 1) * 128],
                                             rhs=whi[:, dc * 512:(dc + 1) * 512],
                                             start=False, stop=(dc == 15))
                                sT.inc(li)
                            mmtok = sT.n
                            wpar_tok[par] = mmtok

                            if tt == 0 and gidx + 1 < NB1 * 8:
                                nb, nrem = divmod(gidx + 1, 8)
                                nfc, nwi = divmod(nrem, 2)
                                nwh = wqk_h if nwi == 0 else wv_h
                                pend_split[(gidx + 1) % 2] = emit_wsplit(
                                    nwh, nfc, wc_hi[(gidx + 1) % 2],
                                    wc_lo[(gidx + 1) % 2], wstg, stg_tok,
                                    wpar_tok, (gidx + 1) % 2)

                            if wi == 0:
                                @on("vector")
                                def _(v, pb=pb, mmtok=mmtok):
                                    v.wait_ge(sT.h, mmtok)
                                    v.wait_ge(sA.h, exp_tok[0])
                                    v.tensor_scalar(out=ef32[0][:], in0=pb[:],
                                                    scalar1=0.0, scalar2=None,
                                                    op0=ALU.min)
                                    sV.inc(v.tensor_scalar(out=ef32[1][:], in0=pb[:],
                                                           scalar1=0.0, scalar2=None,
                                                           op0=ALU.max))
                                vtok = sV.n
                                bank_tok[bi] = (vtok, 0)

                                @on("scalar")
                                def _(a, vtok=vtok):
                                    a.wait_ge(sV.h, vtok)
                                    sA.inc(a.activation(out=ef32[0][:], in_=ef32[0][:],
                                                        func=AF.Exp))
                                exp_tok[0] = sA.n
                                atok = sA.n

                                @on("vector")
                                def _(v, tt=tt, atok=atok):
                                    v.wait_ge(sA.h, atok)
                                    v.wait_ge(dS.h, q_spill_tok[tt])
                                    v.scalar_tensor_tensor(out=ef32[1][:], in0=ef32[0][:],
                                                           scalar=-1.0, in1=ef32[1][:],
                                                           op0=ALU.add, op1=ALU.add)
                                    v.tensor_copy(out=kh_hi[:, tt * 512:(tt + 1) * 512],
                                                  in_=ef32[1][:])
                                    sV.inc(v.tensor_tensor(out=kh_lo[:, tt * 512:(tt + 1) * 512],
                                                    in0=ef32[1][:],
                                                    in1=kh_hi[:, tt * 512:(tt + 1) * 512],
                                                    op=ALU.subtract))
                            else:
                                @on("scalar")
                                def _(a, pb=pb, tt=tt, mmtok=mmtok):
                                    a.wait_ge(sT.h, mmtok)
                                    a.wait_ge(sV.h, sil_tok[0])
                                    sA.inc(a.activation(out=ef32[2][:], in_=pb[:],
                                                        func=AF.Silu))
                                atok = sA.n
                                bank_tok[bi] = (0, atok)

                                @on("vector")
                                def _(v, tt=tt, atok=atok):
                                    v.wait_ge(sA.h, atok)
                                    v.tensor_copy(out=vh_hi[:, tt * 512:(tt + 1) * 512],
                                                  in_=ef32[2][:])
                                    sV.inc(v.tensor_tensor(out=vh_lo[:, tt * 512:(tt + 1) * 512],
                                                    in0=ef32[2][:],
                                                    in1=vh_hi[:, tt * 512:(tt + 1) * 512],
                                                    op=ALU.subtract))
                                sil_tok[0] = sV.n

                            if wi == 0 and own:
                                @on("sync")
                                def _(sy, b=b, tt=tt, fcol=fcol):
                                    sy.wait_ge(sV.h, sV.n)
                                    sy.wait_ge(sA.h, sA.n)
                                    r0 = b * B1 + tt * 128
                                    sy.dma_start(
                                        out=q_hi_s[r0:r0 + 128,
                                                   fcol * 512:(fcol + 1) * 512],
                                        in_=kh_hi[:, tt * 512:(tt + 1) * 512]
                                    ).then_inc(dS.h, 16)
                                    sy.dma_start(
                                        out=q_lo_s[r0:r0 + 128,
                                                   fcol * 512:(fcol + 1) * 512],
                                        in_=kh_lo[:, tt * 512:(tt + 1) * 512]
                                    ).then_inc(dS.h, 16)
                                    dS.n += 32
                                q_spill_tok[tt] = dS.n

                    evactok = (sV.n, sA.n)

                    @on("tensor")
                    def _(t, b=b, fcol=fcol, evactok=evactok):
                        t.wait_ge(sV.h, evactok[0])
                        t.wait_ge(sA.h, evactok[1])
                        first = True
                        for hd in range(4):
                            col = fcol * 512 + hd * 128
                            for tt in range(4):
                                for pi, (lh, rh) in enumerate(((kh_hi, vh_hi),
                                                               (kh_hi, vh_lo),
                                                               (kh_lo, vh_hi))):
                                    last = (hd == 3 and tt == 3 and pi == 2)
                                    sl = slice(tt * 512 + hd * 128,
                                               tt * 512 + (hd + 1) * 128)
                                    li = t.matmul(out=pkv[:, col:col + 128],
                                                  lhsT=lh[:, sl], rhs=rh[:, sl],
                                                  start=first, stop=last)
                                    first = False
                        sT.inc(li)

                kvtok = sT.n

                @on("vector")
                def _(v, kvtok=kvtok):
                    v.wait_ge(sT.h, kvtok)
                    sV.inc(v.tensor_tensor(out=kv_acc[:], in0=pkv[:], in1=kv_acc[:],
                                    op=ALU.add))

            @on("vector")
            def _(v):
                v.tensor_copy(out=kv_hi[:], in_=kv_acc[:])
                v.tensor_tensor(out=kv_lo[:], in0=kv_acc[:], in1=kv_hi[:],
                                op=ALU.subtract)
                v.tensor_copy(out=ihi16[:], in_=AR[:, 16:32])
                v.drain()
                sV.inc(v.tensor_tensor(out=ilo16[:], in0=AR[:, 16:32], in1=ihi16[:],
                                op=ALU.subtract))
            vtok = sV.n

            @on("tensor")
            def _(t, vtok=vtok):
                t.wait_ge(sV.h, vtok)
                t.wait_ge(sA.h, sA.n)
                t.transpose(out=pxb[0:16, 0:128], in_=ihi16[:], identity=eyeb[:])
                sT.inc(t.transpose(out=pxb[0:16, 128:256], in_=ilo16[:], identity=eyeb[:]))
            ttok = sT.n

            @on("scalar")
            def _(a, ttok=ttok):
                a.wait_ge(sT.h, ttok)
                sA.inc(a.copy(out=itr[:, 0:256], in_=pxb[0:16, 0:256]))
            atok = sA.n

            @on("sync")
            def _(sy, atok=atok):
                sy.wait_ge(sA.h, atok)
                for dst, c0s in ((la, (0, 128)), (lb, (128, 0))):
                    for r, c0 in enumerate(c0s):
                        sy.dma_start(
                            out=dst[r:r + 1, :].rearrange("o (c p) -> o c p", p=128),
                            in_=itr[:, c0:c0 + 128]).then_inc(dB.h, 16)
                        dB.n += 16

            if debug:
                @on("sync")
                def _(sy):
                    sy.wait_ge(sV.h, sV.n)
                    sy.dma_start(out=kv_dbg, in_=kv_acc[:]).then_inc(dS.h, 16)
                    sy.dma_start(out=ar_dbg, in_=AR[:]).then_inc(dS.h, 16)
                    sy.dma_start(out=sc_dbg, in_=SC[:]).then_inc(dS.h, 16)
                    dS.n += 48

        def emit_tern(P):
            tstr, tp1, tern2, ttmp, smallF, smbfF, smloF = P
            plans = [(gw_h, gwt_s, HID, D, 0), (vw_h, vwt_s, HID, D, 1),
                     (ow_h, owt_s, D, HID, 2)]
            str_tok = [(sV.n, sG.n), (sV.n, sG.n)]
            spill_tok = [0, 0, 0, 0]
            ter_tok = [0, 0]
            pxb_tok = [0, 0]
            vev_tok = [0]
            for wh, dst_s, R, C, idx in plans:
                dst_img = dst_s.rearrange("p (c n) -> p c n", c=16 if idx < 2 else 64)

                @on("vector")
                def _(v, idx=idx):
                    v.tensor_scalar(out=A2[:, 3:4], in0=SC[:, idx:idx + 1],
                                    scalar1=0.5, scalar2=None, op0=ALU.mult)
                    sV.inc(v.tensor_scalar(out=A2[:, 4:5], in0=SC[:, idx:idx + 1],
                                    scalar1=-0.5, scalar2=None, op0=ALU.mult))
                thr_tok = sV.n


                ntile = (R // 128) * (C // 2048)
                for it in range(ntile):
                    rt, cq = it // (C // 2048), it % (C // 2048)
                    sl = it % 2
                    buf = tstr[sl]
                    ter = tern2[sl]

                    @on("sync")
                    def _(sy, wh=wh, rt=rt, cq=cq, buf=buf, sl=sl):
                        sy.wait_ge(sV.h, str_tok[sl][0])
                        sy.wait_ge(sG.h, str_tok[sl][1])
                        sy.dma_start(out=buf[:],
                                     in_=wh[rt * 128:(rt + 1) * 128,
                                            cq * 2048:(cq + 1) * 2048]).then_inc(dW.h, 16)
                        dW.n += 16
                    wtok = dW.n

                    @on("vector")
                    def _(e, buf=buf, ter=ter, wtok=wtok, sl=sl):
                        e.wait_ge(dW.h, wtok)
                        e.wait_ge(sT.h, ter_tok[sl])
                        e.tensor_scalar(out=tp1[:], in0=buf[:], scalar1=A2[:, 3:4],
                                        scalar2=None, op0=ALU.is_ge)
                        e.tensor_scalar(out=ter[:], in0=buf[:], scalar1=A2[:, 4:5],
                                        scalar2=-1.0, op0=ALU.is_le, op1=ALU.mult)
                        sV.inc(e.tensor_tensor(out=ter[:], in0=ter[:], in1=tp1[:],
                                               op=ALU.add))
                    str_tok[sl] = (sV.n, sG.n)
                    etok = (sV.n, sG.n)
                    tsl4 = it % 4
                    tstage = ttmp[tsl4]

                    for grp in range(4):
                        ph_ = grp % 2

                        @on("tensor")
                        def _(t, ter=ter, grp=grp, etok=etok, ph_=ph_):
                            t.wait_ge(sV.h, etok[0])
                            t.wait_ge(sA.h, pxb_tok[ph_])
                            t.wait_ge(sV.h, vev_tok[0])
                            for i in range(4):
                                c = grp * 4 + i
                                li = t.transpose(
                                    out=pxb[:, (ph_ * 4 + i) * 128:(ph_ * 4 + i + 1) * 128],
                                    in_=ter[:, c * 128:(c + 1) * 128],
                                    identity=eyeb[:])
                            sT.inc(li)
                        ttok = sT.n

                        if grp % 2 == 0:
                            @on("scalar")
                            def _(a, grp=grp, ttok=ttok, tstage=tstage, tsl4=tsl4, ph_=ph_):
                                a.wait_ge(sT.h, ttok)
                                a.wait_ge(dS.h, spill_tok[tsl4])
                                sA.inc(a.copy(out=tstage[:, grp * 512:(grp + 1) * 512],
                                              in_=pxb[:, ph_ * 512:(ph_ + 1) * 512]))
                            pxb_tok[ph_] = sA.n
                        else:
                            @on("vector")
                            def _(v, grp=grp, ttok=ttok, tstage=tstage, tsl4=tsl4, ph_=ph_):
                                v.wait_ge(sT.h, ttok)
                                v.wait_ge(dS.h, spill_tok[tsl4])
                                sV.inc(v.tensor_copy(out=tstage[:, grp * 512:(grp + 1) * 512],
                                                     in_=pxb[:, ph_ * 512:(ph_ + 1) * 512]))
                            pxb_tok[ph_] = 0
                            vev_tok[0] = sV.n
                    ter_tok[sl] = sT.n
                    atok = sA.n

                    @on("sync")
                    def _(sy, rt=rt, cq=cq, idx=idx, atok=atok, dst_img=dst_img,
                          tstage=tstage, vtk=None):
                        sy.wait_ge(sA.h, atok)
                        sy.wait_ge(sV.h, vev_tok[0])
                        c0 = cq * 16
                        dst = dst_img[:, c0:c0 + 16, rt * 128:(rt + 1) * 128]
                        sy.dma_start(out=dst,
                                     in_=tstage[:].rearrange("p (c n) -> p c n", c=16)
                                     ).then_inc(dS.h, 16)
                        dS.n += 16
                    spill_tok[tsl4] = dS.n

            for src, dst, n, idx in ((gb_h, rg_s, HID, 0), (vb_h, rv_s, HID, 1),
                                     (ob_h, ro_s, D, 2)):
                @on("sync")
                def _(sy, src=src, n=n):
                    sy.wait_ge(sV.h, sV.n)
                    sy.dma_start(out=smallF[0:1, 0:n], in_=src[None, :]).then_inc(dB.h, 16)
                    dB.n += 16
                tok = dB.n

                @on("vector")
                def _(v, n=n, idx=idx, tok=tok):
                    v.wait_ge(dB.h, tok)
                    v.tensor_scalar(out=smallF[0:1, 0:n], in0=smallF[0:1, 0:n],
                                    scalar1=SC[0:1, 3 + idx:4 + idx], scalar2=None,
                                    op0=ALU.mult)
                    v.drain()
                    v.tensor_copy(out=smbfF[0:1, 0:n], in_=smallF[0:1, 0:n])
                    sV.inc(v.tensor_tensor(out=smloF[0:1, 0:n], in0=smallF[0:1, 0:n],
                                    in1=smbfF[0:1, 0:n], op=ALU.subtract))
                vtok = sV.n

                @on("sync")
                def _(sy, dst=dst, n=n, vtok=vtok):
                    sy.wait_ge(sV.h, vtok)
                    sy.dma_start(out=dst[0:1, 0:n], in_=smbfF[0:1, 0:n]).then_inc(dB.h, 16)
                    sy.dma_start(out=dst[1:2, 0:n], in_=smloF[0:1, 0:n]).then_inc(dB.h, 16)
                    dB.n += 32
                tok2 = dB.n

                @on("vector")
                def _(v, tok2=tok2):
                    v.wait_ge(dB.h, tok2)
                    sV.mark(v)

        def emit_p3b(P):
            (nxl, gwc, vwc, hq, nq, n_hf, owc, ffn_acc, AR2, ef, lad_v, lad_g,
             lao, lbo, b2o3, hgv) = P
            nxs_img = nx_s.rearrange("p (c n) -> p c n", c=16)
            gwt_img = gwt_s.rearrange("p (c n) -> p c n", c=16)
            vwt_img = vwt_s.rearrange("p (c n) -> p c n", c=16)
            owt_img = owt_s.rearrange("p (c n) -> p c n", c=64)
            nxl_img = nxl[:].rearrange("p (c n) -> p c n", c=16)
            gwc_img = gwc[:].rearrange("p (c n) -> p c n", c=16)
            vwc_img = vwc[:].rearrange("p (c n) -> p c n", c=16)
            nhf_img = n_hf[:].rearrange("p (c n) -> p c n", c=32)
            owc_img = owc[:].rearrange("p (c n) -> p c n", c=32)
            facc_img = ffn_acc[:].rearrange("p (c n) -> p c n", c=4)

            gv_tok = [0, 0]
            out_pkv_tok = [0]
            owc_tok = [0]
            ef_tok = [0, 0]
            h_spill_tok = [0, 0]
            hscl_tok = [0, 0]
            lad_tok = [0, 0]
            nq_tok = [0, 0]

            @on("vector")
            def _(v):
                v.tensor_scalar(out=AR2[:, 0:16], in0=AR[:, 32:48],
                                scalar1=SC[:, 0:1], scalar2=None, op0=ALU.mult)
                v.tensor_scalar(out=AR2[:, 16:32], in0=AR[:, 32:48],
                                scalar1=SC[:, 1:2], scalar2=None, op0=ALU.mult)
                sV.inc(v.memset(AR2[:, 32:48], 0.0))

            def gv_block(b3):
                r0 = b3 * B3B

                @on("sync")
                def _(sy, r0=r0):
                    sy.wait_ge(sT.h, sT.n)
                    sy.dma_start(out=nxl[:], in_=nxs_img[:, :, r0:r0 + B3B]
                                 ).then_inc(dR.h, 16)
                    dR.n += 16
                nxtok = dR.n

                for hcol in range(16):
                    @on("sync")
                    def _(sy, hcol=hcol):
                        sy.wait_ge(sT.h, sT.n)
                        sy.dma_start(out=gwc[:], in_=gwt_img[:, :, hcol * 512:
                                                           (hcol + 1) * 512]
                                     ).then_inc(dR.h, 16)
                        sy.dma_start(out=vwc[:], in_=vwt_img[:, :, hcol * 512:
                                                           (hcol + 1) * 512]
                                     ).then_inc(dR.h, 16)
                        sy.dma_start(out=b2t[:, 0:512],
                                     in_=rg_s[:, hcol * 512:(hcol + 1) * 512]
                                     ).then_inc(dB.h, 16)
                        sy.dma_start(out=b2t[:, 512:1024],
                                     in_=rv_s[:, hcol * 512:(hcol + 1) * 512]
                                     ).then_inc(dB.h, 16)
                        dR.n += 32
                        dB.n += 32
                    wtok = (dR.n, dB.n)

                    for tt4 in range(4):
                        tcol = b3 * 4 + tt4
                        tsl = slice(r0 + tt4 * 128, r0 + (tt4 + 1) * 128)
                        kpar = tt4 % 2
                        pgate = pj[0][:] if kpar == 0 else pkv[:, 0:512]
                        pval = pj[1][:] if kpar == 0 else pkv[:, 512:1024]

                        @on("tensor")
                        def _(t, tt4=tt4, tsl=tsl, wtok=wtok, nxtok=nxtok,
                              kpar=kpar, pgate=pgate, pval=pval):
                            t.wait_ge(dR.h, wtok[0])
                            t.wait_ge(dB.h, wtok[1])
                            t.wait_ge(dR.h, nxtok)
                            t.wait_ge(sV.h, max(gv_tok[kpar], out_pkv_tok[0]))
                            for pb, wimg, bslc in ((pgate, gwc_img, b2t[:, 0:512]),
                                                   (pval, vwc_img, b2t[:, 512:1024])):
                                t.matmul(out=pb, lhsT=la[:, tsl], rhs=bslc,
                                         start=True, stop=False)
                                t.matmul(out=pb, lhsT=lb[:, tsl], rhs=bslc,
                                         start=False, stop=False)
                                for dc in range(16):
                                    li = t.matmul(out=pb,
                                                  lhsT=nxl_img[:, dc,
                                                       tt4 * 128:(tt4 + 1) * 128],
                                                  rhs=wimg[:, dc, :],
                                                  start=False, stop=(dc == 15))
                            sT.inc(li)
                        mmtok = sT.n

                        efg = ef[0] if kpar == 0 else ef[2]
                        efv = ef[1] if kpar == 0 else ef[3]

                        @on("vector")
                        def _(v, tt4=tt4, tcol=tcol, mmtok=mmtok, kpar=kpar,
                              pgate=pgate, pval=pval, efg=efg, efv=efv):
                            v.wait_ge(sT.h, mmtok)
                            v.wait_ge(sA.h, ef_tok[kpar])
                            v.tensor_scalar(out=efg[:], in0=pgate,
                                            scalar1=AR2[:, tcol:tcol + 1],
                                            scalar2=None, op0=ALU.mult)
                            sV.inc(v.tensor_scalar(out=efv[:], in0=pval,
                                            scalar1=AR2[:, 16 + tcol:17 + tcol],
                                            scalar2=None, op0=ALU.mult))
                        vtok = sV.n
                        gv_tok[kpar] = vtok

                        @on("scalar")
                        def _(a, vtok=vtok, efg=efg):
                            a.wait_ge(sV.h, vtok)
                            sA.inc(a.activation(out=efg[:], in_=efg[:], func=AF.Silu))
                        atok = sA.n

                        @on("vector")
                        def _(v, tcol=tcol, atok=atok, kpar=kpar, efg=efg, efv=efv):
                            v.wait_ge(sA.h, atok)
                            v.wait_ge(dS.h, h_spill_tok[kpar])
                            sV.inc(v.tensor_tensor(out=hgv[kpar][:], in0=efg[:],
                                                   in1=efv[:], op=ALU.mult))
                        vtok2 = sV.n

                        @on("scalar")
                        def _(a, vtok2=vtok2, kpar=kpar, efg=efg):
                            a.wait_ge(sV.h, vtok2)
                            sA.inc(a.activation(out=efg[:], in_=hgv[kpar][:],
                                                func=AF.Abs))
                        atok2 = sA.n
                        ef_tok[kpar] = atok2

                        @on("vector")
                        def _(v, tcol=tcol, atok2=atok2, efg=efg):
                            v.wait_ge(sA.h, atok2)
                            v.tensor_reduce(out=A2[:, 0:1], in_=efg[:], axis=AX.X,
                                            op=ALU.max)
                            v.drain()
                            sV.inc(v.tensor_tensor(out=AR2[:, 32 + tcol:33 + tcol],
                                            in0=AR2[:, 32 + tcol:33 + tcol],
                                            in1=A2[:, 0:1], op=ALU.max))
                        vtok3 = sV.n

                        @on("sync")
                        def _(sy, tsl=tsl, hcol=hcol, vtok3=vtok3, kpar=kpar):
                            sy.wait_ge(sV.h, vtok3)
                            sy.dma_start(out=h_sc[tsl, hcol * 512:(hcol + 1) * 512],
                                         in_=hgv[kpar][:]).then_inc(dS.h, 16)
                            dS.n += 16
                        h_spill_tok[kpar] = dS.n
                    yield

                csl = slice(32 + b3 * 4, 32 + b3 * 4 + 4)
                ssl = slice(48 + b3 * 4, 48 + b3 * 4 + 4)
                osl = slice(0 + b3 * 4, 4 + b3 * 4)

                @on("vector")
                def _(v):
                    v.wait_ge(dS.h, dS.n)
                    v.tensor_scalar(out=AR2[:, csl], in0=AR2[:, csl], scalar1=1e-5,
                                    scalar2=None, op0=ALU.max)
                    v.drain()
                    v.reciprocal(out=AR2[:, ssl], in_=AR2[:, csl])
                    v.drain()
                    v.tensor_scalar(out=AR2[:, ssl], in0=AR2[:, ssl], scalar1=127.0,
                                    scalar2=None, op0=ALU.mult)
                    v.tensor_scalar(out=AR2[:, osl], in0=AR2[:, csl],
                                    scalar1=1.0 / 127.0, scalar2=None, op0=ALU.mult)
                    v.drain()
                    v.tensor_scalar(out=AR2[:, osl], in0=AR2[:, osl],
                                    scalar1=SC[:, 2:3], scalar2=None, op0=ALU.mult)
                    v.drain()
                    v.reciprocal(out=A2[:, 4:8], in_=AR2[:, osl])
                    v.drain()
                    v.tensor_copy(out=ihi16[:, b3 * 4:b3 * 4 + 4], in_=A2[:, 4:8])
                    v.drain()
                    sV.inc(v.tensor_tensor(out=ilo16[:, b3 * 4:b3 * 4 + 4], in0=A2[:, 4:8],
                                    in1=ihi16[:, b3 * 4:b3 * 4 + 4], op=ALU.subtract))
                gam_tok = sV.n

                @on("tensor")
                def _(t, gam_tok=gam_tok):
                    t.wait_ge(sV.h, gam_tok)
                    t.wait_ge(sA.h, sA.n)
                    t.transpose(out=pxb[0:16, 0:128], in_=ihi16[:], identity=eyeb[:])
                    sT.inc(t.transpose(out=pxb[0:16, 128:256], in_=ilo16[:], identity=eyeb[:]))
                ttok = sT.n

                @on("scalar")
                def _(a, ttok=ttok):
                    a.wait_ge(sT.h, ttok)
                    a.wait_ge(dB.h, dB.n)
                    sA.inc(a.copy(out=itr[:, 0:256], in_=pxb[0:16, 0:256]))
                atok = sA.n

                @on("sync")
                def _(sy, b3=b3, r0=r0, atok=atok):
                    sy.wait_ge(sA.h, atok)
                    sy.wait_ge(sT.h, sT.n)
                    for dst, c0 in ((lao[b3 % 2], 0), (lbo[b3 % 2], 128)):
                        sy.dma_start(
                            out=dst[0:1, :].rearrange("o (c p) -> o c p", p=128),
                            in_=itr[b3 * 4:b3 * 4 + 4, c0:c0 + 128]).then_inc(dB.h, 16)
                        sy.dma_start(
                            out=dst[1:2, :].rearrange("o (c p) -> o c p", p=128),
                            in_=itr[b3 * 4:b3 * 4 + 4, (c0 + 128) % 256:(c0 + 128) % 256 + 128]
                            ).then_inc(dB.h, 16)
                        dB.n += 32

            def qout_block(b3):
                r0 = b3 * B3B
                for half in range(2):
                    for tt4 in range(4):
                        tcol = b3 * 4 + tt4
                        tsl = slice(r0 + tt4 * 128, r0 + (tt4 + 1) * 128)
                        for hp in range(4):
                            hcol = half * 8 + hp * 2
                            sl = hp % 2

                            @on("sync")
                            def _(sy, tsl=tsl, hcol=hcol, sl=sl):
                                sy.wait_ge(sV.h, lad_tok[sl])
                                sy.dma_start(out=hq[sl][:],
                                             in_=h_sc[tsl, hcol * 512:(hcol + 2) * 512]
                                             ).then_inc(dR.h, 16)
                                dR.n += 16
                            htok = dR.n

                            @on("scalar")
                            def _(a, sl=sl, tcol=tcol, htok=htok):
                                a.wait_ge(dR.h, htok)
                                a.wait_ge(sV.h, lad_tok[sl])
                                sA.inc(a.activation(out=hq[sl][:], in_=hq[sl][:],
                                                    func=AF.Identity,
                                                    scale=AR2[:, 48 + tcol:49 + tcol]))
                            hscl_tok[sl] = sA.n
                            stok = sA.n

                            @on("vector")
                            def _(e, sl=sl, stok=stok):
                                e.wait_ge(sA.h, stok)
                                e.wait_ge(sT.h, nq_tok[0])
                                e.wait_ge(dF.h, dF.n)
                                sV.inc(ladder(e, lad_v, hq[sl][:], nq[sl][:], None))
                            lad_tok[sl] = sV.n
                            etok = sV.n

                            @on("tensor")
                            def _(t, etok=etok, sl=sl):
                                t.wait_ge(sV.h, etok)
                                t.wait_ge(sA.h, sA.n)
                                for i in range(8):
                                    li = t.transpose(out=pxb[:, i * 128:(i + 1) * 128],
                                                in_=nq[sl][:, i * 128:(i + 1) * 128],
                                                identity=eyeb[:])
                                sT.inc(li)
                            nq_tok[0] = sT.n
                            ttok2 = sT.n

                            @on("scalar")
                            def _(a, hp=hp, tt4=tt4, ttok2=ttok2):
                                a.wait_ge(sT.h, ttok2)
                                dst = nhf_img[:, hp * 8:(hp + 1) * 8,
                                              tt4 * 128:(tt4 + 1) * 128]
                                sA.inc(a.copy(out=dst,
                                       in_=pxb[:].rearrange("p (c n) -> p c n", c=8)))
                            yield
                    nh_tok = (sA.n, sV.n, sG.n)

                    for dcol in range(8):
                        @on("sync")
                        def _(sy, half=half, dcol=dcol):
                            sy.wait_ge(sT.h, sT.n)
                            sy.dma_start(out=owc[:],
                                         in_=owt_img[:, half * 32:(half + 1) * 32,
                                                     dcol * 256:(dcol + 1) * 256]
                                         ).then_inc(dR.h, 16)
                            sy.dma_start(out=b2o3[:],
                                         in_=ro_s[:, dcol * 256:(dcol + 1) * 256]
                                         ).then_inc(dB.h, 16)
                            dR.n += 16
                            dB.n += 16
                        otok = (dR.n, dB.n)

                        for tt4 in range(4):
                            tcol = b3 * 4 + tt4
                            tsl = slice(r0 + tt4 * 128, r0 + (tt4 + 1) * 128)
                            pb = pkv[:, tt4 * 512:tt4 * 512 + 256]

                            @on("tensor")
                            def _(t, half=half, tt4=tt4, tsl=tsl, pb=pb, otok=otok,
                                  nh_tok=nh_tok):
                                t.wait_ge(dR.h, otok[0])
                                t.wait_ge(dB.h, otok[1])
                                t.wait_ge(sA.h, nh_tok[0])
                                t.wait_ge(sV.h, sV.n)
                                first = True
                                if half == 0:
                                    lsl = slice(tt4 * 128, (tt4 + 1) * 128)
                                    t.matmul(out=pb, lhsT=lao[b3 % 2][:, lsl],
                                             rhs=b2o3[:, 0:256], start=True, stop=False)
                                    t.matmul(out=pb, lhsT=lbo[b3 % 2][:, lsl],
                                             rhs=b2o3[:, 0:256], start=False, stop=False)
                                    first = False
                                for hc in range(32):
                                    li = t.matmul(out=pb,
                                             lhsT=nhf_img[:, hc, tt4 * 128:(tt4 + 1) * 128],
                                             rhs=owc_img[:, hc, :],
                                             start=first, stop=(hc == 31))
                                    first = False
                                sT.inc(li)
                            mmtok = sT.n

                            @on("vector")
                            def _(v, half=half, tt4=tt4, dcol=dcol, tcol=tcol,
                                  pb=pb, mmtok=mmtok):
                                v.wait_ge(sT.h, mmtok)
                                dsl = facc_img[:, tt4, dcol * 256:(dcol + 1) * 256]
                                if half == 0:
                                    sV.inc(v.tensor_copy(out=dsl, in_=pb))
                                else:
                                    v.wait_ge(dF.h, dF.n)
                                    v.tensor_tensor(out=lad_v[2][:, 0:256], in0=pb,
                                                    in1=dsl, op=ALU.add)
                                    sV.inc(v.tensor_scalar(out=lad_v[3][:, 0:256],
                                                    in0=lad_v[2][:, 0:256],
                                                    scalar1=AR2[:, tcol:tcol + 1],
                                                    scalar2=None, op0=ALU.mult))
                            vtok = sV.n

                            out_pkv_tok[0] = sV.n
                            if half == 1:
                                @on("sync")
                                def _(sy, tsl=tsl, dcol=dcol, vtok=vtok):
                                    sy.wait_ge(sV.h, vtok)
                                    sy.dma_start(out=ffn_s[tsl, dcol * 256:(dcol + 1) * 256],
                                                 in_=lad_v[3][:, 0:256]).then_inc(dF.h, 16)
                                    dF.n += 16
                        yield


            def drain(g, n=10 ** 9):
                for _ in range(n):
                    if next(g, "END") == "END":
                        return True
                return False

            prev_q = None
            for b3 in range(NB3B):
                g = gv_block(b3)
                while True:
                    done_g = drain(g, 1)
                    if prev_q is not None:
                        drain(prev_q, 1)
                    if done_g:
                        break
                if prev_q is not None:
                    drain(prev_q)
                prev_q = qout_block(b3)
            drain(prev_q)

        def emit_p3a(P):
            (q_st, qfh, qfl, xfh, xfl, at1, u_tm, mfh, mfl, wstg, wc_hi, wc_lo,
             ef32, ytile, xtile, ftile, b2u, b2o) = P
            xnhs_img = xn_hi_s.rearrange("p (c n) -> p c n", c=16)
            xnls_img = xn_lo_s.rearrange("p (c n) -> p c n", c=16)
            bank_tok = [(0, 0), (0, 0)]
            mb_tok = [0, 0]
            qpx_tok = [0, 0]
            wpar_tok = [0, 0]
            stg_tok = [(0, 0), (0, 0)]
            for b4 in range(NB3A):
                r0 = b4 * B3A
                t1_toks = []
                for tl in range(2):
                    rr = r0 + tl * 128

                    @on("sync")
                    def _(sy, rr=rr, tl=tl):
                        sy.wait_ge(sT.h, sT.n)
                        sy.dma_start(out=xfh[tl][:], in_=xnhs_img[:, :, rr:rr + 128]
                                     ).then_inc(dR.h, 16)
                        sy.dma_start(out=xfl[tl][:], in_=xnls_img[:, :, rr:rr + 128]
                                     ).then_inc(dR.h, 16)
                        dR.n += 32

                    for qsrc, dst in ((q_hi_s, qfh[tl]), (q_lo_s, qfl[tl])):
                        @on("sync")
                        def _(sy, qsrc=qsrc, rr=rr):
                            sy.wait_ge(sT.h, sT.n)
                            sy.dma_start(out=q_st[:], in_=qsrc[rr:rr + 128, :]
                                         ).then_inc(dR.h, 16)
                            dR.n += 16
                        qtok = dR.n
                        for grp in range(4):
                            ph_ = grp % 2

                            @on("tensor")
                            def _(t, grp=grp, qtok=qtok, ph_=ph_):
                                t.wait_ge(dR.h, qtok)
                                t.wait_ge(sA.h, qpx_tok[ph_])
                                for i in range(4):
                                    c = grp * 4 + i
                                    li = t.transpose(
                                        out=pxb[:, (ph_ * 4 + i) * 128:(ph_ * 4 + i + 1) * 128],
                                        in_=q_st[:, c * 128:(c + 1) * 128],
                                        identity=eyeb[:])
                                sT.inc(li)
                            ttok = sT.n

                            @on("scalar")
                            def _(a, dst=dst, grp=grp, ttok=ttok, ph_=ph_):
                                a.wait_ge(sT.h, ttok)
                                sA.inc(a.copy(out=dst[:, grp * 512:(grp + 1) * 512],
                                       in_=pxb[:, ph_ * 512:(ph_ + 1) * 512]))
                            qpx_tok[ph_] = sA.n
                    qf_tok = sA.n

                    @on("tensor")
                    def _(t, tl=tl, qf_tok=qf_tok):
                        t.wait_ge(sA.h, qf_tok)
                        t.wait_ge(sV.h, sV.n)
                        for hd in range(16):
                            c = hd * 128
                            for pi, (lh, rh) in enumerate(((qfh[tl], kv_hi),
                                                           (qfh[tl], kv_lo),
                                                           (qfl[tl], kv_hi))):
                                li = t.matmul(out=pkv[:, c:c + 128],
                                              lhsT=lh[:, c:c + 128],
                                              rhs=rh[:, c:c + 128],
                                              start=(pi == 0 and hd % 4 == 0),
                                              stop=(pi == 2 and hd % 4 == 3))
                        sT.inc(li)
                    qkv_tok = sT.n

                    @on("vector")
                    def _(v, tl=tl, qkv_tok=qkv_tok):
                        v.wait_ge(sT.h, qkv_tok)
                        v.tensor_copy(out=at1[tl][:], in_=pkv[:])
                        v.reduce_sum(out=A2[:, 0:1], in_=at1[tl][:], axis=AX.X)
                        v.tensor_tensor(out=at1[tl][:], in0=at1[tl][:], in1=at1[tl][:],
                                        op=ALU.mult)
                        v.reduce_sum(out=A2[:, 1:2], in_=at1[tl][:], axis=AX.X)
                        v.drain()
                        v.tensor_scalar(out=A2[:, 2:3], in0=A2[:, 0:1],
                                        scalar1=1.0 / HH, scalar2=None, op0=ALU.mult)
                        v.drain()
                        v.tensor_tensor(out=A2[:, 3:4], in0=A2[:, 2:3],
                                        in1=A2[:, 2:3], op=ALU.mult)
                        v.tensor_scalar(out=A2[:, 4:5], in0=A2[:, 1:2],
                                        scalar1=1.0 / HH, scalar2=None, op0=ALU.mult)
                        v.drain()
                        sV.inc(v.tensor_tensor(out=A2[:, 4:5], in0=A2[:, 4:5],
                                        in1=A2[:, 3:4], op=ALU.subtract))
                    vtok = sV.n

                    @on("scalar")
                    def _(a, vtok=vtok):
                        a.wait_ge(sV.h, vtok)
                        sA.inc(a.activation(out=A2[:, 5:6], in_=A2[:, 4:5], func=AF.Sqrt,
                                     bias=SC[:, 6:7]))
                    atok = sA.n

                    @on("vector")
                    def _(v, tl=tl, atok=atok):
                        v.wait_ge(sA.h, atok)
                        v.reciprocal(out=A2[:, 5:6], in_=A2[:, 5:6])
                        v.drain()
                        sV.inc(v.tensor_scalar(out=at1[tl][:], in0=pkv[:], scalar1=A2[:, 2:3],
                                        scalar2=A2[:, 5:6], op0=ALU.subtract,
                                        op1=ALU.mult))
                    t1_toks.append(sV.n)

                for fcol in range(4):
                    par = fcol % 2
                    splitok = emit_wsplit(wu_h, fcol, wc_hi[par], wc_lo[par],
                                          wstg, stg_tok, wpar_tok, par)

                    for tl in range(2):
                        bi = tl

                        @on("tensor")
                        def _(t, tl=tl, fcol=fcol, par=par, splitok=splitok, bi=bi):
                            t.wait_ge(sA.h, splitok[0])
                            t.wait_ge(sV.h, splitok[1])
                            t.wait_ge(sA.h, bank_tok[bi][0])
                            t.wait_ge(sV.h, bank_tok[bi][1])
                            t.matmul(out=pj[bi][:], lhsT=ones2[:],
                                     rhs=b2u[:, fcol * 512:(fcol + 1) * 512],
                                     start=True, stop=False)
                            xfh_img = xfh[tl][:].rearrange("p (c n) -> p c n", c=16)
                            xfl_img = xfl[tl][:].rearrange("p (c n) -> p c n", c=16)
                            for dc in range(16):
                                t.matmul(out=pj[bi][:], lhsT=xfh_img[:, dc, :],
                                         rhs=wc_hi[par][:, dc * 512:(dc + 1) * 512],
                                         start=False, stop=False)
                            for dc in range(16):
                                t.matmul(out=pj[bi][:], lhsT=xfh_img[:, dc, :],
                                         rhs=wc_lo[par][:, dc * 512:(dc + 1) * 512],
                                         start=False, stop=False)
                            for dc in range(16):
                                li = t.matmul(out=pj[bi][:], lhsT=xfl_img[:, dc, :],
                                         rhs=wc_hi[par][:, dc * 512:(dc + 1) * 512],
                                         start=False, stop=(dc == 15))
                            sT.inc(li)
                        mmtok = sT.n
                        wpar_tok[par] = mmtok

                        @on("scalar")
                        def _(a, tl=tl, fcol=fcol, bi=bi, mmtok=mmtok):
                            a.wait_ge(sT.h, mmtok)
                            sA.inc(a.activation(out=u_tm[tl][:, fcol * 512:(fcol + 1) * 512],
                                         in_=pj[bi][:], func=AF.Silu))
                        bank_tok[bi] = (sA.n, bank_tok[bi][1])

                u_tok = sA.n
                for tl in range(2):
                    for grp in range(4):
                        p_ = grp % 2
                        pa = ptr[:] if p_ == 0 else pkv[:, 0:512]
                        pb_ = pj[0][:] if p_ == 0 else pj[1][:]

                        @on("tensor")
                        def _(t, tl=tl, grp=grp, u_tok=u_tok, t1t=t1_toks[tl],
                              p_=p_, pa=pa, pb_=pb_):
                            t.wait_ge(sA.h, u_tok)
                            t.wait_ge(sV.h, max(t1t, mb_tok[p_]))
                            for i in range(4):
                                c = grp * 4 + i
                                t.transpose(out=pa[:, i * 128:(i + 1) * 128],
                                            in_=at1[tl][:, c * 128:(c + 1) * 128],
                                            identity=eyef[:])
                            for i in range(4):
                                c = grp * 4 + i
                                li = t.transpose(out=pb_[:, i * 128:(i + 1) * 128],
                                                 in_=u_tm[tl][:, c * 128:(c + 1) * 128],
                                                 identity=eyef[:])
                            sT.inc(li)
                        ttok = sT.n

                        @on("vector")
                        def _(v, tl=tl, grp=grp, ttok=ttok, pa=pa, pb_=pb_):
                            v.wait_ge(sT.h, ttok)
                            for i in range(4):
                                c = grp * 4 + i
                                v.tensor_scalar(out=ef32[0][:, i * 128:(i + 1) * 128],
                                                in0=pa[:, i * 128:(i + 1) * 128],
                                                scalar1=g_cols[:, c:c + 1],
                                                scalar2=b_cols[:, c:c + 1],
                                                op0=ALU.mult, op1=ALU.add)
                            v.drain()
                            v.tensor_tensor(out=ef32[1][:], in0=ef32[0][:],
                                            in1=pb_[:, 0:512], op=ALU.mult)
                            v.tensor_copy(out=mfh[tl][:, grp * 512:(grp + 1) * 512],
                                          in_=ef32[1][:])
                            sV.inc(v.tensor_tensor(
                                out=mfl[tl][:, grp * 512:(grp + 1) * 512],
                                in0=ef32[1][:],
                                in1=mfh[tl][:, grp * 512:(grp + 1) * 512],
                                op=ALU.subtract))
                        mb_tok[p_] = sV.n
                m_tok = (sV.n, sA.n)

                for dcol in range(4):
                    par = dcol % 2
                    splitok = emit_wsplit(wo_h, dcol, wc_hi[par], wc_lo[par],
                                          wstg, stg_tok, wpar_tok, par)

                    for tl in range(2):
                        rr = r0 + tl * 128
                        bi = tl

                        @on("sync")
                        def _(sy, rr=rr, dcol=dcol, tl=tl):
                            sy.wait_ge(sV.h, sV.n)
                            sy.dma_start(out=xtile[:],
                                         in_=x_h[rr:rr + 128,
                                                 dcol * 512:(dcol + 1) * 512]
                                         ).then_inc(dX.h, 16)
                            sy.dma_start(out=ftile[:],
                                         in_=ffn_s[rr:rr + 128,
                                                   dcol * 512:(dcol + 1) * 512]
                                         ).then_inc(dR.h, 16)
                            dX.n += 16
                            dR.n += 16
                        xftok = (dX.n, dR.n)

                        @on("tensor")
                        def _(t, tl=tl, dcol=dcol, par=par, splitok=splitok, bi=bi,
                              m_tok=m_tok):
                            t.wait_ge(sA.h, splitok[0])
                            t.wait_ge(sV.h, splitok[1])
                            t.wait_ge(sV.h, m_tok[0])
                            t.wait_ge(sA.h, m_tok[1])
                            t.matmul(out=pj[bi][:], lhsT=ones2[:],
                                     rhs=b2o[:, dcol * 512:(dcol + 1) * 512],
                                     start=True, stop=False)
                            for hc in range(16):
                                t.matmul(out=pj[bi][:],
                                         lhsT=mfh[tl][:, hc * 128:(hc + 1) * 128],
                                         rhs=wc_hi[par][:, hc * 512:(hc + 1) * 512],
                                         start=False, stop=False)
                            for hc in range(16):
                                t.matmul(out=pj[bi][:],
                                         lhsT=mfh[tl][:, hc * 128:(hc + 1) * 128],
                                         rhs=wc_lo[par][:, hc * 512:(hc + 1) * 512],
                                         start=False, stop=False)
                            for hc in range(16):
                                li = t.matmul(out=pj[bi][:],
                                         lhsT=mfl[tl][:, hc * 128:(hc + 1) * 128],
                                         rhs=wc_hi[par][:, hc * 512:(hc + 1) * 512],
                                         start=False, stop=(hc == 15))
                            sT.inc(li)
                        mmtok = sT.n
                        wpar_tok[par] = mmtok

                        @on("vector")
                        def _(v, tl=tl, bi=bi, mmtok=mmtok, xftok=xftok):
                            v.wait_ge(sT.h, mmtok)
                            v.wait_ge(dX.h, xftok[0])
                            v.wait_ge(dR.h, xftok[1])
                            v.wait_ge(dO.h, dO.n)
                            v.tensor_tensor(out=ytile[:], in0=pj[bi][:],
                                            in1=xtile[:], op=ALU.add)
                            sV.inc(v.tensor_tensor(out=ytile[:], in0=ytile[:],
                                            in1=ftile[:], op=ALU.add))
                        bank_tok[bi] = (bank_tok[bi][0], sV.n)
                        vtok = sV.n

                        @on("sync")
                        def _(sy, rr=rr, dcol=dcol, tl=tl, vtok=vtok):
                            sy.wait_ge(sV.h, vtok)
                            sy.dma_start(out=y_h[rr:rr + 128,
                                                 dcol * 512:(dcol + 1) * 512],
                                         in_=ytile[:]).then_inc(dO.h, 16)
                            dO.n += 16

        with ExitStack() as s0:
            small2k = s0.enter_context(nc.sbuf_tensor("small2k", [2, 2048], F32))
            smbf2k = s0.enter_context(nc.sbuf_tensor("smbf2k", [2, 2048], BF16))
            smlo2k = s0.enter_context(nc.sbuf_tensor("smlo2k", [1, 2048], BF16))
            emit_setup(small2k, smbf2k, smlo2k)
            barrier()
        with ExitStack() as ph:
            def sb1(name, shape, dt):
                return ph.enter_context(nc.sbuf_tensor(name, shape, dt))
            x_t = [sb1(f"x_t{i}", [128, D], F32) for i in range(2)]
            emit_p0(x_t)
            barrier()
            if upto not in ("p0",):
                xn_tm = sb1("xn_tm", [128, D], F32)
                nx_tm = sb1("nx_tm", [128, D], BF16)
                xnf_hi = sb1("xnf_hi", [128, 16 * B1], BF16)
                xnf_lo = sb1("xnf_lo", [128, 16 * B1], BF16)
                wstg = [sb1(f"wstg{i}", [128, 1024], F32) for i in range(2)]
                wc_hi = [sb1(f"wc_hi{i}", [128, 16 * 512], BF16) for i in range(2)]
                wc_lo = [sb1(f"wc_lo{i}", [128, 16 * 512], BF16) for i in range(2)]
                b2qk = sb1("b2qk", [2, HH], BF16)
                b2v = sb1("b2v", [2, HH], BF16)

                @on("sync")
                def _(sy):
                    sy.dma_start(out=b2qk[:], in_=bq2_s[:]).then_inc(dB.h, 16)
                    sy.dma_start(out=b2v[:], in_=bv2_s[:]).then_inc(dB.h, 16)
                    dB.n += 32
                b2tok = dB.n

                @on("tensor")
                def _(t, b2tok=b2tok):
                    t.wait_ge(dB.h, b2tok)

                kh_hi = sb1("kh_hi", [128, 4 * 512], BF16)
                kh_lo = sb1("kh_lo", [128, 4 * 512], BF16)
                vh_hi = sb1("vh_hi", [128, 4 * 512], BF16)
                vh_lo = sb1("vh_lo", [128, 4 * 512], BF16)
                ef32 = [sb1(f"ef32{i}", [128, 512], F32) for i in range(3)]
                lad_v = [sb1(f"lad_v{i}", [128, 512], F32) for i in range(4)]
                emit_p1((x_t, xn_tm, nx_tm, xnf_hi, xnf_lo, wstg, wc_hi, wc_lo,
                         kh_hi, kh_lo, vh_hi, vh_lo, ef32, lad_v, b2qk, b2v))
                barrier()

        if upto in ("tern", "p3b", "all"):
            with ExitStack() as ph2:
                def sb2(name, shape, dt):
                    return ph2.enter_context(nc.sbuf_tensor(name, shape, dt))
                tstr = [sb2(f"tstr{i}", [128, 2048], F32) for i in range(2)]
                tp1 = sb2("tp1", [128, 2048], F32)
                tern2 = [sb2(f"tern2{i}", [128, 2048], BF16) for i in range(2)]
                ttmp = [sb2(f"ttmp{i}", [128, 2048], BF16) for i in range(4)]
                smallF = sb2("smallF", [1, HID], F32)
                smbfF = sb2("smbfF", [1, HID], BF16)
                smloF = sb2("smloF", [1, HID], BF16)
                emit_tern((tstr, tp1, tern2, ttmp, smallF, smbfF, smloF))
                barrier()

        if upto in ("p3b", "all"):
            with ExitStack() as ph3:
                def sb3(name, shape, dt):
                    return ph3.enter_context(nc.sbuf_tensor(name, shape, dt))
                nxl = sb3("nxl", [128, 16 * 512], BF16)
                gwc = sb3("gwc", [128, 16 * 512], BF16)
                vwc = sb3("vwc", [128, 16 * 512], BF16)
                hq = [sb3(f"hq{i}", [128, 1024], F32) for i in range(2)]
                nq0 = sb3("nq0", [128, 1024], BF16)
                nq = [nq0, nq0]
                n_hf = sb3("n_hf", [128, 32 * 512], BF16)
                owc = sb3("owc", [128, 32 * 256], BF16)
                ffn_acc = sb3("ffn_acc", [128, 4 * 2048], F32)
                AR2 = sb3("AR2", [128, 64], F32)
                ef = [sb3(f"ef{i}", [128, 512], F32) for i in range(4)]
                lad_v3 = [sb3(f"lad_v3{i}", [128, 1024], F32) for i in range(4)]
                lad_g3 = None
                lao = [sb3(f"lao{i}", [2, B3B], BF16) for i in range(2)]
                lbo = [sb3(f"lbo{i}", [2, B3B], BF16) for i in range(2)]
                b2o3 = sb3("b2o3", [2, 256], BF16)
                hgv = [sb3(f"hgv{i}", [128, 512], F32) for i in range(2)]
                emit_p3b((nxl, gwc, vwc, hq, nq, n_hf, owc, ffn_acc, AR2, ef,
                          lad_v3, lad_g3, lao, lbo, b2o3, hgv))
                barrier()

        if upto == "all":
            with ExitStack() as ph4:
                def sb4(name, shape, dt):
                    return ph4.enter_context(nc.sbuf_tensor(name, shape, dt))
                q_st = sb4("q_st", [128, 2048], BF16)
                qfh0 = sb4("qfh0", [128, 2048], BF16)
                qfl0 = sb4("qfl0", [128, 2048], BF16)
                qfh = [qfh0, qfh0]
                qfl = [qfl0, qfl0]
                xfh = [sb4(f"xfh{i}", [128, 2048], BF16) for i in range(2)]
                xfl = [sb4(f"xfl{i}", [128, 2048], BF16) for i in range(2)]
                at1 = [sb4(f"at1{i}", [128, 2048], F32) for i in range(2)]
                u_tm = [sb4(f"u_tm{i}", [128, 2048], F32) for i in range(2)]
                mfh = [sb4(f"mfh{i}", [128, 2048], BF16) for i in range(2)]
                mfl = [sb4(f"mfl{i}", [128, 2048], BF16) for i in range(2)]
                wstg4 = [sb4(f"wstg4{i}", [128, 1024], F32) for i in range(2)]
                wc_hi4 = [sb4(f"wc_hi4{i}", [128, 16 * 512], BF16) for i in range(2)]
                wc_lo4 = [sb4(f"wc_lo4{i}", [128, 16 * 512], BF16) for i in range(2)]
                ef324 = [sb4(f"ef324{i}", [128, 512], F32) for i in range(2)]
                b2u = sb4("b2u", [2, HH], BF16)
                b2o = sb4("b2o", [2, D], BF16)

                @on("sync")
                def _(sy):
                    sy.dma_start(out=b2u[:], in_=bu2_s[:]).then_inc(dB.h, 16)
                    sy.dma_start(out=b2o[:], in_=bo2_s[:]).then_inc(dB.h, 16)
                    dB.n += 32
                b4tok = dB.n

                @on("tensor")
                def _(t, b4tok=b4tok):
                    t.wait_ge(dB.h, b4tok)

                ytile = sb4("ytile", [128, 512], F32)
                xtile = sb4("xtile", [128, 512], F32)
                ftile = sb4("ftile", [128, 512], F32)
                emit_p3a((q_st, qfh, qfl, xfh, xfl, at1, u_tm, mfh, mfl, wstg4,
                          wc_hi4, wc_lo4, ef324, ytile, xtile, ftile, b2u, b2o))

        @on("gpsimd")
        def _(g):
            g.wait_ge(dS.h, dS.n)
            g.wait_ge(dX.h, dX.n)
            g.wait_ge(dW.h, dW.n)
            g.wait_ge(dB.h, dB.n)
            g.wait_ge(dR.h, dR.n)
            g.wait_ge(dF.h, dF.n)
            if dO.n:
                g.wait_ge(dO.h, dO.n)

    return nc


def _in_maps(inputs):
    x = np.ascontiguousarray(np.asarray(inputs["L0_attn_x"], dtype=np.float32))
    eyef = np.eye(128, dtype=np.float32)
    eyeb = np.eye(128, dtype=ml_dtypes.bfloat16)
    shared = {
        "Wqk": np.ascontiguousarray(inputs["L0_attn_Wqk"]),
        "bqk": np.ascontiguousarray(inputs["L0_attn_bqk"]),
        "Wv": np.ascontiguousarray(inputs["L0_attn_Wv"]),
        "bv": np.ascontiguousarray(inputs["L0_attn_bv"]),
        "Wu": np.ascontiguousarray(inputs["L0_attn_Wu"]),
        "bu": np.ascontiguousarray(inputs["L0_attn_bu"]),
        "Wo": np.ascontiguousarray(inputs["L0_attn_Wo"]),
        "bo": np.ascontiguousarray(inputs["L0_attn_bo"]),
        "ln_g": np.ascontiguousarray(inputs["L0_attn_ln_g"]),
        "ln_b": np.ascontiguousarray(inputs["L0_attn_ln_b"]),
        "gate_w": np.ascontiguousarray(inputs["L0_ffn_gate_w"]),
        "gate_b": np.ascontiguousarray(inputs["L0_ffn_gate_b"]),
        "val_w": np.ascontiguousarray(inputs["L0_ffn_val_w"]),
        "val_b": np.ascontiguousarray(inputs["L0_ffn_val_b"]),
        "out_w": np.ascontiguousarray(inputs["L0_ffn_out_w"]),
        "out_b": np.ascontiguousarray(inputs["L0_ffn_out_b"]),
        "eyef": eyef, "eyeb": eyeb,
    }
    maps = []
    for c in range(8):
        b, hf = c // 2, c % 2
        own = x[b, hf * OWN:(hf + 1) * OWN]
        oth = x[b, (1 - hf) * OWN:(2 - hf) * OWN]
        m = dict(shared)
        m["x_seq"] = np.ascontiguousarray(np.concatenate([own, oth], axis=0))
        maps.append(m)
    return maps


def kernel(**inputs):
    nc = build_program()
    maps = _in_maps(inputs)
    res = run_bass_kernel_spmd(nc, maps, core_ids=list(range(8)))
    y = np.empty((4, SEQ, D), np.float32)
    for c in range(8):
        b, hf = c // 2, c % 2
        y[b, hf * OWN:(hf + 1) * OWN] = res.results[c]["y"]
    return y


# version 16
# speedup vs baseline: 1.0550x; 1.0312x over previous
import numpy as np
import ml_dtypes
import concourse.bass as bass
import concourse.mybir as mybir
from concourse.bass_utils import run_bass_kernel_spmd
from contextlib import ExitStack

F32 = mybir.dt.float32
BF16 = mybir.dt.bfloat16
AF = mybir.ActivationFunctionType
ALU = mybir.AluOpType
AX = mybir.AxisListType

SEQ, OWN, D, HH, HID, HEADS = 4096, 2048, 2048, 2048, 8192, 16
LN_EPS = 1e-5
B1 = 512
NB1 = SEQ // B1
B3B = 512
NB3B = OWN // B3B
B3A = 256
NB3A = OWN // B3A
LADDER = (128.0, 64.0, 32.0, 16.0, 8.0, 4.0, 2.0, 1.0)


class Sem:
    def __init__(self, es, nc, name):
        self.h = es.enter_context(nc.semaphore(name))
        self.n = 0

    def inc(self, inst, k=1):
        inst.then_inc(self.h, k)
        self.n += k
        return self.n

    def mark(self, eng):
        eng.drain()
        eng.sem_inc(self.h, 1)
        self.n += 1
        return self.n


def build_program(upto="all", debug=False):
    nc = bass.Bass("TRN2", target_bir_lowering=False)
    dk = "ExternalOutput" if debug else "Internal"

    x_h = nc.dram_tensor("x_seq", [SEQ, D], F32, kind="ExternalInput").ap()
    wqk_h = nc.dram_tensor("Wqk", [D, HH], F32, kind="ExternalInput").ap()
    bqk_h = nc.dram_tensor("bqk", [HH], F32, kind="ExternalInput").ap()
    wv_h = nc.dram_tensor("Wv", [D, HH], F32, kind="ExternalInput").ap()
    bv_h = nc.dram_tensor("bv", [HH], F32, kind="ExternalInput").ap()
    wu_h = nc.dram_tensor("Wu", [D, HH], F32, kind="ExternalInput").ap()
    bu_h = nc.dram_tensor("bu", [HH], F32, kind="ExternalInput").ap()
    wo_h = nc.dram_tensor("Wo", [HH, D], F32, kind="ExternalInput").ap()
    bo_h = nc.dram_tensor("bo", [D], F32, kind="ExternalInput").ap()
    lng_h = nc.dram_tensor("ln_g", [HH], F32, kind="ExternalInput").ap()
    lnb_h = nc.dram_tensor("ln_b", [HH], F32, kind="ExternalInput").ap()
    gw_h = nc.dram_tensor("gate_w", [HID, D], F32, kind="ExternalInput").ap()
    gb_h = nc.dram_tensor("gate_b", [HID], F32, kind="ExternalInput").ap()
    vw_h = nc.dram_tensor("val_w", [HID, D], F32, kind="ExternalInput").ap()
    vb_h = nc.dram_tensor("val_b", [HID], F32, kind="ExternalInput").ap()
    ow_h = nc.dram_tensor("out_w", [D, HID], F32, kind="ExternalInput").ap()
    ob_h = nc.dram_tensor("out_b", [D], F32, kind="ExternalInput").ap()
    eyef_h = nc.dram_tensor("eyef", [128, 128], F32, kind="ExternalInput").ap()
    eyeb_h = nc.dram_tensor("eyeb", [128, 128], BF16, kind="ExternalInput").ap()
    y_h = nc.dram_tensor("y", [OWN, D], F32, kind="ExternalOutput").ap()

    q_hi_s = nc.dram_tensor("q_hi_s", [OWN, HH], BF16, kind=dk).ap()
    q_lo_s = nc.dram_tensor("q_lo_s", [OWN, HH], BF16, kind=dk).ap()
    xn_hi_s = nc.dram_tensor("xn_hi_s", [128, 16 * OWN], BF16, kind=dk).ap()
    xn_lo_s = nc.dram_tensor("xn_lo_s", [128, 16 * OWN], BF16, kind=dk).ap()
    nx_s = nc.dram_tensor("nx_s", [128, 16 * OWN], BF16, kind=dk).ap()
    gwt_s = nc.dram_tensor("gwt_s", [128, 16 * HID], BF16, kind=dk).ap()
    vwt_s = nc.dram_tensor("vwt_s", [128, 16 * HID], BF16, kind=dk).ap()
    owt_s = nc.dram_tensor("owt_s", [128, 64 * D], BF16, kind=dk).ap()
    h_sc = nc.dram_tensor("h_sc", [OWN, HID], F32, kind=dk).ap()
    ffn_s = nc.dram_tensor("ffn_s", [OWN, D], F32, kind=dk).ap()
    bq2_s = nc.dram_tensor("bq2_s", [2, HH], BF16, kind=dk).ap()
    bv2_s = nc.dram_tensor("bv2_s", [2, HH], BF16, kind=dk).ap()
    bu2_s = nc.dram_tensor("bu2_s", [2, HH], BF16, kind=dk).ap()
    bo2_s = nc.dram_tensor("bo2_s", [2, D], BF16, kind=dk).ap()
    rg_s = nc.dram_tensor("rg_s", [2, HID], BF16, kind=dk).ap()
    rv_s = nc.dram_tensor("rv_s", [2, HID], BF16, kind=dk).ap()
    ro_s = nc.dram_tensor("ro_s", [2, D], BF16, kind=dk).ap()
    if debug:
        kv_dbg = nc.dram_tensor("kv_dbg", [128, HH], F32, kind="ExternalOutput").ap()
        ar_dbg = nc.dram_tensor("ar_dbg", [128, 64], F32, kind="ExternalOutput").ap()
        sc_dbg = nc.dram_tensor("sc_dbg", [128, 8], F32, kind="ExternalOutput").ap()

    es = ExitStack()
    with es:
        def sb(name, shape, dt):
            return es.enter_context(nc.sbuf_tensor(name, shape, dt))

        kv_acc = sb("kv_acc", [128, HH], F32)
        kv_hi = sb("kv_hi", [128, HH], BF16)
        kv_lo = sb("kv_lo", [128, HH], BF16)
        AR = sb("AR", [128, 64], F32)
        A2 = sb("A2", [128, 8], F32)
        SC = sb("SC", [128, 8], F32)
        onesc = sb("onesc", [128, 1], F32)
        ones1 = sb("ones1", [1, 128], F32)
        ones2 = sb("ones2", [2, 128], BF16)
        eyef = sb("eyef_t", [128, 128], F32)
        eyeb = sb("eyeb_t", [128, 128], BF16)
        b2t = sb("b2t", [2, 1024], BF16)
        la = sb("la", [2, OWN], BF16)
        lb = sb("lb", [2, OWN], BF16)
        g_cols = sb("g_cols", [128, 16], F32)
        b_cols = sb("b_cols", [128, 16], F32)
        ihi16 = sb("ihi16", [128, 16], BF16)
        ilo16 = sb("ilo16", [128, 16], BF16)
        itr = sb("itr", [16, 256], BF16)

        pj = [es.enter_context(nc.psum_tensor(f"pj{i}", [128, 512], F32)) for i in range(2)]
        pkv = es.enter_context(nc.psum_tensor("pkv", [128, HH], F32))
        ptr = es.enter_context(nc.psum_tensor("ptr", [128, 512], F32))
        pxb = es.enter_context(nc.psum_tensor("pxb", [128, 1024], BF16))

        dX = Sem(es, nc, "dX")
        dW = Sem(es, nc, "dW")
        dS = Sem(es, nc, "dS")
        dR = Sem(es, nc, "dR")
        dO = Sem(es, nc, "dO")
        dB = Sem(es, nc, "dB")
        dF = Sem(es, nc, "dF")
        sV = Sem(es, nc, "sV")
        sA = Sem(es, nc, "sA")
        sG = Sem(es, nc, "sG")
        sT = Sem(es, nc, "sT")

        block = es.enter_context(nc.Block())

        def on(name):
            return getattr(block, name)

        def barrier():
            toks = {}

            @on("vector")
            def _(v):
                toks["v"] = sV.mark(v)

            @on("scalar")
            def _(a):
                toks["a"] = sA.mark(a)

            @on("gpsimd")
            def _(g):
                toks["g"] = sG.mark(g)

            @on("tensor")
            def _(t):
                toks["t"] = sT.mark(t)

            for en in ("vector", "scalar", "gpsimd", "tensor", "sync"):
                @on(en)
                def _(e):
                    e.wait_ge(sV.h, toks["v"])
                    e.wait_ge(sA.h, toks["a"])
                    e.wait_ge(sG.h, toks["g"])
                    e.wait_ge(sT.h, toks["t"])

        def ladder(eng, scr, src_ap, dst_ap, scale_ap):
            s2, aa, dd, md = scr
            n = src_ap.shape[-1]
            s2v, aav, ddv, mdv = s2[:, 0:n], aa[:, 0:n], dd[:, 0:n], md[:, 0:n]
            if scale_ap is not None:
                eng.tensor_scalar(out=mdv, in0=src_ap, scalar1=scale_ap,
                                  scalar2=None, op0=ALU.mult)
                t = mdv
            else:
                t = src_ap
            eng.tensor_scalar(out=s2v, in0=t, scalar1=0.0, scalar2=2.0,
                              op0=ALU.is_ge, op1=ALU.mult)
            eng.tensor_tensor(out=aav, in0=t, in1=s2v, op=ALU.mult)
            eng.tensor_tensor(out=aav, in0=aav, in1=t, op=ALU.subtract)
            eng.tensor_copy(out=ddv, in_=aav)
            for k in LADDER:
                eng.tensor_scalar(out=mdv, in0=ddv, scalar1=k - 0.5, scalar2=-k,
                                  op0=ALU.is_ge, op1=ALU.mult)
                eng.tensor_tensor(out=ddv, in0=ddv, in1=mdv, op=ALU.add)
            eng.tensor_tensor(out=aav, in0=aav, in1=ddv, op=ALU.subtract)
            eng.tensor_tensor(out=mdv, in0=aav, in1=s2v, op=ALU.mult)
            return eng.tensor_tensor(out=dst_ap, in0=mdv, in1=aav, op=ALU.subtract)

        def emit_setup(small, smbf, smlo):
            @on("vector")
            def _(v):
                v.memset(onesc[:], 1.0)
                v.memset(SC[:, 6:7], LN_EPS)
                v.memset(ones1[:], 1.0)
                v.memset(ones2[:], 1.0)
                v.memset(kv_acc[:], 0.0)
                v.memset(AR[:], 0.0)

            @on("sync")
            def _(s):
                s.dma_start(out=eyef[:], in_=eyef_h[:]).then_inc(dB.h, 16)
                s.dma_start(out=eyeb[:], in_=eyeb_h[:]).then_inc(dB.h, 16)
                s.dma_start(out=g_cols[:],
                            in_=lng_h.rearrange("(c p) -> p c", p=128),
                            allow_slow_non_contiguous=True).then_inc(dB.h, 16)
                s.dma_start(out=b_cols[:],
                            in_=lnb_h.rearrange("(c p) -> p c", p=128),
                            allow_slow_non_contiguous=True).then_inc(dB.h, 16)
                dB.n += 64

            for src, dst, n in ((bqk_h, bq2_s, HH), (bv_h, bv2_s, HH),
                                (bu_h, bu2_s, HH), (bo_h, bo2_s, D)):
                @on("sync")
                def _(s, src=src, n=n):
                    s.wait_ge(sV.h, sV.n)
                    s.dma_start(out=small[0:1, 0:n], in_=src[None, :]).then_inc(dB.h, 16)
                    dB.n += 16
                tok = dB.n

                @on("vector")
                def _(v, n=n, tok=tok):
                    v.wait_ge(dB.h, tok)
                    v.tensor_copy(out=smbf[0:1, 0:n], in_=small[0:1, 0:n])
                    sV.inc(v.tensor_tensor(out=smlo[0:1, 0:n], in0=small[0:1, 0:n],
                                    in1=smbf[0:1, 0:n], op=ALU.subtract))
                vtok = sV.n

                @on("sync")
                def _(s, dst=dst, n=n, vtok=vtok):
                    s.wait_ge(sV.h, vtok)
                    s.dma_start(out=dst[0:1, 0:n], in_=smbf[0:1, 0:n]).then_inc(dB.h, 16)
                    s.dma_start(out=dst[1:2, 0:n], in_=smlo[0:1, 0:n]).then_inc(dB.h, 16)
                    dB.n += 32
                tok2 = dB.n

                @on("vector")
                def _(v, tok2=tok2):
                    v.wait_ge(dB.h, tok2)
                    sV.mark(v)

        def emit_p0(bw):
            plans = []
            for idx, (wh, R, C) in (((1, (vw_h, HID, D))), ((0, (gw_h, HID, D))),
                                    ((2, (ow_h, D, HID)))):
                tiles = []
                for rt in range(R // 128):
                    for cq in range(C // 2048):
                        tiles.append(wh[rt * 128:(rt + 1) * 128,
                                        cq * 2048:(cq + 1) * 2048])
                plans.append((idx, R * C, tiles))

            for idx, nel, tiles in plans:
                @on("vector")
                def _(v):
                    sV.inc(v.memset(A2[:, 0:1], 0.0))

                red_tok = [sV.n, sV.n]
                for it, tile in enumerate(tiles):
                    buf = bw[it % 2]

                    @on("sync")
                    def _(s, tile=tile, buf=buf, it=it):
                        s.wait_ge(sV.h, red_tok[it % 2])
                        s.dma_start(out=buf[:, 0:2048], in_=tile).then_inc(dX.h, 16)
                        dX.n += 16
                    tok = dX.n

                    @on("vector")
                    def _(v, buf=buf, tok=tok):
                        v.wait_ge(dX.h, tok)
                        v.tensor_reduce(out=A2[:, 1:2], in_=buf[:, 0:2048], axis=AX.X,
                                        op=ALU.add, apply_absolute_value=True)
                        v.drain()
                        sV.inc(v.tensor_tensor(out=A2[:, 0:1], in0=A2[:, 0:1],
                                               in1=A2[:, 1:2], op=ALU.add))
                    red_tok[it % 2] = sV.n

                @on("tensor")
                def _(t):
                    t.wait_ge(sV.h, sV.n)
                    sT.inc(t.matmul(out=ptr[0:1, 0:1], lhsT=onesc[:],
                                    rhs=A2[:, 0:1], start=True, stop=True))
                ttok = sT.n

                @on("vector")
                def _(v, nel=nel, ttok=ttok):
                    v.wait_ge(sT.h, ttok)
                    sV.inc(v.tensor_scalar(out=A2[0:1, 2:3], in0=ptr[0:1, 0:1],
                                    scalar1=1.0 / nel, scalar2=1e-5,
                                    op0=ALU.mult, op1=ALU.max))
                vtok = sV.n

                @on("tensor")
                def _(t, vtok=vtok):
                    t.wait_ge(sV.h, vtok)
                    sT.inc(t.matmul(out=ptr[:, 1:2], lhsT=ones1[:],
                                    rhs=A2[0:1, 2:3], start=True, stop=True))
                ttok2 = sT.n

                @on("vector")
                def _(v, idx=idx, ttok2=ttok2):
                    v.wait_ge(sT.h, ttok2)
                    v.tensor_copy(out=SC[:, idx:idx + 1], in_=ptr[:, 1:2])
                    v.drain()
                    sV.inc(v.reciprocal(out=SC[:, 3 + idx:4 + idx], in_=SC[:, idx:idx + 1]))


        def emit_wsplit(wh, fcol, whi, wlo, wstg, stg_tok, wpar_tok, par):
            for q in range(8):
                sl = q % 2
                stg = wstg[sl]

                @on("sync")
                def _(sy, q=q, sl=sl, stg=stg, wh=wh, fcol=fcol):
                    sy.wait_ge(sA.h, stg_tok[sl][0])
                    sy.wait_ge(sV.h, stg_tok[sl][1])
                    src = wh[q * 256:(q + 1) * 256, fcol * 512:(fcol + 1) * 512]
                    sy.dma_start(out=stg[:].rearrange("p (c n) -> p c n", c=2),
                                 in_=src.rearrange("(c p) n -> p c n", p=128)
                                 ).then_inc(dW.h, 16)
                    dW.n += 16
                wtok = dW.n

                @on("scalar")
                def _(a, q=q, stg=stg, whi=whi, wtok=wtok, par=par):
                    a.wait_ge(dW.h, wtok)
                    a.wait_ge(sT.h, wpar_tok[par])
                    sA.inc(a.copy(out=whi[:, q * 1024:(q + 1) * 1024], in_=stg[:]))
                atok = sA.n

                @on("vector")
                def _(v, q=q, stg=stg, whi=whi, wlo=wlo, atok=atok):
                    v.wait_ge(sA.h, atok)
                    sV.inc(v.tensor_tensor(out=wlo[:, q * 1024:(q + 1) * 1024],
                                           in0=stg[:],
                                           in1=whi[:, q * 1024:(q + 1) * 1024],
                                           op=ALU.subtract))
                stg_tok[sl] = (atok, sV.n)
            return (sA.n, sV.n)

        def emit_p1(P):
            (x_t, xn_tm, nx_tm, xnf_hi, xnf_lo, wstg, wc_hi, wc_lo,
             kh_hi, kh_lo, vh_hi, vh_lo, ef32, lad_v, b2qk, b2v) = P
            nsub = B1 // 128
            xn_img_hi = xnf_hi[:].rearrange("p (c n) -> p c n", c=16)
            xn_img_lo = xnf_lo[:].rearrange("p (c n) -> p c n", c=16)
            nxs_img = nx_s.rearrange("p (c n) -> p c n", c=16)
            xnhs_img = xn_hi_s.rearrange("p (c n) -> p c n", c=16)
            xnls_img = xn_lo_s.rearrange("p (c n) -> p c n", c=16)
            ptr_img = ptr[:].rearrange("p (c n) -> p c n", c=4)
            x_tok = [sV.n, sV.n]
            q_spill_tok = [0, 0, 0, 0]
            xn_spill_tok = 0
            nx_spill_tok = 0
            exp_tok = [0]
            sil_tok = [0]
            bank_tok = [(0, 0), (0, 0)]
            wpar_tok = [0, 0]
            stg_tok = [(0, 0), (0, 0)]
            pend_split = [None, None]
            pend_split[0] = emit_wsplit(wqk_h, 0, wc_hi[0], wc_lo[0], wstg,
                                        stg_tok, wpar_tok, 0)

            for b in range(NB1):
                own = b < OWN // B1
                for s in range(nsub):
                    xb = x_t[s % 2]
                    row0 = b * B1 + s * 128
                    tcol = row0 // 128 if own else 0

                    @on("sync")
                    def _(sy, xb=xb, row0=row0, s=s):
                        sy.wait_ge(sV.h, x_tok[s % 2])
                        sy.dma_start(out=xb[:], in_=x_h[row0:row0 + 128, :]).then_inc(dX.h, 16)
                        dX.n += 16
                    xtok = dX.n

                    @on("vector")
                    def _(v, xb=xb, xtok=xtok):
                        v.wait_ge(dX.h, xtok)
                        v.wait_ge(sT.h, sT.n)
                        v.wait_ge(sG.h, sG.n)
                        v.reduce_sum(out=A2[:, 0:1], in_=xb[:], axis=AX.X)
                        v.tensor_tensor(out=xn_tm[:], in0=xb[:], in1=xb[:], op=ALU.mult)
                        v.reduce_sum(out=A2[:, 1:2], in_=xn_tm[:], axis=AX.X)
                        v.drain()
                        v.tensor_scalar(out=A2[:, 2:3], in0=A2[:, 0:1],
                                        scalar1=1.0 / D, scalar2=None, op0=ALU.mult)
                        v.drain()
                        v.tensor_tensor(out=A2[:, 3:4], in0=A2[:, 2:3],
                                        in1=A2[:, 2:3], op=ALU.mult)
                        v.tensor_scalar(out=A2[:, 4:5], in0=A2[:, 1:2],
                                        scalar1=1.0 / D, scalar2=None, op0=ALU.mult)
                        v.drain()
                        sV.inc(v.tensor_tensor(out=A2[:, 4:5], in0=A2[:, 4:5],
                                        in1=A2[:, 3:4], op=ALU.subtract))
                    vtok = sV.n

                    @on("scalar")
                    def _(a, vtok=vtok):
                        a.wait_ge(sV.h, vtok)
                        sA.inc(a.activation(out=A2[:, 5:6], in_=A2[:, 4:5], func=AF.Sqrt,
                                     bias=SC[:, 6:7]))
                    atok = sA.n

                    @on("vector")
                    def _(v, xb=xb, atok=atok, tcol=tcol, own=own):
                        v.wait_ge(sA.h, atok)
                        if own:
                            v.wait_ge(dS.h, nx_spill_tok)
                        v.reciprocal(out=A2[:, 5:6], in_=A2[:, 5:6])
                        v.drain()
                        li = v.tensor_scalar(out=xn_tm[:], in0=xb[:], scalar1=A2[:, 2:3],
                                             scalar2=A2[:, 5:6], op0=ALU.subtract,
                                             op1=ALU.mult)
                        if own:
                            v.tensor_reduce(out=A2[:, 6:7], in_=xn_tm[:], axis=AX.X,
                                            op=ALU.max, apply_absolute_value=True)
                            v.drain()
                            v.tensor_scalar(out=AR[:, tcol:tcol + 1], in0=A2[:, 6:7],
                                            scalar1=1e-5, scalar2=None, op0=ALU.max)
                            v.drain()
                            v.reciprocal(out=A2[:, 7:8], in_=AR[:, tcol:tcol + 1])
                            v.drain()
                            v.tensor_scalar(out=AR[:, 16 + tcol:17 + tcol],
                                            in0=A2[:, 7:8], scalar1=127.0,
                                            scalar2=None, op0=ALU.mult)
                            v.tensor_scalar(out=AR[:, 32 + tcol:33 + tcol],
                                            in0=AR[:, tcol:tcol + 1],
                                            scalar1=1.0 / 127.0, scalar2=None,
                                            op0=ALU.mult)
                            v.drain()
                            for j in range(4):
                                li = ladder(v, lad_v, xn_tm[:, j * 512:(j + 1) * 512],
                                            nx_tm[:, j * 512:(j + 1) * 512],
                                            AR[:, 16 + tcol:17 + tcol])
                        sV.inc(li)
                    lntok = sV.n
                    x_tok[s % 2] = lntok

                    for grp in range(4):
                        @on("tensor")
                        def _(t, grp=grp, lntok=lntok):
                            t.wait_ge(sV.h, lntok)
                            t.wait_ge(sA.h, sA.n)
                            t.wait_ge(sV.h, sV.n)
                            for i in range(4):
                                c = grp * 4 + i
                                li = t.transpose(out=ptr[:, i * 128:(i + 1) * 128],
                                            in_=xn_tm[:, c * 128:(c + 1) * 128],
                                            identity=eyef[:])
                            sT.inc(li)
                        ttok = sT.n

                        @on("scalar")
                        def _(a, grp=grp, s=s, ttok=ttok):
                            a.wait_ge(sT.h, ttok)
                            a.wait_ge(dS.h, xn_spill_tok)
                            sA.inc(a.copy(out=xn_img_hi[:, grp * 4:(grp + 1) * 4,
                                                 s * 128:(s + 1) * 128],
                                   in_=ptr_img))
                        atok2 = sA.n

                        @on("vector")
                        def _(v, grp=grp, s=s, ttok=ttok, atok2=atok2):
                            v.wait_ge(sT.h, ttok)
                            v.wait_ge(sA.h, atok2)
                            sV.inc(v.tensor_tensor(out=xn_img_lo[:, grp * 4:(grp + 1) * 4,
                                                          s * 128:(s + 1) * 128],
                                            in0=ptr_img,
                                            in1=xn_img_hi[:, grp * 4:(grp + 1) * 4,
                                                          s * 128:(s + 1) * 128],
                                            op=ALU.subtract))

                    if own:
                        for grp in range(4):
                            @on("tensor")
                            def _(t, grp=grp, lntok=lntok):
                                t.wait_ge(sV.h, lntok)
                                t.wait_ge(sA.h, sA.n)
                                for i in range(4):
                                    c = grp * 4 + i
                                    li = t.transpose(out=pxb[:, i * 128:(i + 1) * 128],
                                                in_=nx_tm[:, c * 128:(c + 1) * 128],
                                                identity=eyeb[:])
                                sT.inc(li)
                            ttok = sT.n

                            @on("scalar")
                            def _(a, grp=grp, ttok=ttok):
                                a.wait_ge(sT.h, ttok)
                                sA.inc(a.copy(out=nx_tm[:, grp * 512:(grp + 1) * 512],
                                       in_=pxb[:, 0:512]))
                        gtok = sA.n

                        @on("sync")
                        def _(sy, row0=row0, gtok=gtok):
                            sy.wait_ge(sA.h, gtok)
                            sy.dma_start(out=nxs_img[:, :, row0:row0 + 128],
                                         in_=nx_tm[:].rearrange("p (c n) -> p c n",
                                                                c=16)).then_inc(dS.h, 16)
                            dS.n += 16
                        nx_spill_tok = dS.n

                        @on("sync")
                        def _(sy, s=s, row0=row0):
                            sy.wait_ge(sV.h, sV.n)
                            sy.wait_ge(sA.h, sA.n)
                            sy.dma_start(out=xnhs_img[:, :, row0:row0 + 128],
                                         in_=xn_img_hi[:, :, s * 128:(s + 1) * 128]
                                         ).then_inc(dS.h, 16)
                            sy.dma_start(out=xnls_img[:, :, row0:row0 + 128],
                                         in_=xn_img_lo[:, :, s * 128:(s + 1) * 128]
                                         ).then_inc(dS.h, 16)
                            dS.n += 32
                        xn_spill_tok = dS.n

                for fcol in range(4):
                    for wi, (wh, b2w) in enumerate(((wqk_h, b2qk), (wv_h, b2v))):
                        gidx = b * 8 + fcol * 2 + wi
                        par = gidx % 2
                        whi, wlo = wc_hi[par], wc_lo[par]
                        splitok = pend_split[par]

                        for tt in range(4):
                            bi = tt % 2
                            pb = pj[bi]

                            @on("tensor")
                            def _(t, tt=tt, fcol=fcol, b2w=b2w, whi=whi, wlo=wlo,
                                  pb=pb, splitok=splitok, bi=bi):
                                t.wait_ge(sA.h, splitok[0])
                                t.wait_ge(sV.h, splitok[1])
                                t.wait_ge(sV.h, bank_tok[bi][0])
                                t.wait_ge(sA.h, bank_tok[bi][1])
                                t.matmul(out=pb[:], lhsT=ones2[:],
                                         rhs=b2w[:, fcol * 512:(fcol + 1) * 512],
                                         start=True, stop=False)
                                for dc in range(16):
                                    t.matmul(out=pb[:],
                                             lhsT=xn_img_hi[:, dc, tt * 128:(tt + 1) * 128],
                                             rhs=whi[:, dc * 512:(dc + 1) * 512],
                                             start=False, stop=False)
                                for dc in range(16):
                                    t.matmul(out=pb[:],
                                             lhsT=xn_img_hi[:, dc, tt * 128:(tt + 1) * 128],
                                             rhs=wlo[:, dc * 512:(dc + 1) * 512],
                                             start=False, stop=False)
                                for dc in range(16):
                                    li = t.matmul(out=pb[:],
                                             lhsT=xn_img_lo[:, dc, tt * 128:(tt + 1) * 128],
                                             rhs=whi[:, dc * 512:(dc + 1) * 512],
                                             start=False, stop=(dc == 15))
                                sT.inc(li)
                            mmtok = sT.n
                            wpar_tok[par] = mmtok

                            if tt == 0 and gidx + 1 < NB1 * 8:
                                nb, nrem = divmod(gidx + 1, 8)
                                nfc, nwi = divmod(nrem, 2)
                                nwh = wqk_h if nwi == 0 else wv_h
                                pend_split[(gidx + 1) % 2] = emit_wsplit(
                                    nwh, nfc, wc_hi[(gidx + 1) % 2],
                                    wc_lo[(gidx + 1) % 2], wstg, stg_tok,
                                    wpar_tok, (gidx + 1) % 2)

                            if wi == 0:
                                @on("vector")
                                def _(v, pb=pb, mmtok=mmtok):
                                    v.wait_ge(sT.h, mmtok)
                                    v.wait_ge(sA.h, exp_tok[0])
                                    v.tensor_scalar(out=ef32[0][:], in0=pb[:],
                                                    scalar1=0.0, scalar2=None,
                                                    op0=ALU.min)
                                    sV.inc(v.tensor_scalar(out=ef32[1][:], in0=pb[:],
                                                           scalar1=0.0, scalar2=None,
                                                           op0=ALU.max))
                                vtok = sV.n
                                bank_tok[bi] = (vtok, 0)

                                @on("scalar")
                                def _(a, vtok=vtok):
                                    a.wait_ge(sV.h, vtok)
                                    sA.inc(a.activation(out=ef32[0][:], in_=ef32[0][:],
                                                        func=AF.Exp))
                                exp_tok[0] = sA.n
                                atok = sA.n

                                @on("vector")
                                def _(v, tt=tt, atok=atok):
                                    v.wait_ge(sA.h, atok)
                                    v.wait_ge(dS.h, q_spill_tok[tt])
                                    v.scalar_tensor_tensor(out=ef32[1][:], in0=ef32[0][:],
                                                           scalar=-1.0, in1=ef32[1][:],
                                                           op0=ALU.add, op1=ALU.add)
                                    v.tensor_copy(out=kh_hi[:, tt * 512:(tt + 1) * 512],
                                                  in_=ef32[1][:])
                                    sV.inc(v.tensor_tensor(out=kh_lo[:, tt * 512:(tt + 1) * 512],
                                                    in0=ef32[1][:],
                                                    in1=kh_hi[:, tt * 512:(tt + 1) * 512],
                                                    op=ALU.subtract))
                            else:
                                @on("scalar")
                                def _(a, pb=pb, tt=tt, mmtok=mmtok):
                                    a.wait_ge(sT.h, mmtok)
                                    a.wait_ge(sV.h, sil_tok[0])
                                    sA.inc(a.activation(out=ef32[2][:], in_=pb[:],
                                                        func=AF.Silu))
                                atok = sA.n
                                bank_tok[bi] = (0, atok)

                                @on("vector")
                                def _(v, tt=tt, atok=atok):
                                    v.wait_ge(sA.h, atok)
                                    v.tensor_copy(out=vh_hi[:, tt * 512:(tt + 1) * 512],
                                                  in_=ef32[2][:])
                                    sV.inc(v.tensor_tensor(out=vh_lo[:, tt * 512:(tt + 1) * 512],
                                                    in0=ef32[2][:],
                                                    in1=vh_hi[:, tt * 512:(tt + 1) * 512],
                                                    op=ALU.subtract))
                                sil_tok[0] = sV.n

                            if wi == 0 and own:
                                @on("sync")
                                def _(sy, b=b, tt=tt, fcol=fcol):
                                    sy.wait_ge(sV.h, sV.n)
                                    sy.wait_ge(sA.h, sA.n)
                                    r0 = b * B1 + tt * 128
                                    sy.dma_start(
                                        out=q_hi_s[r0:r0 + 128,
                                                   fcol * 512:(fcol + 1) * 512],
                                        in_=kh_hi[:, tt * 512:(tt + 1) * 512]
                                    ).then_inc(dS.h, 16)
                                    sy.dma_start(
                                        out=q_lo_s[r0:r0 + 128,
                                                   fcol * 512:(fcol + 1) * 512],
                                        in_=kh_lo[:, tt * 512:(tt + 1) * 512]
                                    ).then_inc(dS.h, 16)
                                    dS.n += 32
                                q_spill_tok[tt] = dS.n

                    evactok = (sV.n, sA.n)

                    @on("tensor")
                    def _(t, b=b, fcol=fcol, evactok=evactok):
                        t.wait_ge(sV.h, evactok[0])
                        t.wait_ge(sA.h, evactok[1])
                        first = True
                        for hd in range(4):
                            col = fcol * 512 + hd * 128
                            for tt in range(4):
                                for pi, (lh, rh) in enumerate(((kh_hi, vh_hi),
                                                               (kh_hi, vh_lo),
                                                               (kh_lo, vh_hi))):
                                    last = (hd == 3 and tt == 3 and pi == 2)
                                    sl = slice(tt * 512 + hd * 128,
                                               tt * 512 + (hd + 1) * 128)
                                    li = t.matmul(out=pkv[:, col:col + 128],
                                                  lhsT=lh[:, sl], rhs=rh[:, sl],
                                                  start=first, stop=last)
                                    first = False
                        sT.inc(li)

                kvtok = sT.n

                @on("vector")
                def _(v, kvtok=kvtok):
                    v.wait_ge(sT.h, kvtok)
                    sV.inc(v.tensor_tensor(out=kv_acc[:], in0=pkv[:], in1=kv_acc[:],
                                    op=ALU.add))

            @on("vector")
            def _(v):
                v.tensor_copy(out=kv_hi[:], in_=kv_acc[:])
                v.tensor_tensor(out=kv_lo[:], in0=kv_acc[:], in1=kv_hi[:],
                                op=ALU.subtract)
                v.tensor_copy(out=ihi16[:], in_=AR[:, 16:32])
                v.drain()
                sV.inc(v.tensor_tensor(out=ilo16[:], in0=AR[:, 16:32], in1=ihi16[:],
                                op=ALU.subtract))
            vtok = sV.n

            @on("tensor")
            def _(t, vtok=vtok):
                t.wait_ge(sV.h, vtok)
                t.wait_ge(sA.h, sA.n)
                t.transpose(out=pxb[0:16, 0:128], in_=ihi16[:], identity=eyeb[:])
                sT.inc(t.transpose(out=pxb[0:16, 128:256], in_=ilo16[:], identity=eyeb[:]))
            ttok = sT.n

            @on("scalar")
            def _(a, ttok=ttok):
                a.wait_ge(sT.h, ttok)
                sA.inc(a.copy(out=itr[:, 0:256], in_=pxb[0:16, 0:256]))
            atok = sA.n

            @on("sync")
            def _(sy, atok=atok):
                sy.wait_ge(sA.h, atok)
                for dst, c0s in ((la, (0, 128)), (lb, (128, 0))):
                    for r, c0 in enumerate(c0s):
                        sy.dma_start(
                            out=dst[r:r + 1, :].rearrange("o (c p) -> o c p", p=128),
                            in_=itr[:, c0:c0 + 128]).then_inc(dB.h, 16)
                        dB.n += 16

            if debug:
                @on("sync")
                def _(sy):
                    sy.wait_ge(sV.h, sV.n)
                    sy.dma_start(out=kv_dbg, in_=kv_acc[:]).then_inc(dS.h, 16)
                    sy.dma_start(out=ar_dbg, in_=AR[:]).then_inc(dS.h, 16)
                    sy.dma_start(out=sc_dbg, in_=SC[:]).then_inc(dS.h, 16)
                    dS.n += 48

        def emit_tern(P):
            tstr, tp1, tern2, ttmp, smallF, smbfF, smloF = P
            plans = [(gw_h, gwt_s, HID, D, 0), (vw_h, vwt_s, HID, D, 1),
                     (ow_h, owt_s, D, HID, 2)]
            str_tok = [(sV.n, sG.n), (sV.n, sG.n)]
            spill_tok = [0, 0, 0, 0]
            ter_tok = [0, 0]
            pxb_tok = [0, 0]
            vev_tok = [0]
            for wh, dst_s, R, C, idx in plans:
                dst_img = dst_s.rearrange("p (c n) -> p c n", c=16 if idx < 2 else 64)

                @on("vector")
                def _(v, idx=idx):
                    v.tensor_scalar(out=A2[:, 3:4], in0=SC[:, idx:idx + 1],
                                    scalar1=0.5, scalar2=None, op0=ALU.mult)
                    sV.inc(v.tensor_scalar(out=A2[:, 4:5], in0=SC[:, idx:idx + 1],
                                    scalar1=-0.5, scalar2=None, op0=ALU.mult))
                thr_tok = sV.n


                ntile = (R // 128) * (C // 2048)
                for it in range(ntile):
                    rt, cq = it // (C // 2048), it % (C // 2048)
                    sl = it % 2
                    buf = tstr[sl]
                    ter = tern2[sl]

                    @on("sync")
                    def _(sy, wh=wh, rt=rt, cq=cq, buf=buf, sl=sl):
                        sy.wait_ge(sV.h, str_tok[sl][0])
                        sy.wait_ge(sG.h, str_tok[sl][1])
                        sy.dma_start(out=buf[:],
                                     in_=wh[rt * 128:(rt + 1) * 128,
                                            cq * 2048:(cq + 1) * 2048]).then_inc(dW.h, 16)
                        dW.n += 16
                    wtok = dW.n

                    @on("vector")
                    def _(e, buf=buf, ter=ter, wtok=wtok, sl=sl):
                        e.wait_ge(dW.h, wtok)
                        e.wait_ge(sT.h, ter_tok[sl])
                        e.tensor_scalar(out=tp1[:], in0=buf[:], scalar1=A2[:, 3:4],
                                        scalar2=None, op0=ALU.is_ge)
                        e.tensor_scalar(out=ter[:], in0=buf[:], scalar1=A2[:, 4:5],
                                        scalar2=-1.0, op0=ALU.is_le, op1=ALU.mult)
                        sV.inc(e.tensor_tensor(out=ter[:], in0=ter[:], in1=tp1[:],
                                               op=ALU.add))
                    str_tok[sl] = (sV.n, sG.n)
                    etok = (sV.n, sG.n)
                    tsl4 = it % 4
                    tstage = ttmp[tsl4]

                    for grp in range(4):
                        ph_ = grp % 2

                        @on("tensor")
                        def _(t, ter=ter, grp=grp, etok=etok, ph_=ph_):
                            t.wait_ge(sV.h, etok[0])
                            t.wait_ge(sA.h, pxb_tok[ph_])
                            t.wait_ge(sV.h, vev_tok[0])
                            for i in range(4):
                                c = grp * 4 + i
                                li = t.transpose(
                                    out=pxb[:, (ph_ * 4 + i) * 128:(ph_ * 4 + i + 1) * 128],
                                    in_=ter[:, c * 128:(c + 1) * 128],
                                    identity=eyeb[:])
                            sT.inc(li)
                        ttok = sT.n

                        if grp % 2 == 0:
                            @on("scalar")
                            def _(a, grp=grp, ttok=ttok, tstage=tstage, tsl4=tsl4, ph_=ph_):
                                a.wait_ge(sT.h, ttok)
                                a.wait_ge(dS.h, spill_tok[tsl4])
                                sA.inc(a.copy(out=tstage[:, grp * 512:(grp + 1) * 512],
                                              in_=pxb[:, ph_ * 512:(ph_ + 1) * 512]))
                            pxb_tok[ph_] = sA.n
                        else:
                            @on("vector")
                            def _(v, grp=grp, ttok=ttok, tstage=tstage, tsl4=tsl4, ph_=ph_):
                                v.wait_ge(sT.h, ttok)
                                v.wait_ge(dS.h, spill_tok[tsl4])
                                sV.inc(v.tensor_copy(out=tstage[:, grp * 512:(grp + 1) * 512],
                                                     in_=pxb[:, ph_ * 512:(ph_ + 1) * 512]))
                            pxb_tok[ph_] = 0
                            vev_tok[0] = sV.n
                    ter_tok[sl] = sT.n
                    atok = sA.n

                    @on("sync")
                    def _(sy, rt=rt, cq=cq, idx=idx, atok=atok, dst_img=dst_img,
                          tstage=tstage, vtk=None):
                        sy.wait_ge(sA.h, atok)
                        sy.wait_ge(sV.h, vev_tok[0])
                        c0 = cq * 16
                        dst = dst_img[:, c0:c0 + 16, rt * 128:(rt + 1) * 128]
                        sy.dma_start(out=dst,
                                     in_=tstage[:].rearrange("p (c n) -> p c n", c=16)
                                     ).then_inc(dS.h, 16)
                        dS.n += 16
                    spill_tok[tsl4] = dS.n

            for src, dst, n, idx in ((gb_h, rg_s, HID, 0), (vb_h, rv_s, HID, 1),
                                     (ob_h, ro_s, D, 2)):
                @on("sync")
                def _(sy, src=src, n=n):
                    sy.wait_ge(sV.h, sV.n)
                    sy.dma_start(out=smallF[0:1, 0:n], in_=src[None, :]).then_inc(dB.h, 16)
                    dB.n += 16
                tok = dB.n

                @on("vector")
                def _(v, n=n, idx=idx, tok=tok):
                    v.wait_ge(dB.h, tok)
                    v.tensor_scalar(out=smallF[0:1, 0:n], in0=smallF[0:1, 0:n],
                                    scalar1=SC[0:1, 3 + idx:4 + idx], scalar2=None,
                                    op0=ALU.mult)
                    v.drain()
                    v.tensor_copy(out=smbfF[0:1, 0:n], in_=smallF[0:1, 0:n])
                    sV.inc(v.tensor_tensor(out=smloF[0:1, 0:n], in0=smallF[0:1, 0:n],
                                    in1=smbfF[0:1, 0:n], op=ALU.subtract))
                vtok = sV.n

                @on("sync")
                def _(sy, dst=dst, n=n, vtok=vtok):
                    sy.wait_ge(sV.h, vtok)
                    sy.dma_start(out=dst[0:1, 0:n], in_=smbfF[0:1, 0:n]).then_inc(dB.h, 16)
                    sy.dma_start(out=dst[1:2, 0:n], in_=smloF[0:1, 0:n]).then_inc(dB.h, 16)
                    dB.n += 32
                tok2 = dB.n

                @on("vector")
                def _(v, tok2=tok2):
                    v.wait_ge(dB.h, tok2)
                    sV.mark(v)

        def emit_p3b(P):
            (nxl, gwc, vwc, hq, nq, n_hf, owc, ffn_acc, AR2, ef, lad_v, lad_g,
             lao, lbo, b2o3, hgv) = P
            nxs_img = nx_s.rearrange("p (c n) -> p c n", c=16)
            gwt_img = gwt_s.rearrange("p (c n) -> p c n", c=16)
            vwt_img = vwt_s.rearrange("p (c n) -> p c n", c=16)
            owt_img = owt_s.rearrange("p (c n) -> p c n", c=64)
            nxl_img = nxl[:].rearrange("p (c n) -> p c n", c=16)
            gwc_img = gwc[:].rearrange("p (c n) -> p c n", c=16)
            vwc_img = vwc[:].rearrange("p (c n) -> p c n", c=16)
            nhf_img = n_hf[:].rearrange("p (c n) -> p c n", c=32)
            owc_img = owc[:].rearrange("p (c n) -> p c n", c=32)
            facc_img = ffn_acc[:].rearrange("p (c n) -> p c n", c=4)

            gv_tok = [0, 0]
            out_pkv_tok = [0]
            owc_tok = [0]
            gvmm_tok = [0]
            ef_tok = [0, 0]
            h_spill_tok = [0, 0]
            hscl_tok = [0, 0]
            lad_tok = [0, 0]
            nq_tok = [0, 0]

            @on("vector")
            def _(v):
                v.tensor_scalar(out=AR2[:, 0:16], in0=AR[:, 32:48],
                                scalar1=SC[:, 0:1], scalar2=None, op0=ALU.mult)
                v.tensor_scalar(out=AR2[:, 16:32], in0=AR[:, 32:48],
                                scalar1=SC[:, 1:2], scalar2=None, op0=ALU.mult)
                sV.inc(v.memset(AR2[:, 32:48], 0.0))

            def gv_block(b3):
                r0 = b3 * B3B

                @on("sync")
                def _(sy, r0=r0):
                    sy.wait_ge(sT.h, gvmm_tok[0])
                    sy.dma_start(out=nxl[:], in_=nxs_img[:, :, r0:r0 + B3B]
                                 ).then_inc(dR.h, 16)
                    dR.n += 16
                nxtok = dR.n

                for hcol in range(16):
                    @on("sync")
                    def _(sy, hcol=hcol):
                        sy.wait_ge(sT.h, gvmm_tok[0])
                        sy.dma_start(out=gwc[:], in_=gwt_img[:, :, hcol * 512:
                                                           (hcol + 1) * 512]
                                     ).then_inc(dR.h, 16)
                        sy.dma_start(out=vwc[:], in_=vwt_img[:, :, hcol * 512:
                                                           (hcol + 1) * 512]
                                     ).then_inc(dR.h, 16)
                        sy.dma_start(out=b2t[:, 0:512],
                                     in_=rg_s[:, hcol * 512:(hcol + 1) * 512]
                                     ).then_inc(dB.h, 16)
                        sy.dma_start(out=b2t[:, 512:1024],
                                     in_=rv_s[:, hcol * 512:(hcol + 1) * 512]
                                     ).then_inc(dB.h, 16)
                        dR.n += 32
                        dB.n += 32
                    wtok = (dR.n, dB.n)

                    for tt4 in range(4):
                        tcol = b3 * 4 + tt4
                        tsl = slice(r0 + tt4 * 128, r0 + (tt4 + 1) * 128)
                        kpar = tt4 % 2
                        pgate = pj[0][:] if kpar == 0 else pkv[:, 0:512]
                        pval = pj[1][:] if kpar == 0 else pkv[:, 512:1024]

                        @on("tensor")
                        def _(t, tt4=tt4, tsl=tsl, wtok=wtok, nxtok=nxtok,
                              kpar=kpar, pgate=pgate, pval=pval):
                            t.wait_ge(dR.h, wtok[0])
                            t.wait_ge(dB.h, wtok[1])
                            t.wait_ge(dR.h, nxtok)
                            t.wait_ge(sV.h, max(gv_tok[kpar], out_pkv_tok[0]))
                            for pb, wimg, bslc in ((pgate, gwc_img, b2t[:, 0:512]),
                                                   (pval, vwc_img, b2t[:, 512:1024])):
                                t.matmul(out=pb, lhsT=la[:, tsl], rhs=bslc,
                                         start=True, stop=False)
                                t.matmul(out=pb, lhsT=lb[:, tsl], rhs=bslc,
                                         start=False, stop=False)
                                for dc in range(16):
                                    li = t.matmul(out=pb,
                                                  lhsT=nxl_img[:, dc,
                                                       tt4 * 128:(tt4 + 1) * 128],
                                                  rhs=wimg[:, dc, :],
                                                  start=False, stop=(dc == 15))
                            sT.inc(li)
                        mmtok = sT.n
                        gvmm_tok[0] = mmtok

                        efg = ef[0] if kpar == 0 else ef[2]
                        efv = ef[1] if kpar == 0 else ef[3]

                        @on("vector")
                        def _(v, tt4=tt4, tcol=tcol, mmtok=mmtok, kpar=kpar,
                              pgate=pgate, pval=pval, efg=efg, efv=efv):
                            v.wait_ge(sT.h, mmtok)
                            v.wait_ge(sA.h, ef_tok[kpar])
                            v.tensor_scalar(out=efg[:], in0=pgate,
                                            scalar1=AR2[:, tcol:tcol + 1],
                                            scalar2=None, op0=ALU.mult)
                            sV.inc(v.tensor_scalar(out=efv[:], in0=pval,
                                            scalar1=AR2[:, 16 + tcol:17 + tcol],
                                            scalar2=None, op0=ALU.mult))
                        vtok = sV.n
                        gv_tok[kpar] = vtok

                        @on("scalar")
                        def _(a, vtok=vtok, efg=efg):
                            a.wait_ge(sV.h, vtok)
                            sA.inc(a.activation(out=efg[:], in_=efg[:], func=AF.Silu))
                        atok = sA.n

                        @on("vector")
                        def _(v, tcol=tcol, atok=atok, kpar=kpar, efg=efg, efv=efv):
                            v.wait_ge(sA.h, atok)
                            v.wait_ge(dS.h, h_spill_tok[kpar])
                            sV.inc(v.tensor_tensor(out=hgv[kpar][:], in0=efg[:],
                                                   in1=efv[:], op=ALU.mult))
                        vtok2 = sV.n

                        @on("scalar")
                        def _(a, vtok2=vtok2, kpar=kpar, efg=efg):
                            a.wait_ge(sV.h, vtok2)
                            sA.inc(a.activation(out=efg[:], in_=hgv[kpar][:],
                                                func=AF.Abs))
                        atok2 = sA.n
                        ef_tok[kpar] = atok2

                        @on("vector")
                        def _(v, tcol=tcol, atok2=atok2, efg=efg):
                            v.wait_ge(sA.h, atok2)
                            v.tensor_reduce(out=A2[:, 0:1], in_=efg[:], axis=AX.X,
                                            op=ALU.max)
                            v.drain()
                            sV.inc(v.tensor_tensor(out=AR2[:, 32 + tcol:33 + tcol],
                                            in0=AR2[:, 32 + tcol:33 + tcol],
                                            in1=A2[:, 0:1], op=ALU.max))
                        vtok3 = sV.n

                        @on("sync")
                        def _(sy, tsl=tsl, hcol=hcol, vtok3=vtok3, kpar=kpar):
                            sy.wait_ge(sV.h, vtok3)
                            sy.dma_start(out=h_sc[tsl, hcol * 512:(hcol + 1) * 512],
                                         in_=hgv[kpar][:]).then_inc(dS.h, 16)
                            dS.n += 16
                        h_spill_tok[kpar] = dS.n
                    yield

                csl = slice(32 + b3 * 4, 32 + b3 * 4 + 4)
                ssl = slice(48 + b3 * 4, 48 + b3 * 4 + 4)
                osl = slice(0 + b3 * 4, 4 + b3 * 4)

                @on("vector")
                def _(v):
                    v.wait_ge(dS.h, dS.n)
                    v.tensor_scalar(out=AR2[:, csl], in0=AR2[:, csl], scalar1=1e-5,
                                    scalar2=None, op0=ALU.max)
                    v.drain()
                    v.reciprocal(out=AR2[:, ssl], in_=AR2[:, csl])
                    v.drain()
                    v.tensor_scalar(out=AR2[:, ssl], in0=AR2[:, ssl], scalar1=127.0,
                                    scalar2=None, op0=ALU.mult)
                    v.tensor_scalar(out=AR2[:, osl], in0=AR2[:, csl],
                                    scalar1=1.0 / 127.0, scalar2=None, op0=ALU.mult)
                    v.drain()
                    v.tensor_scalar(out=AR2[:, osl], in0=AR2[:, osl],
                                    scalar1=SC[:, 2:3], scalar2=None, op0=ALU.mult)
                    v.drain()
                    v.reciprocal(out=A2[:, 4:8], in_=AR2[:, osl])
                    v.drain()
                    v.tensor_copy(out=ihi16[:, b3 * 4:b3 * 4 + 4], in_=A2[:, 4:8])
                    v.drain()
                    sV.inc(v.tensor_tensor(out=ilo16[:, b3 * 4:b3 * 4 + 4], in0=A2[:, 4:8],
                                    in1=ihi16[:, b3 * 4:b3 * 4 + 4], op=ALU.subtract))
                gam_tok = sV.n

                @on("tensor")
                def _(t, gam_tok=gam_tok):
                    t.wait_ge(sV.h, gam_tok)
                    t.wait_ge(sA.h, sA.n)
                    t.transpose(out=pxb[0:16, 0:128], in_=ihi16[:], identity=eyeb[:])
                    sT.inc(t.transpose(out=pxb[0:16, 128:256], in_=ilo16[:], identity=eyeb[:]))
                ttok = sT.n

                @on("scalar")
                def _(a, ttok=ttok):
                    a.wait_ge(sT.h, ttok)
                    a.wait_ge(dB.h, dB.n)
                    sA.inc(a.copy(out=itr[:, 0:256], in_=pxb[0:16, 0:256]))
                atok = sA.n

                @on("sync")
                def _(sy, b3=b3, r0=r0, atok=atok):
                    sy.wait_ge(sA.h, atok)
                    sy.wait_ge(sT.h, sT.n)
                    for dst, c0 in ((lao[b3 % 2], 0), (lbo[b3 % 2], 128)):
                        sy.dma_start(
                            out=dst[0:1, :].rearrange("o (c p) -> o c p", p=128),
                            in_=itr[b3 * 4:b3 * 4 + 4, c0:c0 + 128]).then_inc(dB.h, 16)
                        sy.dma_start(
                            out=dst[1:2, :].rearrange("o (c p) -> o c p", p=128),
                            in_=itr[b3 * 4:b3 * 4 + 4, (c0 + 128) % 256:(c0 + 128) % 256 + 128]
                            ).then_inc(dB.h, 16)
                        dB.n += 32

            def qout_block(b3):
                r0 = b3 * B3B
                for half in range(2):
                    for tt4 in range(4):
                        tcol = b3 * 4 + tt4
                        tsl = slice(r0 + tt4 * 128, r0 + (tt4 + 1) * 128)
                        for hp in range(4):
                            hcol = half * 8 + hp * 2
                            sl = hp % 2

                            @on("sync")
                            def _(sy, tsl=tsl, hcol=hcol, sl=sl):
                                sy.wait_ge(sV.h, lad_tok[sl])
                                sy.dma_start(out=hq[sl][:],
                                             in_=h_sc[tsl, hcol * 512:(hcol + 2) * 512]
                                             ).then_inc(dR.h, 16)
                                dR.n += 16
                            htok = dR.n

                            @on("scalar")
                            def _(a, sl=sl, tcol=tcol, htok=htok):
                                a.wait_ge(dR.h, htok)
                                a.wait_ge(sV.h, lad_tok[sl])
                                sA.inc(a.activation(out=hq[sl][:], in_=hq[sl][:],
                                                    func=AF.Identity,
                                                    scale=AR2[:, 48 + tcol:49 + tcol]))
                            hscl_tok[sl] = sA.n
                            stok = sA.n

                            @on("vector")
                            def _(e, sl=sl, stok=stok):
                                e.wait_ge(sA.h, stok)
                                e.wait_ge(sT.h, nq_tok[0])
                                e.wait_ge(dF.h, dF.n)
                                sV.inc(ladder(e, lad_v, hq[sl][:], nq[sl][:], None))
                            lad_tok[sl] = sV.n
                            etok = sV.n

                            @on("tensor")
                            def _(t, etok=etok, sl=sl):
                                t.wait_ge(sV.h, etok)
                                t.wait_ge(sA.h, sA.n)
                                for i in range(8):
                                    li = t.transpose(out=pxb[:, i * 128:(i + 1) * 128],
                                                in_=nq[sl][:, i * 128:(i + 1) * 128],
                                                identity=eyeb[:])
                                sT.inc(li)
                            nq_tok[0] = sT.n
                            ttok2 = sT.n

                            @on("scalar")
                            def _(a, hp=hp, tt4=tt4, ttok2=ttok2):
                                a.wait_ge(sT.h, ttok2)
                                dst = nhf_img[:, hp * 8:(hp + 1) * 8,
                                              tt4 * 128:(tt4 + 1) * 128]
                                sA.inc(a.copy(out=dst,
                                       in_=pxb[:].rearrange("p (c n) -> p c n", c=8)))
                            yield
                    nh_tok = (sA.n, sV.n, sG.n)

                    for dcol in range(8):
                        @on("sync")
                        def _(sy, half=half, dcol=dcol):
                            sy.wait_ge(sT.h, sT.n)
                            sy.dma_start(out=owc[:],
                                         in_=owt_img[:, half * 32:(half + 1) * 32,
                                                     dcol * 256:(dcol + 1) * 256]
                                         ).then_inc(dR.h, 16)
                            sy.dma_start(out=b2o3[:],
                                         in_=ro_s[:, dcol * 256:(dcol + 1) * 256]
                                         ).then_inc(dB.h, 16)
                            dR.n += 16
                            dB.n += 16
                        otok = (dR.n, dB.n)

                        for tt4 in range(4):
                            tcol = b3 * 4 + tt4
                            tsl = slice(r0 + tt4 * 128, r0 + (tt4 + 1) * 128)
                            pb = pkv[:, tt4 * 512:tt4 * 512 + 256]

                            @on("tensor")
                            def _(t, half=half, tt4=tt4, tsl=tsl, pb=pb, otok=otok,
                                  nh_tok=nh_tok):
                                t.wait_ge(dR.h, otok[0])
                                t.wait_ge(dB.h, otok[1])
                                t.wait_ge(sA.h, nh_tok[0])
                                t.wait_ge(sV.h, sV.n)
                                first = True
                                if half == 0:
                                    lsl = slice(tt4 * 128, (tt4 + 1) * 128)
                                    t.matmul(out=pb, lhsT=lao[b3 % 2][:, lsl],
                                             rhs=b2o3[:, 0:256], start=True, stop=False)
                                    t.matmul(out=pb, lhsT=lbo[b3 % 2][:, lsl],
                                             rhs=b2o3[:, 0:256], start=False, stop=False)
                                    first = False
                                for hc in range(32):
                                    li = t.matmul(out=pb,
                                             lhsT=nhf_img[:, hc, tt4 * 128:(tt4 + 1) * 128],
                                             rhs=owc_img[:, hc, :],
                                             start=first, stop=(hc == 31))
                                    first = False
                                sT.inc(li)
                            mmtok = sT.n

                            @on("vector")
                            def _(v, half=half, tt4=tt4, dcol=dcol, tcol=tcol,
                                  pb=pb, mmtok=mmtok):
                                v.wait_ge(sT.h, mmtok)
                                dsl = facc_img[:, tt4, dcol * 256:(dcol + 1) * 256]
                                if half == 0:
                                    sV.inc(v.tensor_copy(out=dsl, in_=pb))
                                else:
                                    v.wait_ge(dF.h, dF.n)
                                    v.tensor_tensor(out=lad_v[2][:, 0:256], in0=pb,
                                                    in1=dsl, op=ALU.add)
                                    sV.inc(v.tensor_scalar(out=lad_v[3][:, 0:256],
                                                    in0=lad_v[2][:, 0:256],
                                                    scalar1=AR2[:, tcol:tcol + 1],
                                                    scalar2=None, op0=ALU.mult))
                            vtok = sV.n

                            out_pkv_tok[0] = sV.n
                            if half == 1:
                                @on("sync")
                                def _(sy, tsl=tsl, dcol=dcol, vtok=vtok):
                                    sy.wait_ge(sV.h, vtok)
                                    sy.dma_start(out=ffn_s[tsl, dcol * 256:(dcol + 1) * 256],
                                                 in_=lad_v[3][:, 0:256]).then_inc(dF.h, 16)
                                    dF.n += 16
                        yield


            def drain(g, n=10 ** 9):
                for _ in range(n):
                    if next(g, "END") == "END":
                        return True
                return False

            prev_q = None
            for b3 in range(NB3B):
                g = gv_block(b3)
                while True:
                    done_g = drain(g, 1)
                    if prev_q is not None:
                        drain(prev_q, 1)
                    if done_g:
                        break
                if prev_q is not None:
                    drain(prev_q)
                prev_q = qout_block(b3)
            drain(prev_q)

        def emit_p3a(P):
            (q_st, qfh, qfl, xfh, xfl, at1, u_tm, mfh, mfl, wstg, wc_hi, wc_lo,
             ef32, ytile, xtile, ftile, b2u, b2o) = P
            xnhs_img = xn_hi_s.rearrange("p (c n) -> p c n", c=16)
            xnls_img = xn_lo_s.rearrange("p (c n) -> p c n", c=16)
            bank_tok = [(0, 0), (0, 0)]
            mb_tok = [0, 0]
            qpx_tok = [0, 0]
            wpar_tok = [0, 0]
            stg_tok = [(0, 0), (0, 0)]
            for b4 in range(NB3A):
                r0 = b4 * B3A
                t1_toks = []
                for tl in range(2):
                    rr = r0 + tl * 128

                    @on("sync")
                    def _(sy, rr=rr, tl=tl):
                        sy.wait_ge(sT.h, sT.n)
                        sy.dma_start(out=xfh[tl][:], in_=xnhs_img[:, :, rr:rr + 128]
                                     ).then_inc(dR.h, 16)
                        sy.dma_start(out=xfl[tl][:], in_=xnls_img[:, :, rr:rr + 128]
                                     ).then_inc(dR.h, 16)
                        dR.n += 32

                    for qsrc, dst in ((q_hi_s, qfh[tl]), (q_lo_s, qfl[tl])):
                        @on("sync")
                        def _(sy, qsrc=qsrc, rr=rr):
                            sy.wait_ge(sT.h, sT.n)
                            sy.dma_start(out=q_st[:], in_=qsrc[rr:rr + 128, :]
                                         ).then_inc(dR.h, 16)
                            dR.n += 16
                        qtok = dR.n
                        for grp in range(4):
                            ph_ = grp % 2

                            @on("tensor")
                            def _(t, grp=grp, qtok=qtok, ph_=ph_):
                                t.wait_ge(dR.h, qtok)
                                t.wait_ge(sA.h, qpx_tok[ph_])
                                for i in range(4):
                                    c = grp * 4 + i
                                    li = t.transpose(
                                        out=pxb[:, (ph_ * 4 + i) * 128:(ph_ * 4 + i + 1) * 128],
                                        in_=q_st[:, c * 128:(c + 1) * 128],
                                        identity=eyeb[:])
                                sT.inc(li)
                            ttok = sT.n

                            @on("scalar")
                            def _(a, dst=dst, grp=grp, ttok=ttok, ph_=ph_):
                                a.wait_ge(sT.h, ttok)
                                sA.inc(a.copy(out=dst[:, grp * 512:(grp + 1) * 512],
                                       in_=pxb[:, ph_ * 512:(ph_ + 1) * 512]))
                            qpx_tok[ph_] = sA.n
                    qf_tok = sA.n

                    @on("tensor")
                    def _(t, tl=tl, qf_tok=qf_tok):
                        t.wait_ge(sA.h, qf_tok)
                        t.wait_ge(sV.h, sV.n)
                        for hd in range(16):
                            c = hd * 128
                            for pi, (lh, rh) in enumerate(((qfh[tl], kv_hi),
                                                           (qfh[tl], kv_lo),
                                                           (qfl[tl], kv_hi))):
                                li = t.matmul(out=pkv[:, c:c + 128],
                                              lhsT=lh[:, c:c + 128],
                                              rhs=rh[:, c:c + 128],
                                              start=(pi == 0 and hd % 4 == 0),
                                              stop=(pi == 2 and hd % 4 == 3))
                        sT.inc(li)
                    qkv_tok = sT.n

                    @on("vector")
                    def _(v, tl=tl, qkv_tok=qkv_tok):
                        v.wait_ge(sT.h, qkv_tok)
                        v.tensor_copy(out=at1[tl][:], in_=pkv[:])
                        v.reduce_sum(out=A2[:, 0:1], in_=at1[tl][:], axis=AX.X)
                        v.tensor_tensor(out=at1[tl][:], in0=at1[tl][:], in1=at1[tl][:],
                                        op=ALU.mult)
                        v.reduce_sum(out=A2[:, 1:2], in_=at1[tl][:], axis=AX.X)
                        v.drain()
                        v.tensor_scalar(out=A2[:, 2:3], in0=A2[:, 0:1],
                                        scalar1=1.0 / HH, scalar2=None, op0=ALU.mult)
                        v.drain()
                        v.tensor_tensor(out=A2[:, 3:4], in0=A2[:, 2:3],
                                        in1=A2[:, 2:3], op=ALU.mult)
                        v.tensor_scalar(out=A2[:, 4:5], in0=A2[:, 1:2],
                                        scalar1=1.0 / HH, scalar2=None, op0=ALU.mult)
                        v.drain()
                        sV.inc(v.tensor_tensor(out=A2[:, 4:5], in0=A2[:, 4:5],
                                        in1=A2[:, 3:4], op=ALU.subtract))
                    vtok = sV.n

                    @on("scalar")
                    def _(a, vtok=vtok):
                        a.wait_ge(sV.h, vtok)
                        sA.inc(a.activation(out=A2[:, 5:6], in_=A2[:, 4:5], func=AF.Sqrt,
                                     bias=SC[:, 6:7]))
                    atok = sA.n

                    @on("vector")
                    def _(v, tl=tl, atok=atok):
                        v.wait_ge(sA.h, atok)
                        v.reciprocal(out=A2[:, 5:6], in_=A2[:, 5:6])
                        v.drain()
                        sV.inc(v.tensor_scalar(out=at1[tl][:], in0=pkv[:], scalar1=A2[:, 2:3],
                                        scalar2=A2[:, 5:6], op0=ALU.subtract,
                                        op1=ALU.mult))
                    t1_toks.append(sV.n)

                for fcol in range(4):
                    par = fcol % 2
                    splitok = emit_wsplit(wu_h, fcol, wc_hi[par], wc_lo[par],
                                          wstg, stg_tok, wpar_tok, par)

                    for tl in range(2):
                        bi = tl

                        @on("tensor")
                        def _(t, tl=tl, fcol=fcol, par=par, splitok=splitok, bi=bi):
                            t.wait_ge(sA.h, splitok[0])
                            t.wait_ge(sV.h, splitok[1])
                            t.wait_ge(sA.h, bank_tok[bi][0])
                            t.wait_ge(sV.h, bank_tok[bi][1])
                            t.matmul(out=pj[bi][:], lhsT=ones2[:],
                                     rhs=b2u[:, fcol * 512:(fcol + 1) * 512],
                                     start=True, stop=False)
                            xfh_img = xfh[tl][:].rearrange("p (c n) -> p c n", c=16)
                            xfl_img = xfl[tl][:].rearrange("p (c n) -> p c n", c=16)
                            for dc in range(16):
                                t.matmul(out=pj[bi][:], lhsT=xfh_img[:, dc, :],
                                         rhs=wc_hi[par][:, dc * 512:(dc + 1) * 512],
                                         start=False, stop=False)
                            for dc in range(16):
                                t.matmul(out=pj[bi][:], lhsT=xfh_img[:, dc, :],
                                         rhs=wc_lo[par][:, dc * 512:(dc + 1) * 512],
                                         start=False, stop=False)
                            for dc in range(16):
                                li = t.matmul(out=pj[bi][:], lhsT=xfl_img[:, dc, :],
                                         rhs=wc_hi[par][:, dc * 512:(dc + 1) * 512],
                                         start=False, stop=(dc == 15))
                            sT.inc(li)
                        mmtok = sT.n
                        wpar_tok[par] = mmtok

                        @on("scalar")
                        def _(a, tl=tl, fcol=fcol, bi=bi, mmtok=mmtok):
                            a.wait_ge(sT.h, mmtok)
                            sA.inc(a.activation(out=u_tm[tl][:, fcol * 512:(fcol + 1) * 512],
                                         in_=pj[bi][:], func=AF.Silu))
                        bank_tok[bi] = (sA.n, bank_tok[bi][1])

                u_tok = sA.n
                for tl in range(2):
                    for grp in range(4):
                        p_ = grp % 2
                        pa = ptr[:] if p_ == 0 else pkv[:, 0:512]
                        pb_ = pj[0][:] if p_ == 0 else pj[1][:]

                        @on("tensor")
                        def _(t, tl=tl, grp=grp, u_tok=u_tok, t1t=t1_toks[tl],
                              p_=p_, pa=pa, pb_=pb_):
                            t.wait_ge(sA.h, u_tok)
                            t.wait_ge(sV.h, max(t1t, mb_tok[p_]))
                            for i in range(4):
                                c = grp * 4 + i
                                t.transpose(out=pa[:, i * 128:(i + 1) * 128],
                                            in_=at1[tl][:, c * 128:(c + 1) * 128],
                                            identity=eyef[:])
                            for i in range(4):
                                c = grp * 4 + i
                                li = t.transpose(out=pb_[:, i * 128:(i + 1) * 128],
                                                 in_=u_tm[tl][:, c * 128:(c + 1) * 128],
                                                 identity=eyef[:])
                            sT.inc(li)
                        ttok = sT.n

                        @on("vector")
                        def _(v, tl=tl, grp=grp, ttok=ttok, pa=pa, pb_=pb_):
                            v.wait_ge(sT.h, ttok)
                            for i in range(4):
                                c = grp * 4 + i
                                v.tensor_scalar(out=ef32[0][:, i * 128:(i + 1) * 128],
                                                in0=pa[:, i * 128:(i + 1) * 128],
                                                scalar1=g_cols[:, c:c + 1],
                                                scalar2=b_cols[:, c:c + 1],
                                                op0=ALU.mult, op1=ALU.add)
                            v.drain()
                            v.tensor_tensor(out=ef32[1][:], in0=ef32[0][:],
                                            in1=pb_[:, 0:512], op=ALU.mult)
                            v.tensor_copy(out=mfh[tl][:, grp * 512:(grp + 1) * 512],
                                          in_=ef32[1][:])
                            sV.inc(v.tensor_tensor(
                                out=mfl[tl][:, grp * 512:(grp + 1) * 512],
                                in0=ef32[1][:],
                                in1=mfh[tl][:, grp * 512:(grp + 1) * 512],
                                op=ALU.subtract))
                        mb_tok[p_] = sV.n
                m_tok = (sV.n, sA.n)

                for dcol in range(4):
                    par = dcol % 2
                    splitok = emit_wsplit(wo_h, dcol, wc_hi[par], wc_lo[par],
                                          wstg, stg_tok, wpar_tok, par)

                    for tl in range(2):
                        rr = r0 + tl * 128
                        bi = tl

                        @on("sync")
                        def _(sy, rr=rr, dcol=dcol, tl=tl):
                            sy.wait_ge(sV.h, sV.n)
                            sy.dma_start(out=xtile[:],
                                         in_=x_h[rr:rr + 128,
                                                 dcol * 512:(dcol + 1) * 512]
                                         ).then_inc(dX.h, 16)
                            sy.dma_start(out=ftile[:],
                                         in_=ffn_s[rr:rr + 128,
                                                   dcol * 512:(dcol + 1) * 512]
                                         ).then_inc(dR.h, 16)
                            dX.n += 16
                            dR.n += 16
                        xftok = (dX.n, dR.n)

                        @on("tensor")
                        def _(t, tl=tl, dcol=dcol, par=par, splitok=splitok, bi=bi,
                              m_tok=m_tok):
                            t.wait_ge(sA.h, splitok[0])
                            t.wait_ge(sV.h, splitok[1])
                            t.wait_ge(sV.h, m_tok[0])
                            t.wait_ge(sA.h, m_tok[1])
                            t.matmul(out=pj[bi][:], lhsT=ones2[:],
                                     rhs=b2o[:, dcol * 512:(dcol + 1) * 512],
                                     start=True, stop=False)
                            for hc in range(16):
                                t.matmul(out=pj[bi][:],
                                         lhsT=mfh[tl][:, hc * 128:(hc + 1) * 128],
                                         rhs=wc_hi[par][:, hc * 512:(hc + 1) * 512],
                                         start=False, stop=False)
                            for hc in range(16):
                                t.matmul(out=pj[bi][:],
                                         lhsT=mfh[tl][:, hc * 128:(hc + 1) * 128],
                                         rhs=wc_lo[par][:, hc * 512:(hc + 1) * 512],
                                         start=False, stop=False)
                            for hc in range(16):
                                li = t.matmul(out=pj[bi][:],
                                         lhsT=mfl[tl][:, hc * 128:(hc + 1) * 128],
                                         rhs=wc_hi[par][:, hc * 512:(hc + 1) * 512],
                                         start=False, stop=(hc == 15))
                            sT.inc(li)
                        mmtok = sT.n
                        wpar_tok[par] = mmtok

                        @on("vector")
                        def _(v, tl=tl, bi=bi, mmtok=mmtok, xftok=xftok):
                            v.wait_ge(sT.h, mmtok)
                            v.wait_ge(dX.h, xftok[0])
                            v.wait_ge(dR.h, xftok[1])
                            v.wait_ge(dO.h, dO.n)
                            v.tensor_tensor(out=ytile[:], in0=pj[bi][:],
                                            in1=xtile[:], op=ALU.add)
                            sV.inc(v.tensor_tensor(out=ytile[:], in0=ytile[:],
                                            in1=ftile[:], op=ALU.add))
                        bank_tok[bi] = (bank_tok[bi][0], sV.n)
                        vtok = sV.n

                        @on("sync")
                        def _(sy, rr=rr, dcol=dcol, tl=tl, vtok=vtok):
                            sy.wait_ge(sV.h, vtok)
                            sy.dma_start(out=y_h[rr:rr + 128,
                                                 dcol * 512:(dcol + 1) * 512],
                                         in_=ytile[:]).then_inc(dO.h, 16)
                            dO.n += 16

        with ExitStack() as s0:
            small2k = s0.enter_context(nc.sbuf_tensor("small2k", [2, 2048], F32))
            smbf2k = s0.enter_context(nc.sbuf_tensor("smbf2k", [2, 2048], BF16))
            smlo2k = s0.enter_context(nc.sbuf_tensor("smlo2k", [1, 2048], BF16))
            emit_setup(small2k, smbf2k, smlo2k)
            barrier()
        with ExitStack() as ph:
            def sb1(name, shape, dt):
                return ph.enter_context(nc.sbuf_tensor(name, shape, dt))
            x_t = [sb1(f"x_t{i}", [128, D], F32) for i in range(2)]
            emit_p0(x_t)
            barrier()
            if upto not in ("p0",):
                xn_tm = sb1("xn_tm", [128, D], F32)
                nx_tm = sb1("nx_tm", [128, D], BF16)
                xnf_hi = sb1("xnf_hi", [128, 16 * B1], BF16)
                xnf_lo = sb1("xnf_lo", [128, 16 * B1], BF16)
                wstg = [sb1(f"wstg{i}", [128, 1024], F32) for i in range(2)]
                wc_hi = [sb1(f"wc_hi{i}", [128, 16 * 512], BF16) for i in range(2)]
                wc_lo = [sb1(f"wc_lo{i}", [128, 16 * 512], BF16) for i in range(2)]
                b2qk = sb1("b2qk", [2, HH], BF16)
                b2v = sb1("b2v", [2, HH], BF16)

                @on("sync")
                def _(sy):
                    sy.dma_start(out=b2qk[:], in_=bq2_s[:]).then_inc(dB.h, 16)
                    sy.dma_start(out=b2v[:], in_=bv2_s[:]).then_inc(dB.h, 16)
                    dB.n += 32
                b2tok = dB.n

                @on("tensor")
                def _(t, b2tok=b2tok):
                    t.wait_ge(dB.h, b2tok)

                kh_hi = sb1("kh_hi", [128, 4 * 512], BF16)
                kh_lo = sb1("kh_lo", [128, 4 * 512], BF16)
                vh_hi = sb1("vh_hi", [128, 4 * 512], BF16)
                vh_lo = sb1("vh_lo", [128, 4 * 512], BF16)
                ef32 = [sb1(f"ef32{i}", [128, 512], F32) for i in range(3)]
                lad_v = [sb1(f"lad_v{i}", [128, 512], F32) for i in range(4)]
                emit_p1((x_t, xn_tm, nx_tm, xnf_hi, xnf_lo, wstg, wc_hi, wc_lo,
                         kh_hi, kh_lo, vh_hi, vh_lo, ef32, lad_v, b2qk, b2v))
                barrier()

        if upto in ("tern", "p3b", "all"):
            with ExitStack() as ph2:
                def sb2(name, shape, dt):
                    return ph2.enter_context(nc.sbuf_tensor(name, shape, dt))
                tstr = [sb2(f"tstr{i}", [128, 2048], F32) for i in range(2)]
                tp1 = sb2("tp1", [128, 2048], F32)
                tern2 = [sb2(f"tern2{i}", [128, 2048], BF16) for i in range(2)]
                ttmp = [sb2(f"ttmp{i}", [128, 2048], BF16) for i in range(4)]
                smallF = sb2("smallF", [1, HID], F32)
                smbfF = sb2("smbfF", [1, HID], BF16)
                smloF = sb2("smloF", [1, HID], BF16)
                emit_tern((tstr, tp1, tern2, ttmp, smallF, smbfF, smloF))
                barrier()

        if upto in ("p3b", "all"):
            with ExitStack() as ph3:
                def sb3(name, shape, dt):
                    return ph3.enter_context(nc.sbuf_tensor(name, shape, dt))
                nxl = sb3("nxl", [128, 16 * 512], BF16)
                gwc = sb3("gwc", [128, 16 * 512], BF16)
                vwc = sb3("vwc", [128, 16 * 512], BF16)
                hq = [sb3(f"hq{i}", [128, 1024], F32) for i in range(2)]
                nq0 = sb3("nq0", [128, 1024], BF16)
                nq = [nq0, nq0]
                n_hf = sb3("n_hf", [128, 32 * 512], BF16)
                owc = sb3("owc", [128, 32 * 256], BF16)
                ffn_acc = sb3("ffn_acc", [128, 4 * 2048], F32)
                AR2 = sb3("AR2", [128, 64], F32)
                ef = [sb3(f"ef{i}", [128, 512], F32) for i in range(4)]
                lad_v3 = [sb3(f"lad_v3{i}", [128, 1024], F32) for i in range(4)]
                lad_g3 = None
                lao = [sb3(f"lao{i}", [2, B3B], BF16) for i in range(2)]
                lbo = [sb3(f"lbo{i}", [2, B3B], BF16) for i in range(2)]
                b2o3 = sb3("b2o3", [2, 256], BF16)
                hgv = [sb3(f"hgv{i}", [128, 512], F32) for i in range(2)]
                emit_p3b((nxl, gwc, vwc, hq, nq, n_hf, owc, ffn_acc, AR2, ef,
                          lad_v3, lad_g3, lao, lbo, b2o3, hgv))
                barrier()

        if upto == "all":
            with ExitStack() as ph4:
                def sb4(name, shape, dt):
                    return ph4.enter_context(nc.sbuf_tensor(name, shape, dt))
                q_st = sb4("q_st", [128, 2048], BF16)
                qfh0 = sb4("qfh0", [128, 2048], BF16)
                qfl0 = sb4("qfl0", [128, 2048], BF16)
                qfh = [qfh0, qfh0]
                qfl = [qfl0, qfl0]
                xfh = [sb4(f"xfh{i}", [128, 2048], BF16) for i in range(2)]
                xfl = [sb4(f"xfl{i}", [128, 2048], BF16) for i in range(2)]
                at1 = [sb4(f"at1{i}", [128, 2048], F32) for i in range(2)]
                u_tm = [sb4(f"u_tm{i}", [128, 2048], F32) for i in range(2)]
                mfh = [sb4(f"mfh{i}", [128, 2048], BF16) for i in range(2)]
                mfl = [sb4(f"mfl{i}", [128, 2048], BF16) for i in range(2)]
                wstg4 = [sb4(f"wstg4{i}", [128, 1024], F32) for i in range(2)]
                wc_hi4 = [sb4(f"wc_hi4{i}", [128, 16 * 512], BF16) for i in range(2)]
                wc_lo4 = [sb4(f"wc_lo4{i}", [128, 16 * 512], BF16) for i in range(2)]
                ef324 = [sb4(f"ef324{i}", [128, 512], F32) for i in range(2)]
                b2u = sb4("b2u", [2, HH], BF16)
                b2o = sb4("b2o", [2, D], BF16)

                @on("sync")
                def _(sy):
                    sy.dma_start(out=b2u[:], in_=bu2_s[:]).then_inc(dB.h, 16)
                    sy.dma_start(out=b2o[:], in_=bo2_s[:]).then_inc(dB.h, 16)
                    dB.n += 32
                b4tok = dB.n

                @on("tensor")
                def _(t, b4tok=b4tok):
                    t.wait_ge(dB.h, b4tok)

                ytile = sb4("ytile", [128, 512], F32)
                xtile = sb4("xtile", [128, 512], F32)
                ftile = sb4("ftile", [128, 512], F32)
                emit_p3a((q_st, qfh, qfl, xfh, xfl, at1, u_tm, mfh, mfl, wstg4,
                          wc_hi4, wc_lo4, ef324, ytile, xtile, ftile, b2u, b2o))

        @on("gpsimd")
        def _(g):
            g.wait_ge(dS.h, dS.n)
            g.wait_ge(dX.h, dX.n)
            g.wait_ge(dW.h, dW.n)
            g.wait_ge(dB.h, dB.n)
            g.wait_ge(dR.h, dR.n)
            g.wait_ge(dF.h, dF.n)
            if dO.n:
                g.wait_ge(dO.h, dO.n)

    return nc


def _in_maps(inputs):
    x = np.ascontiguousarray(np.asarray(inputs["L0_attn_x"], dtype=np.float32))
    eyef = np.eye(128, dtype=np.float32)
    eyeb = np.eye(128, dtype=ml_dtypes.bfloat16)
    shared = {
        "Wqk": np.ascontiguousarray(inputs["L0_attn_Wqk"]),
        "bqk": np.ascontiguousarray(inputs["L0_attn_bqk"]),
        "Wv": np.ascontiguousarray(inputs["L0_attn_Wv"]),
        "bv": np.ascontiguousarray(inputs["L0_attn_bv"]),
        "Wu": np.ascontiguousarray(inputs["L0_attn_Wu"]),
        "bu": np.ascontiguousarray(inputs["L0_attn_bu"]),
        "Wo": np.ascontiguousarray(inputs["L0_attn_Wo"]),
        "bo": np.ascontiguousarray(inputs["L0_attn_bo"]),
        "ln_g": np.ascontiguousarray(inputs["L0_attn_ln_g"]),
        "ln_b": np.ascontiguousarray(inputs["L0_attn_ln_b"]),
        "gate_w": np.ascontiguousarray(inputs["L0_ffn_gate_w"]),
        "gate_b": np.ascontiguousarray(inputs["L0_ffn_gate_b"]),
        "val_w": np.ascontiguousarray(inputs["L0_ffn_val_w"]),
        "val_b": np.ascontiguousarray(inputs["L0_ffn_val_b"]),
        "out_w": np.ascontiguousarray(inputs["L0_ffn_out_w"]),
        "out_b": np.ascontiguousarray(inputs["L0_ffn_out_b"]),
        "eyef": eyef, "eyeb": eyeb,
    }
    maps = []
    for c in range(8):
        b, hf = c // 2, c % 2
        own = x[b, hf * OWN:(hf + 1) * OWN]
        oth = x[b, (1 - hf) * OWN:(2 - hf) * OWN]
        m = dict(shared)
        m["x_seq"] = np.ascontiguousarray(np.concatenate([own, oth], axis=0))
        maps.append(m)
    return maps


def kernel(**inputs):
    nc = build_program()
    maps = _in_maps(inputs)
    res = run_bass_kernel_spmd(nc, maps, core_ids=list(range(8)))
    y = np.empty((4, SEQ, D), np.float32)
    for c in range(8):
        b, hf = c // 2, c % 2
        y[b, hf * OWN:(hf + 1) * OWN] = res.results[c]["y"]
    return y


# version 17
# speedup vs baseline: 1.0671x; 1.0115x over previous
import numpy as np
import ml_dtypes
import concourse.bass as bass
import concourse.mybir as mybir
from concourse.bass_utils import run_bass_kernel_spmd
from contextlib import ExitStack

F32 = mybir.dt.float32
BF16 = mybir.dt.bfloat16
AF = mybir.ActivationFunctionType
ALU = mybir.AluOpType
AX = mybir.AxisListType

SEQ, OWN, D, HH, HID, HEADS = 4096, 2048, 2048, 2048, 8192, 16
LN_EPS = 1e-5
B1 = 512
NB1 = SEQ // B1
B3B = 512
NB3B = OWN // B3B
B3A = 256
NB3A = OWN // B3A
LADDER = (128.0, 64.0, 32.0, 16.0, 8.0, 4.0, 2.0, 1.0)


class Sem:
    def __init__(self, es, nc, name):
        self.h = es.enter_context(nc.semaphore(name))
        self.n = 0

    def inc(self, inst, k=1):
        inst.then_inc(self.h, k)
        self.n += k
        return self.n

    def mark(self, eng):
        eng.drain()
        eng.sem_inc(self.h, 1)
        self.n += 1
        return self.n


def build_program(upto="all", debug=False):
    nc = bass.Bass("TRN2", target_bir_lowering=False)
    dk = "ExternalOutput" if debug else "Internal"

    x_h = nc.dram_tensor("x_seq", [SEQ, D], F32, kind="ExternalInput").ap()
    wqk_h = nc.dram_tensor("Wqk", [D, HH], F32, kind="ExternalInput").ap()
    bqk_h = nc.dram_tensor("bqk", [HH], F32, kind="ExternalInput").ap()
    wv_h = nc.dram_tensor("Wv", [D, HH], F32, kind="ExternalInput").ap()
    bv_h = nc.dram_tensor("bv", [HH], F32, kind="ExternalInput").ap()
    wu_h = nc.dram_tensor("Wu", [D, HH], F32, kind="ExternalInput").ap()
    bu_h = nc.dram_tensor("bu", [HH], F32, kind="ExternalInput").ap()
    wo_h = nc.dram_tensor("Wo", [HH, D], F32, kind="ExternalInput").ap()
    bo_h = nc.dram_tensor("bo", [D], F32, kind="ExternalInput").ap()
    lng_h = nc.dram_tensor("ln_g", [HH], F32, kind="ExternalInput").ap()
    lnb_h = nc.dram_tensor("ln_b", [HH], F32, kind="ExternalInput").ap()
    gw_h = nc.dram_tensor("gate_w", [HID, D], F32, kind="ExternalInput").ap()
    gb_h = nc.dram_tensor("gate_b", [HID], F32, kind="ExternalInput").ap()
    vw_h = nc.dram_tensor("val_w", [HID, D], F32, kind="ExternalInput").ap()
    vb_h = nc.dram_tensor("val_b", [HID], F32, kind="ExternalInput").ap()
    ow_h = nc.dram_tensor("out_w", [D, HID], F32, kind="ExternalInput").ap()
    ob_h = nc.dram_tensor("out_b", [D], F32, kind="ExternalInput").ap()
    eyef_h = nc.dram_tensor("eyef", [128, 128], F32, kind="ExternalInput").ap()
    eyeb_h = nc.dram_tensor("eyeb", [128, 128], BF16, kind="ExternalInput").ap()
    y_h = nc.dram_tensor("y", [OWN, D], F32, kind="ExternalOutput").ap()

    q_hi_s = nc.dram_tensor("q_hi_s", [OWN, HH], BF16, kind=dk).ap()
    q_lo_s = nc.dram_tensor("q_lo_s", [OWN, HH], BF16, kind=dk).ap()
    xn_hi_s = nc.dram_tensor("xn_hi_s", [128, 16 * OWN], BF16, kind=dk).ap()
    xn_lo_s = nc.dram_tensor("xn_lo_s", [128, 16 * OWN], BF16, kind=dk).ap()
    nx_s = nc.dram_tensor("nx_s", [128, 16 * OWN], BF16, kind=dk).ap()
    gwt_s = nc.dram_tensor("gwt_s", [128, 16 * HID], BF16, kind=dk).ap()
    vwt_s = nc.dram_tensor("vwt_s", [128, 16 * HID], BF16, kind=dk).ap()
    owt_s = nc.dram_tensor("owt_s", [128, 64 * D], BF16, kind=dk).ap()
    h_sc = nc.dram_tensor("h_sc", [OWN, HID], F32, kind=dk).ap()
    ffn_s = nc.dram_tensor("ffn_s", [OWN, D], F32, kind=dk).ap()
    bq2_s = nc.dram_tensor("bq2_s", [2, HH], BF16, kind=dk).ap()
    bv2_s = nc.dram_tensor("bv2_s", [2, HH], BF16, kind=dk).ap()
    bu2_s = nc.dram_tensor("bu2_s", [2, HH], BF16, kind=dk).ap()
    bo2_s = nc.dram_tensor("bo2_s", [2, D], BF16, kind=dk).ap()
    rg_s = nc.dram_tensor("rg_s", [2, HID], BF16, kind=dk).ap()
    rv_s = nc.dram_tensor("rv_s", [2, HID], BF16, kind=dk).ap()
    ro_s = nc.dram_tensor("ro_s", [2, D], BF16, kind=dk).ap()
    if debug:
        kv_dbg = nc.dram_tensor("kv_dbg", [128, HH], F32, kind="ExternalOutput").ap()
        ar_dbg = nc.dram_tensor("ar_dbg", [128, 64], F32, kind="ExternalOutput").ap()
        sc_dbg = nc.dram_tensor("sc_dbg", [128, 8], F32, kind="ExternalOutput").ap()

    es = ExitStack()
    with es:
        def sb(name, shape, dt):
            return es.enter_context(nc.sbuf_tensor(name, shape, dt))

        kv_acc = sb("kv_acc", [128, HH], F32)
        kv_hi = sb("kv_hi", [128, HH], BF16)
        kv_lo = sb("kv_lo", [128, HH], BF16)
        AR = sb("AR", [128, 64], F32)
        A2 = sb("A2", [128, 8], F32)
        SC = sb("SC", [128, 8], F32)
        onesc = sb("onesc", [128, 1], F32)
        ones1 = sb("ones1", [1, 128], F32)
        ones2 = sb("ones2", [2, 128], BF16)
        eyef = sb("eyef_t", [128, 128], F32)
        eyeb = sb("eyeb_t", [128, 128], BF16)
        b2t = sb("b2t", [2, 1024], BF16)
        la = sb("la", [2, OWN], BF16)
        lb = sb("lb", [2, OWN], BF16)
        g_cols = sb("g_cols", [128, 16], F32)
        b_cols = sb("b_cols", [128, 16], F32)
        ihi16 = sb("ihi16", [128, 16], BF16)
        ilo16 = sb("ilo16", [128, 16], BF16)
        itr = sb("itr", [16, 256], BF16)

        pj = [es.enter_context(nc.psum_tensor(f"pj{i}", [128, 512], F32)) for i in range(2)]
        pkv = es.enter_context(nc.psum_tensor("pkv", [128, HH], F32))
        ptr = es.enter_context(nc.psum_tensor("ptr", [128, 512], F32))
        pxb = es.enter_context(nc.psum_tensor("pxb", [128, 1024], BF16))

        dX = Sem(es, nc, "dX")
        dW = Sem(es, nc, "dW")
        dS = Sem(es, nc, "dS")
        dR = Sem(es, nc, "dR")
        dO = Sem(es, nc, "dO")
        dB = Sem(es, nc, "dB")
        dF = Sem(es, nc, "dF")
        sV = Sem(es, nc, "sV")
        sA = Sem(es, nc, "sA")
        sG = Sem(es, nc, "sG")
        sT = Sem(es, nc, "sT")

        block = es.enter_context(nc.Block())

        def on(name):
            return getattr(block, name)

        def barrier():
            toks = {}

            @on("vector")
            def _(v):
                toks["v"] = sV.mark(v)

            @on("scalar")
            def _(a):
                toks["a"] = sA.mark(a)

            @on("gpsimd")
            def _(g):
                toks["g"] = sG.mark(g)

            @on("tensor")
            def _(t):
                toks["t"] = sT.mark(t)

            for en in ("vector", "scalar", "gpsimd", "tensor", "sync"):
                @on(en)
                def _(e):
                    e.wait_ge(sV.h, toks["v"])
                    e.wait_ge(sA.h, toks["a"])
                    e.wait_ge(sG.h, toks["g"])
                    e.wait_ge(sT.h, toks["t"])

        def ladder(eng, scr, src_ap, dst_ap, scale_ap):
            s2, aa, dd, md = scr
            n = src_ap.shape[-1]
            s2v, aav, ddv, mdv = s2[:, 0:n], aa[:, 0:n], dd[:, 0:n], md[:, 0:n]
            if scale_ap is not None:
                eng.tensor_scalar(out=mdv, in0=src_ap, scalar1=scale_ap,
                                  scalar2=None, op0=ALU.mult)
                t = mdv
            else:
                t = src_ap
            eng.tensor_scalar(out=s2v, in0=t, scalar1=0.0, scalar2=2.0,
                              op0=ALU.is_ge, op1=ALU.mult)
            eng.tensor_tensor(out=aav, in0=t, in1=s2v, op=ALU.mult)
            eng.tensor_tensor(out=aav, in0=aav, in1=t, op=ALU.subtract)
            eng.tensor_copy(out=ddv, in_=aav)
            for k in LADDER:
                eng.tensor_scalar(out=mdv, in0=ddv, scalar1=k - 0.5, scalar2=-k,
                                  op0=ALU.is_ge, op1=ALU.mult)
                eng.tensor_tensor(out=ddv, in0=ddv, in1=mdv, op=ALU.add)
            eng.tensor_tensor(out=aav, in0=aav, in1=ddv, op=ALU.subtract)
            eng.tensor_tensor(out=mdv, in0=aav, in1=s2v, op=ALU.mult)
            return eng.tensor_tensor(out=dst_ap, in0=mdv, in1=aav, op=ALU.subtract)

        def emit_setup(small, smbf, smlo):
            @on("vector")
            def _(v):
                v.memset(onesc[:], 1.0)
                v.memset(SC[:, 6:7], LN_EPS)
                v.memset(ones1[:], 1.0)
                v.memset(ones2[:], 1.0)
                v.memset(kv_acc[:], 0.0)
                v.memset(AR[:], 0.0)

            @on("sync")
            def _(s):
                s.dma_start(out=eyef[:], in_=eyef_h[:]).then_inc(dB.h, 16)
                s.dma_start(out=eyeb[:], in_=eyeb_h[:]).then_inc(dB.h, 16)
                s.dma_start(out=g_cols[:],
                            in_=lng_h.rearrange("(c p) -> p c", p=128),
                            allow_slow_non_contiguous=True).then_inc(dB.h, 16)
                s.dma_start(out=b_cols[:],
                            in_=lnb_h.rearrange("(c p) -> p c", p=128),
                            allow_slow_non_contiguous=True).then_inc(dB.h, 16)
                dB.n += 64

            for src, dst, n in ((bqk_h, bq2_s, HH), (bv_h, bv2_s, HH),
                                (bu_h, bu2_s, HH), (bo_h, bo2_s, D)):
                @on("sync")
                def _(s, src=src, n=n):
                    s.wait_ge(sV.h, sV.n)
                    s.dma_start(out=small[0:1, 0:n], in_=src[None, :]).then_inc(dB.h, 16)
                    dB.n += 16
                tok = dB.n

                @on("vector")
                def _(v, n=n, tok=tok):
                    v.wait_ge(dB.h, tok)
                    v.tensor_copy(out=smbf[0:1, 0:n], in_=small[0:1, 0:n])
                    sV.inc(v.tensor_tensor(out=smlo[0:1, 0:n], in0=small[0:1, 0:n],
                                    in1=smbf[0:1, 0:n], op=ALU.subtract))
                vtok = sV.n

                @on("sync")
                def _(s, dst=dst, n=n, vtok=vtok):
                    s.wait_ge(sV.h, vtok)
                    s.dma_start(out=dst[0:1, 0:n], in_=smbf[0:1, 0:n]).then_inc(dB.h, 16)
                    s.dma_start(out=dst[1:2, 0:n], in_=smlo[0:1, 0:n]).then_inc(dB.h, 16)
                    dB.n += 32
                tok2 = dB.n

                @on("vector")
                def _(v, tok2=tok2):
                    v.wait_ge(dB.h, tok2)
                    sV.mark(v)

        def emit_p0(bw):
            plans = []
            for idx, (wh, R, C) in (((1, (vw_h, HID, D))), ((0, (gw_h, HID, D))),
                                    ((2, (ow_h, D, HID)))):
                tiles = []
                for rt in range(R // 128):
                    for cq in range(C // 2048):
                        tiles.append(wh[rt * 128:(rt + 1) * 128,
                                        cq * 2048:(cq + 1) * 2048])
                plans.append((idx, R * C, tiles))

            for idx, nel, tiles in plans:
                @on("vector")
                def _(v):
                    sV.inc(v.memset(A2[:, 0:1], 0.0))

                red_tok = [sV.n, sV.n]
                for it, tile in enumerate(tiles):
                    buf = bw[it % 2]

                    @on("sync")
                    def _(s, tile=tile, buf=buf, it=it):
                        s.wait_ge(sV.h, red_tok[it % 2])
                        s.dma_start(out=buf[:, 0:2048], in_=tile).then_inc(dX.h, 16)
                        dX.n += 16
                    tok = dX.n

                    @on("vector")
                    def _(v, buf=buf, tok=tok):
                        v.wait_ge(dX.h, tok)
                        v.tensor_reduce(out=A2[:, 1:2], in_=buf[:, 0:2048], axis=AX.X,
                                        op=ALU.add, apply_absolute_value=True)
                        v.drain()
                        sV.inc(v.tensor_tensor(out=A2[:, 0:1], in0=A2[:, 0:1],
                                               in1=A2[:, 1:2], op=ALU.add))
                    red_tok[it % 2] = sV.n

                @on("tensor")
                def _(t):
                    t.wait_ge(sV.h, sV.n)
                    sT.inc(t.matmul(out=ptr[0:1, 0:1], lhsT=onesc[:],
                                    rhs=A2[:, 0:1], start=True, stop=True))
                ttok = sT.n

                @on("vector")
                def _(v, nel=nel, ttok=ttok):
                    v.wait_ge(sT.h, ttok)
                    sV.inc(v.tensor_scalar(out=A2[0:1, 2:3], in0=ptr[0:1, 0:1],
                                    scalar1=1.0 / nel, scalar2=1e-5,
                                    op0=ALU.mult, op1=ALU.max))
                vtok = sV.n

                @on("tensor")
                def _(t, vtok=vtok):
                    t.wait_ge(sV.h, vtok)
                    sT.inc(t.matmul(out=ptr[:, 1:2], lhsT=ones1[:],
                                    rhs=A2[0:1, 2:3], start=True, stop=True))
                ttok2 = sT.n

                @on("vector")
                def _(v, idx=idx, ttok2=ttok2):
                    v.wait_ge(sT.h, ttok2)
                    v.tensor_copy(out=SC[:, idx:idx + 1], in_=ptr[:, 1:2])
                    v.drain()
                    sV.inc(v.reciprocal(out=SC[:, 3 + idx:4 + idx], in_=SC[:, idx:idx + 1]))


        def emit_wsplit(wh, fcol, whi, wlo, wstg, stg_tok, wpar_tok, par):
            for q in range(8):
                sl = q % 2
                stg = wstg[sl]

                @on("sync")
                def _(sy, q=q, sl=sl, stg=stg, wh=wh, fcol=fcol):
                    sy.wait_ge(sA.h, stg_tok[sl][0])
                    sy.wait_ge(sV.h, stg_tok[sl][1])
                    src = wh[q * 256:(q + 1) * 256, fcol * 512:(fcol + 1) * 512]
                    sy.dma_start(out=stg[:].rearrange("p (c n) -> p c n", c=2),
                                 in_=src.rearrange("(c p) n -> p c n", p=128)
                                 ).then_inc(dW.h, 16)
                    dW.n += 16
                wtok = dW.n

                @on("scalar")
                def _(a, q=q, stg=stg, whi=whi, wtok=wtok, par=par):
                    a.wait_ge(dW.h, wtok)
                    a.wait_ge(sT.h, wpar_tok[par])
                    sA.inc(a.copy(out=whi[:, q * 1024:(q + 1) * 1024], in_=stg[:]))
                atok = sA.n

                @on("vector")
                def _(v, q=q, stg=stg, whi=whi, wlo=wlo, atok=atok):
                    v.wait_ge(sA.h, atok)
                    sV.inc(v.tensor_tensor(out=wlo[:, q * 1024:(q + 1) * 1024],
                                           in0=stg[:],
                                           in1=whi[:, q * 1024:(q + 1) * 1024],
                                           op=ALU.subtract))
                stg_tok[sl] = (atok, sV.n)
            return (sA.n, sV.n)

        def emit_p1(P):
            (x_t, xn_tm, nx_tm, xnf_hi, xnf_lo, wstg, wc_hi, wc_lo,
             kh_hi, kh_lo, vh_hi, vh_lo, ef32, lad_v, b2qk, b2v) = P
            nsub = B1 // 128
            xn_img_hi = xnf_hi[:].rearrange("p (c n) -> p c n", c=16)
            xn_img_lo = xnf_lo[:].rearrange("p (c n) -> p c n", c=16)
            nxs_img = nx_s.rearrange("p (c n) -> p c n", c=16)
            xnhs_img = xn_hi_s.rearrange("p (c n) -> p c n", c=16)
            xnls_img = xn_lo_s.rearrange("p (c n) -> p c n", c=16)
            ptr_img = ptr[:].rearrange("p (c n) -> p c n", c=4)
            x_tok = [sV.n, sV.n]
            q_spill_tok = [0, 0, 0, 0]
            xn_spill_tok = 0
            nx_spill_tok = 0
            exp_tok = [0]
            sil_tok = [0]
            bank_tok = [(0, 0), (0, 0)]
            wpar_tok = [0, 0]
            stg_tok = [(0, 0), (0, 0)]
            pend_split = [None, None]
            pend_split[0] = emit_wsplit(wqk_h, 0, wc_hi[0], wc_lo[0], wstg,
                                        stg_tok, wpar_tok, 0)

            for b in range(NB1):
                own = b < OWN // B1
                for s in range(nsub):
                    xb = x_t[s % 2]
                    row0 = b * B1 + s * 128
                    tcol = row0 // 128 if own else 0

                    @on("sync")
                    def _(sy, xb=xb, row0=row0, s=s):
                        sy.wait_ge(sV.h, x_tok[s % 2])
                        sy.dma_start(out=xb[:], in_=x_h[row0:row0 + 128, :]).then_inc(dX.h, 16)
                        dX.n += 16
                    xtok = dX.n

                    @on("vector")
                    def _(v, xb=xb, xtok=xtok):
                        v.wait_ge(dX.h, xtok)
                        v.wait_ge(sT.h, sT.n)
                        v.wait_ge(sG.h, sG.n)
                        v.reduce_sum(out=A2[:, 0:1], in_=xb[:], axis=AX.X)
                        v.tensor_tensor(out=xn_tm[:], in0=xb[:], in1=xb[:], op=ALU.mult)
                        v.reduce_sum(out=A2[:, 1:2], in_=xn_tm[:], axis=AX.X)
                        v.drain()
                        v.tensor_scalar(out=A2[:, 2:3], in0=A2[:, 0:1],
                                        scalar1=1.0 / D, scalar2=None, op0=ALU.mult)
                        v.drain()
                        v.tensor_tensor(out=A2[:, 3:4], in0=A2[:, 2:3],
                                        in1=A2[:, 2:3], op=ALU.mult)
                        v.tensor_scalar(out=A2[:, 4:5], in0=A2[:, 1:2],
                                        scalar1=1.0 / D, scalar2=None, op0=ALU.mult)
                        v.drain()
                        sV.inc(v.tensor_tensor(out=A2[:, 4:5], in0=A2[:, 4:5],
                                        in1=A2[:, 3:4], op=ALU.subtract))
                    vtok = sV.n

                    @on("scalar")
                    def _(a, vtok=vtok):
                        a.wait_ge(sV.h, vtok)
                        sA.inc(a.activation(out=A2[:, 5:6], in_=A2[:, 4:5], func=AF.Sqrt,
                                     bias=SC[:, 6:7]))
                    atok = sA.n

                    @on("vector")
                    def _(v, xb=xb, atok=atok):
                        v.wait_ge(sA.h, atok)
                        v.reciprocal(out=A2[:, 5:6], in_=A2[:, 5:6])
                        v.drain()
                        sV.inc(v.tensor_scalar(out=xn_tm[:], in0=xb[:],
                                               scalar1=A2[:, 2:3],
                                               scalar2=A2[:, 5:6], op0=ALU.subtract,
                                               op1=ALU.mult))
                    lnapply_tok = sV.n

                    if own:
                        @on("vector")
                        def _(v, tcol=tcol):
                            v.wait_ge(dS.h, nx_spill_tok)
                            v.tensor_reduce(out=A2[:, 6:7], in_=xn_tm[:], axis=AX.X,
                                            op=ALU.max, apply_absolute_value=True)
                            v.drain()
                            v.tensor_scalar(out=AR[:, tcol:tcol + 1], in0=A2[:, 6:7],
                                            scalar1=1e-5, scalar2=None, op0=ALU.max)
                            v.drain()
                            v.reciprocal(out=A2[:, 7:8], in_=AR[:, tcol:tcol + 1])
                            v.drain()
                            v.tensor_scalar(out=AR[:, 16 + tcol:17 + tcol],
                                            in0=A2[:, 7:8], scalar1=127.0,
                                            scalar2=None, op0=ALU.mult)
                            v.tensor_scalar(out=AR[:, 32 + tcol:33 + tcol],
                                            in0=AR[:, tcol:tcol + 1],
                                            scalar1=1.0 / 127.0, scalar2=None,
                                            op0=ALU.mult)
                            v.drain()
                            for j in range(4):
                                li = ladder(v, lad_v, xn_tm[:, j * 512:(j + 1) * 512],
                                            nx_tm[:, j * 512:(j + 1) * 512],
                                            AR[:, 16 + tcol:17 + tcol])
                            sV.inc(li)
                    lntok = sV.n
                    x_tok[s % 2] = lnapply_tok

                    for grp in range(4):
                        @on("tensor")
                        def _(t, grp=grp, lntok=lnapply_tok):
                            t.wait_ge(sV.h, lntok)
                            t.wait_ge(sA.h, sA.n)
                            t.wait_ge(sV.h, sV.n)
                            for i in range(4):
                                c = grp * 4 + i
                                li = t.transpose(out=ptr[:, i * 128:(i + 1) * 128],
                                            in_=xn_tm[:, c * 128:(c + 1) * 128],
                                            identity=eyef[:])
                            sT.inc(li)
                        ttok = sT.n

                        @on("scalar")
                        def _(a, grp=grp, s=s, ttok=ttok):
                            a.wait_ge(sT.h, ttok)
                            a.wait_ge(dS.h, xn_spill_tok)
                            sA.inc(a.copy(out=xn_img_hi[:, grp * 4:(grp + 1) * 4,
                                                 s * 128:(s + 1) * 128],
                                   in_=ptr_img))
                        atok2 = sA.n

                        @on("vector")
                        def _(v, grp=grp, s=s, ttok=ttok, atok2=atok2):
                            v.wait_ge(sT.h, ttok)
                            v.wait_ge(sA.h, atok2)
                            sV.inc(v.tensor_tensor(out=xn_img_lo[:, grp * 4:(grp + 1) * 4,
                                                          s * 128:(s + 1) * 128],
                                            in0=ptr_img,
                                            in1=xn_img_hi[:, grp * 4:(grp + 1) * 4,
                                                          s * 128:(s + 1) * 128],
                                            op=ALU.subtract))

                    if own:
                        for grp in range(4):
                            @on("tensor")
                            def _(t, grp=grp, lntok=lntok):
                                t.wait_ge(sV.h, lntok)
                                t.wait_ge(sA.h, sA.n)
                                for i in range(4):
                                    c = grp * 4 + i
                                    li = t.transpose(out=pxb[:, i * 128:(i + 1) * 128],
                                                in_=nx_tm[:, c * 128:(c + 1) * 128],
                                                identity=eyeb[:])
                                sT.inc(li)
                            ttok = sT.n

                            @on("scalar")
                            def _(a, grp=grp, ttok=ttok):
                                a.wait_ge(sT.h, ttok)
                                sA.inc(a.copy(out=nx_tm[:, grp * 512:(grp + 1) * 512],
                                       in_=pxb[:, 0:512]))
                        gtok = sA.n

                        @on("sync")
                        def _(sy, row0=row0, gtok=gtok):
                            sy.wait_ge(sA.h, gtok)
                            sy.dma_start(out=nxs_img[:, :, row0:row0 + 128],
                                         in_=nx_tm[:].rearrange("p (c n) -> p c n",
                                                                c=16)).then_inc(dS.h, 16)
                            dS.n += 16
                        nx_spill_tok = dS.n

                        @on("sync")
                        def _(sy, s=s, row0=row0):
                            sy.wait_ge(sV.h, sV.n)
                            sy.wait_ge(sA.h, sA.n)
                            sy.dma_start(out=xnhs_img[:, :, row0:row0 + 128],
                                         in_=xn_img_hi[:, :, s * 128:(s + 1) * 128]
                                         ).then_inc(dS.h, 16)
                            sy.dma_start(out=xnls_img[:, :, row0:row0 + 128],
                                         in_=xn_img_lo[:, :, s * 128:(s + 1) * 128]
                                         ).then_inc(dS.h, 16)
                            dS.n += 32
                        xn_spill_tok = dS.n

                for fcol in range(4):
                    for wi, (wh, b2w) in enumerate(((wqk_h, b2qk), (wv_h, b2v))):
                        gidx = b * 8 + fcol * 2 + wi
                        par = gidx % 2
                        whi, wlo = wc_hi[par], wc_lo[par]
                        splitok = pend_split[par]

                        for tt in range(4):
                            bi = tt % 2
                            pb = pj[bi]

                            @on("tensor")
                            def _(t, tt=tt, fcol=fcol, b2w=b2w, whi=whi, wlo=wlo,
                                  pb=pb, splitok=splitok, bi=bi):
                                t.wait_ge(sA.h, splitok[0])
                                t.wait_ge(sV.h, splitok[1])
                                t.wait_ge(sV.h, bank_tok[bi][0])
                                t.wait_ge(sA.h, bank_tok[bi][1])
                                t.matmul(out=pb[:], lhsT=ones2[:],
                                         rhs=b2w[:, fcol * 512:(fcol + 1) * 512],
                                         start=True, stop=False)
                                for dc in range(16):
                                    t.matmul(out=pb[:],
                                             lhsT=xn_img_hi[:, dc, tt * 128:(tt + 1) * 128],
                                             rhs=whi[:, dc * 512:(dc + 1) * 512],
                                             start=False, stop=False)
                                for dc in range(16):
                                    t.matmul(out=pb[:],
                                             lhsT=xn_img_hi[:, dc, tt * 128:(tt + 1) * 128],
                                             rhs=wlo[:, dc * 512:(dc + 1) * 512],
                                             start=False, stop=False)
                                for dc in range(16):
                                    li = t.matmul(out=pb[:],
                                             lhsT=xn_img_lo[:, dc, tt * 128:(tt + 1) * 128],
                                             rhs=whi[:, dc * 512:(dc + 1) * 512],
                                             start=False, stop=(dc == 15))
                                sT.inc(li)
                            mmtok = sT.n
                            wpar_tok[par] = mmtok

                            if tt == 0 and gidx + 1 < NB1 * 8:
                                nb, nrem = divmod(gidx + 1, 8)
                                nfc, nwi = divmod(nrem, 2)
                                nwh = wqk_h if nwi == 0 else wv_h
                                pend_split[(gidx + 1) % 2] = emit_wsplit(
                                    nwh, nfc, wc_hi[(gidx + 1) % 2],
                                    wc_lo[(gidx + 1) % 2], wstg, stg_tok,
                                    wpar_tok, (gidx + 1) % 2)

                            if wi == 0:
                                @on("vector")
                                def _(v, pb=pb, mmtok=mmtok):
                                    v.wait_ge(sT.h, mmtok)
                                    v.wait_ge(sA.h, exp_tok[0])
                                    v.tensor_scalar(out=ef32[0][:], in0=pb[:],
                                                    scalar1=0.0, scalar2=None,
                                                    op0=ALU.min)
                                    sV.inc(v.tensor_scalar(out=ef32[1][:], in0=pb[:],
                                                           scalar1=0.0, scalar2=None,
                                                           op0=ALU.max))
                                vtok = sV.n
                                bank_tok[bi] = (vtok, 0)

                                @on("scalar")
                                def _(a, vtok=vtok):
                                    a.wait_ge(sV.h, vtok)
                                    sA.inc(a.activation(out=ef32[0][:], in_=ef32[0][:],
                                                        func=AF.Exp))
                                exp_tok[0] = sA.n
                                atok = sA.n

                                @on("vector")
                                def _(v, tt=tt, atok=atok):
                                    v.wait_ge(sA.h, atok)
                                    v.wait_ge(dS.h, q_spill_tok[tt])
                                    v.scalar_tensor_tensor(out=ef32[1][:], in0=ef32[0][:],
                                                           scalar=-1.0, in1=ef32[1][:],
                                                           op0=ALU.add, op1=ALU.add)
                                    v.tensor_copy(out=kh_hi[:, tt * 512:(tt + 1) * 512],
                                                  in_=ef32[1][:])
                                    sV.inc(v.tensor_tensor(out=kh_lo[:, tt * 512:(tt + 1) * 512],
                                                    in0=ef32[1][:],
                                                    in1=kh_hi[:, tt * 512:(tt + 1) * 512],
                                                    op=ALU.subtract))
                            else:
                                @on("scalar")
                                def _(a, pb=pb, tt=tt, mmtok=mmtok):
                                    a.wait_ge(sT.h, mmtok)
                                    a.wait_ge(sV.h, sil_tok[0])
                                    sA.inc(a.activation(out=ef32[2][:], in_=pb[:],
                                                        func=AF.Silu))
                                atok = sA.n
                                bank_tok[bi] = (0, atok)

                                @on("vector")
                                def _(v, tt=tt, atok=atok):
                                    v.wait_ge(sA.h, atok)
                                    v.tensor_copy(out=vh_hi[:, tt * 512:(tt + 1) * 512],
                                                  in_=ef32[2][:])
                                    sV.inc(v.tensor_tensor(out=vh_lo[:, tt * 512:(tt + 1) * 512],
                                                    in0=ef32[2][:],
                                                    in1=vh_hi[:, tt * 512:(tt + 1) * 512],
                                                    op=ALU.subtract))
                                sil_tok[0] = sV.n

                            if wi == 0 and own:
                                @on("sync")
                                def _(sy, b=b, tt=tt, fcol=fcol):
                                    sy.wait_ge(sV.h, sV.n)
                                    sy.wait_ge(sA.h, sA.n)
                                    r0 = b * B1 + tt * 128
                                    sy.dma_start(
                                        out=q_hi_s[r0:r0 + 128,
                                                   fcol * 512:(fcol + 1) * 512],
                                        in_=kh_hi[:, tt * 512:(tt + 1) * 512]
                                    ).then_inc(dS.h, 16)
                                    sy.dma_start(
                                        out=q_lo_s[r0:r0 + 128,
                                                   fcol * 512:(fcol + 1) * 512],
                                        in_=kh_lo[:, tt * 512:(tt + 1) * 512]
                                    ).then_inc(dS.h, 16)
                                    dS.n += 32
                                q_spill_tok[tt] = dS.n

                    evactok = (sV.n, sA.n)

                    @on("tensor")
                    def _(t, b=b, fcol=fcol, evactok=evactok):
                        t.wait_ge(sV.h, evactok[0])
                        t.wait_ge(sA.h, evactok[1])
                        first = True
                        for hd in range(4):
                            col = fcol * 512 + hd * 128
                            for tt in range(4):
                                for pi, (lh, rh) in enumerate(((kh_hi, vh_hi),
                                                               (kh_hi, vh_lo),
                                                               (kh_lo, vh_hi))):
                                    last = (hd == 3 and tt == 3 and pi == 2)
                                    sl = slice(tt * 512 + hd * 128,
                                               tt * 512 + (hd + 1) * 128)
                                    li = t.matmul(out=pkv[:, col:col + 128],
                                                  lhsT=lh[:, sl], rhs=rh[:, sl],
                                                  start=first, stop=last)
                                    first = False
                        sT.inc(li)

                kvtok = sT.n

                @on("vector")
                def _(v, kvtok=kvtok):
                    v.wait_ge(sT.h, kvtok)
                    sV.inc(v.tensor_tensor(out=kv_acc[:], in0=pkv[:], in1=kv_acc[:],
                                    op=ALU.add))

            @on("vector")
            def _(v):
                v.tensor_copy(out=kv_hi[:], in_=kv_acc[:])
                v.tensor_tensor(out=kv_lo[:], in0=kv_acc[:], in1=kv_hi[:],
                                op=ALU.subtract)
                v.tensor_copy(out=ihi16[:], in_=AR[:, 16:32])
                v.drain()
                sV.inc(v.tensor_tensor(out=ilo16[:], in0=AR[:, 16:32], in1=ihi16[:],
                                op=ALU.subtract))
            vtok = sV.n

            @on("tensor")
            def _(t, vtok=vtok):
                t.wait_ge(sV.h, vtok)
                t.wait_ge(sA.h, sA.n)
                t.transpose(out=pxb[0:16, 0:128], in_=ihi16[:], identity=eyeb[:])
                sT.inc(t.transpose(out=pxb[0:16, 128:256], in_=ilo16[:], identity=eyeb[:]))
            ttok = sT.n

            @on("scalar")
            def _(a, ttok=ttok):
                a.wait_ge(sT.h, ttok)
                sA.inc(a.copy(out=itr[:, 0:256], in_=pxb[0:16, 0:256]))
            atok = sA.n

            @on("sync")
            def _(sy, atok=atok):
                sy.wait_ge(sA.h, atok)
                for dst, c0s in ((la, (0, 128)), (lb, (128, 0))):
                    for r, c0 in enumerate(c0s):
                        sy.dma_start(
                            out=dst[r:r + 1, :].rearrange("o (c p) -> o c p", p=128),
                            in_=itr[:, c0:c0 + 128]).then_inc(dB.h, 16)
                        dB.n += 16

            if debug:
                @on("sync")
                def _(sy):
                    sy.wait_ge(sV.h, sV.n)
                    sy.dma_start(out=kv_dbg, in_=kv_acc[:]).then_inc(dS.h, 16)
                    sy.dma_start(out=ar_dbg, in_=AR[:]).then_inc(dS.h, 16)
                    sy.dma_start(out=sc_dbg, in_=SC[:]).then_inc(dS.h, 16)
                    dS.n += 48

        def emit_tern(P):
            tstr, tp1, tern2, ttmp, smallF, smbfF, smloF = P
            plans = [(gw_h, gwt_s, HID, D, 0), (vw_h, vwt_s, HID, D, 1),
                     (ow_h, owt_s, D, HID, 2)]
            str_tok = [(sV.n, sG.n), (sV.n, sG.n)]
            spill_tok = [0, 0, 0, 0]
            ter_tok = [0, 0]
            pxb_tok = [0, 0]
            vev_tok = [0]
            for wh, dst_s, R, C, idx in plans:
                dst_img = dst_s.rearrange("p (c n) -> p c n", c=16 if idx < 2 else 64)

                @on("vector")
                def _(v, idx=idx):
                    v.tensor_scalar(out=A2[:, 3:4], in0=SC[:, idx:idx + 1],
                                    scalar1=0.5, scalar2=None, op0=ALU.mult)
                    sV.inc(v.tensor_scalar(out=A2[:, 4:5], in0=SC[:, idx:idx + 1],
                                    scalar1=-0.5, scalar2=None, op0=ALU.mult))
                thr_tok = sV.n


                ntile = (R // 128) * (C // 2048)
                for it in range(ntile):
                    rt, cq = it // (C // 2048), it % (C // 2048)
                    sl = it % 2
                    buf = tstr[sl]
                    ter = tern2[sl]

                    @on("sync")
                    def _(sy, wh=wh, rt=rt, cq=cq, buf=buf, sl=sl):
                        sy.wait_ge(sV.h, str_tok[sl][0])
                        sy.wait_ge(sG.h, str_tok[sl][1])
                        sy.dma_start(out=buf[:],
                                     in_=wh[rt * 128:(rt + 1) * 128,
                                            cq * 2048:(cq + 1) * 2048]).then_inc(dW.h, 16)
                        dW.n += 16
                    wtok = dW.n

                    @on("vector")
                    def _(e, buf=buf, ter=ter, wtok=wtok, sl=sl):
                        e.wait_ge(dW.h, wtok)
                        e.wait_ge(sT.h, ter_tok[sl])
                        e.tensor_scalar(out=tp1[:], in0=buf[:], scalar1=A2[:, 3:4],
                                        scalar2=None, op0=ALU.is_ge)
                        e.tensor_scalar(out=ter[:], in0=buf[:], scalar1=A2[:, 4:5],
                                        scalar2=-1.0, op0=ALU.is_le, op1=ALU.mult)
                        sV.inc(e.tensor_tensor(out=ter[:], in0=ter[:], in1=tp1[:],
                                               op=ALU.add))
                    str_tok[sl] = (sV.n, sG.n)
                    etok = (sV.n, sG.n)
                    tsl4 = it % 4
                    tstage = ttmp[tsl4]

                    for grp in range(4):
                        ph_ = grp % 2

                        @on("tensor")
                        def _(t, ter=ter, grp=grp, etok=etok, ph_=ph_):
                            t.wait_ge(sV.h, etok[0])
                            t.wait_ge(sA.h, pxb_tok[ph_])
                            t.wait_ge(sV.h, vev_tok[0])
                            for i in range(4):
                                c = grp * 4 + i
                                li = t.transpose(
                                    out=pxb[:, (ph_ * 4 + i) * 128:(ph_ * 4 + i + 1) * 128],
                                    in_=ter[:, c * 128:(c + 1) * 128],
                                    identity=eyeb[:])
                            sT.inc(li)
                        ttok = sT.n

                        if grp % 2 == 0:
                            @on("scalar")
                            def _(a, grp=grp, ttok=ttok, tstage=tstage, tsl4=tsl4, ph_=ph_):
                                a.wait_ge(sT.h, ttok)
                                a.wait_ge(dS.h, spill_tok[tsl4])
                                sA.inc(a.copy(out=tstage[:, grp * 512:(grp + 1) * 512],
                                              in_=pxb[:, ph_ * 512:(ph_ + 1) * 512]))
                            pxb_tok[ph_] = sA.n
                        else:
                            @on("vector")
                            def _(v, grp=grp, ttok=ttok, tstage=tstage, tsl4=tsl4, ph_=ph_):
                                v.wait_ge(sT.h, ttok)
                                v.wait_ge(dS.h, spill_tok[tsl4])
                                sV.inc(v.tensor_copy(out=tstage[:, grp * 512:(grp + 1) * 512],
                                                     in_=pxb[:, ph_ * 512:(ph_ + 1) * 512]))
                            pxb_tok[ph_] = 0
                            vev_tok[0] = sV.n
                    ter_tok[sl] = sT.n
                    atok = sA.n

                    @on("sync")
                    def _(sy, rt=rt, cq=cq, idx=idx, atok=atok, dst_img=dst_img,
                          tstage=tstage, vtk=None):
                        sy.wait_ge(sA.h, atok)
                        sy.wait_ge(sV.h, vev_tok[0])
                        c0 = cq * 16
                        dst = dst_img[:, c0:c0 + 16, rt * 128:(rt + 1) * 128]
                        sy.dma_start(out=dst,
                                     in_=tstage[:].rearrange("p (c n) -> p c n", c=16)
                                     ).then_inc(dS.h, 16)
                        dS.n += 16
                    spill_tok[tsl4] = dS.n

            for src, dst, n, idx in ((gb_h, rg_s, HID, 0), (vb_h, rv_s, HID, 1),
                                     (ob_h, ro_s, D, 2)):
                @on("sync")
                def _(sy, src=src, n=n):
                    sy.wait_ge(sV.h, sV.n)
                    sy.dma_start(out=smallF[0:1, 0:n], in_=src[None, :]).then_inc(dB.h, 16)
                    dB.n += 16
                tok = dB.n

                @on("vector")
                def _(v, n=n, idx=idx, tok=tok):
                    v.wait_ge(dB.h, tok)
                    v.tensor_scalar(out=smallF[0:1, 0:n], in0=smallF[0:1, 0:n],
                                    scalar1=SC[0:1, 3 + idx:4 + idx], scalar2=None,
                                    op0=ALU.mult)
                    v.drain()
                    v.tensor_copy(out=smbfF[0:1, 0:n], in_=smallF[0:1, 0:n])
                    sV.inc(v.tensor_tensor(out=smloF[0:1, 0:n], in0=smallF[0:1, 0:n],
                                    in1=smbfF[0:1, 0:n], op=ALU.subtract))
                vtok = sV.n

                @on("sync")
                def _(sy, dst=dst, n=n, vtok=vtok):
                    sy.wait_ge(sV.h, vtok)
                    sy.dma_start(out=dst[0:1, 0:n], in_=smbfF[0:1, 0:n]).then_inc(dB.h, 16)
                    sy.dma_start(out=dst[1:2, 0:n], in_=smloF[0:1, 0:n]).then_inc(dB.h, 16)
                    dB.n += 32
                tok2 = dB.n

                @on("vector")
                def _(v, tok2=tok2):
                    v.wait_ge(dB.h, tok2)
                    sV.mark(v)

        def emit_p3b(P):
            (nxl, gwc, vwc, hq, nq, n_hf, owc, ffn_acc, AR2, ef, lad_v, lad_g,
             lao, lbo, b2o3, hgv) = P
            nxs_img = nx_s.rearrange("p (c n) -> p c n", c=16)
            gwt_img = gwt_s.rearrange("p (c n) -> p c n", c=16)
            vwt_img = vwt_s.rearrange("p (c n) -> p c n", c=16)
            owt_img = owt_s.rearrange("p (c n) -> p c n", c=64)
            nxl_img = nxl[:].rearrange("p (c n) -> p c n", c=16)
            gwc_img = gwc[:].rearrange("p (c n) -> p c n", c=16)
            vwc_img = vwc[:].rearrange("p (c n) -> p c n", c=16)
            nhf_img = n_hf[:].rearrange("p (c n) -> p c n", c=32)
            owc_img = owc[:].rearrange("p (c n) -> p c n", c=32)
            facc_img = ffn_acc[:].rearrange("p (c n) -> p c n", c=4)

            gv_tok = [0, 0]
            out_pkv_tok = [0]
            owc_tok = [0]
            gvmm_tok = [0]
            ef_tok = [0, 0]
            h_spill_tok = [0, 0]
            hscl_tok = [0, 0]
            lad_tok = [0, 0]
            nq_tok = [0, 0]

            @on("vector")
            def _(v):
                v.tensor_scalar(out=AR2[:, 0:16], in0=AR[:, 32:48],
                                scalar1=SC[:, 0:1], scalar2=None, op0=ALU.mult)
                v.tensor_scalar(out=AR2[:, 16:32], in0=AR[:, 32:48],
                                scalar1=SC[:, 1:2], scalar2=None, op0=ALU.mult)
                sV.inc(v.memset(AR2[:, 32:48], 0.0))

            def gv_block(b3):
                r0 = b3 * B3B

                @on("sync")
                def _(sy, r0=r0):
                    sy.wait_ge(sT.h, gvmm_tok[0])
                    sy.dma_start(out=nxl[:], in_=nxs_img[:, :, r0:r0 + B3B]
                                 ).then_inc(dR.h, 16)
                    dR.n += 16
                nxtok = dR.n

                for hcol in range(16):
                    @on("sync")
                    def _(sy, hcol=hcol):
                        sy.wait_ge(sT.h, gvmm_tok[0])
                        sy.dma_start(out=gwc[:], in_=gwt_img[:, :, hcol * 512:
                                                           (hcol + 1) * 512]
                                     ).then_inc(dR.h, 16)
                        sy.dma_start(out=vwc[:], in_=vwt_img[:, :, hcol * 512:
                                                           (hcol + 1) * 512]
                                     ).then_inc(dR.h, 16)
                        sy.dma_start(out=b2t[:, 0:512],
                                     in_=rg_s[:, hcol * 512:(hcol + 1) * 512]
                                     ).then_inc(dB.h, 16)
                        sy.dma_start(out=b2t[:, 512:1024],
                                     in_=rv_s[:, hcol * 512:(hcol + 1) * 512]
                                     ).then_inc(dB.h, 16)
                        dR.n += 32
                        dB.n += 32
                    wtok = (dR.n, dB.n)

                    for tt4 in range(4):
                        tcol = b3 * 4 + tt4
                        tsl = slice(r0 + tt4 * 128, r0 + (tt4 + 1) * 128)
                        kpar = tt4 % 2
                        pgate = pj[0][:] if kpar == 0 else pkv[:, 0:512]
                        pval = pj[1][:] if kpar == 0 else pkv[:, 512:1024]

                        @on("tensor")
                        def _(t, tt4=tt4, tsl=tsl, wtok=wtok, nxtok=nxtok,
                              kpar=kpar, pgate=pgate, pval=pval):
                            t.wait_ge(dR.h, wtok[0])
                            t.wait_ge(dB.h, wtok[1])
                            t.wait_ge(dR.h, nxtok)
                            t.wait_ge(sV.h, max(gv_tok[kpar], out_pkv_tok[0]))
                            for pb, wimg, bslc in ((pgate, gwc_img, b2t[:, 0:512]),
                                                   (pval, vwc_img, b2t[:, 512:1024])):
                                t.matmul(out=pb, lhsT=la[:, tsl], rhs=bslc,
                                         start=True, stop=False)
                                t.matmul(out=pb, lhsT=lb[:, tsl], rhs=bslc,
                                         start=False, stop=False)
                                for dc in range(16):
                                    li = t.matmul(out=pb,
                                                  lhsT=nxl_img[:, dc,
                                                       tt4 * 128:(tt4 + 1) * 128],
                                                  rhs=wimg[:, dc, :],
                                                  start=False, stop=(dc == 15))
                            sT.inc(li)
                        mmtok = sT.n
                        gvmm_tok[0] = mmtok

                        efg = ef[0] if kpar == 0 else ef[2]
                        efv = ef[1] if kpar == 0 else ef[3]

                        @on("vector")
                        def _(v, tt4=tt4, tcol=tcol, mmtok=mmtok, kpar=kpar,
                              pgate=pgate, pval=pval, efg=efg, efv=efv):
                            v.wait_ge(sT.h, mmtok)
                            v.wait_ge(sA.h, ef_tok[kpar])
                            v.tensor_scalar(out=efg[:], in0=pgate,
                                            scalar1=AR2[:, tcol:tcol + 1],
                                            scalar2=None, op0=ALU.mult)
                            sV.inc(v.tensor_scalar(out=efv[:], in0=pval,
                                            scalar1=AR2[:, 16 + tcol:17 + tcol],
                                            scalar2=None, op0=ALU.mult))
                        vtok = sV.n
                        gv_tok[kpar] = vtok

                        @on("scalar")
                        def _(a, vtok=vtok, efg=efg):
                            a.wait_ge(sV.h, vtok)
                            sA.inc(a.activation(out=efg[:], in_=efg[:], func=AF.Silu))
                        atok = sA.n

                        @on("vector")
                        def _(v, tcol=tcol, atok=atok, kpar=kpar, efg=efg, efv=efv):
                            v.wait_ge(sA.h, atok)
                            v.wait_ge(dS.h, h_spill_tok[kpar])
                            sV.inc(v.tensor_tensor(out=hgv[kpar][:], in0=efg[:],
                                                   in1=efv[:], op=ALU.mult))
                        vtok2 = sV.n

                        @on("scalar")
                        def _(a, vtok2=vtok2, kpar=kpar, efg=efg):
                            a.wait_ge(sV.h, vtok2)
                            sA.inc(a.activation(out=efg[:], in_=hgv[kpar][:],
                                                func=AF.Abs))
                        atok2 = sA.n
                        ef_tok[kpar] = atok2

                        @on("vector")
                        def _(v, tcol=tcol, atok2=atok2, efg=efg):
                            v.wait_ge(sA.h, atok2)
                            v.tensor_reduce(out=A2[:, 0:1], in_=efg[:], axis=AX.X,
                                            op=ALU.max)
                            v.drain()
                            sV.inc(v.tensor_tensor(out=AR2[:, 32 + tcol:33 + tcol],
                                            in0=AR2[:, 32 + tcol:33 + tcol],
                                            in1=A2[:, 0:1], op=ALU.max))
                        vtok3 = sV.n

                        @on("sync")
                        def _(sy, tsl=tsl, hcol=hcol, vtok3=vtok3, kpar=kpar):
                            sy.wait_ge(sV.h, vtok3)
                            sy.dma_start(out=h_sc[tsl, hcol * 512:(hcol + 1) * 512],
                                         in_=hgv[kpar][:]).then_inc(dS.h, 16)
                            dS.n += 16
                        h_spill_tok[kpar] = dS.n
                    yield

                csl = slice(32 + b3 * 4, 32 + b3 * 4 + 4)
                ssl = slice(48 + b3 * 4, 48 + b3 * 4 + 4)
                osl = slice(0 + b3 * 4, 4 + b3 * 4)

                @on("vector")
                def _(v):
                    v.wait_ge(dS.h, dS.n)
                    v.tensor_scalar(out=AR2[:, csl], in0=AR2[:, csl], scalar1=1e-5,
                                    scalar2=None, op0=ALU.max)
                    v.drain()
                    v.reciprocal(out=AR2[:, ssl], in_=AR2[:, csl])
                    v.drain()
                    v.tensor_scalar(out=AR2[:, ssl], in0=AR2[:, ssl], scalar1=127.0,
                                    scalar2=None, op0=ALU.mult)
                    v.tensor_scalar(out=AR2[:, osl], in0=AR2[:, csl],
                                    scalar1=1.0 / 127.0, scalar2=None, op0=ALU.mult)
                    v.drain()
                    v.tensor_scalar(out=AR2[:, osl], in0=AR2[:, osl],
                                    scalar1=SC[:, 2:3], scalar2=None, op0=ALU.mult)
                    v.drain()
                    v.reciprocal(out=A2[:, 4:8], in_=AR2[:, osl])
                    v.drain()
                    v.tensor_copy(out=ihi16[:, b3 * 4:b3 * 4 + 4], in_=A2[:, 4:8])
                    v.drain()
                    sV.inc(v.tensor_tensor(out=ilo16[:, b3 * 4:b3 * 4 + 4], in0=A2[:, 4:8],
                                    in1=ihi16[:, b3 * 4:b3 * 4 + 4], op=ALU.subtract))
                gam_tok = sV.n

                @on("tensor")
                def _(t, gam_tok=gam_tok):
                    t.wait_ge(sV.h, gam_tok)
                    t.wait_ge(sA.h, sA.n)
                    t.transpose(out=pxb[0:16, 0:128], in_=ihi16[:], identity=eyeb[:])
                    sT.inc(t.transpose(out=pxb[0:16, 128:256], in_=ilo16[:], identity=eyeb[:]))
                ttok = sT.n

                @on("scalar")
                def _(a, ttok=ttok):
                    a.wait_ge(sT.h, ttok)
                    a.wait_ge(dB.h, dB.n)
                    sA.inc(a.copy(out=itr[:, 0:256], in_=pxb[0:16, 0:256]))
                atok = sA.n

                @on("sync")
                def _(sy, b3=b3, r0=r0, atok=atok):
                    sy.wait_ge(sA.h, atok)
                    sy.wait_ge(sT.h, sT.n)
                    for dst, c0 in ((lao[b3 % 2], 0), (lbo[b3 % 2], 128)):
                        sy.dma_start(
                            out=dst[0:1, :].rearrange("o (c p) -> o c p", p=128),
                            in_=itr[b3 * 4:b3 * 4 + 4, c0:c0 + 128]).then_inc(dB.h, 16)
                        sy.dma_start(
                            out=dst[1:2, :].rearrange("o (c p) -> o c p", p=128),
                            in_=itr[b3 * 4:b3 * 4 + 4, (c0 + 128) % 256:(c0 + 128) % 256 + 128]
                            ).then_inc(dB.h, 16)
                        dB.n += 32

            def qout_block(b3):
                r0 = b3 * B3B
                for half in range(2):
                    for tt4 in range(4):
                        tcol = b3 * 4 + tt4
                        tsl = slice(r0 + tt4 * 128, r0 + (tt4 + 1) * 128)
                        for hp in range(4):
                            hcol = half * 8 + hp * 2
                            sl = hp % 2

                            @on("sync")
                            def _(sy, tsl=tsl, hcol=hcol, sl=sl):
                                sy.wait_ge(sV.h, lad_tok[sl])
                                sy.dma_start(out=hq[sl][:],
                                             in_=h_sc[tsl, hcol * 512:(hcol + 2) * 512]
                                             ).then_inc(dR.h, 16)
                                dR.n += 16
                            htok = dR.n

                            @on("scalar")
                            def _(a, sl=sl, tcol=tcol, htok=htok):
                                a.wait_ge(dR.h, htok)
                                a.wait_ge(sV.h, lad_tok[sl])
                                sA.inc(a.activation(out=hq[sl][:], in_=hq[sl][:],
                                                    func=AF.Identity,
                                                    scale=AR2[:, 48 + tcol:49 + tcol]))
                            hscl_tok[sl] = sA.n
                            stok = sA.n

                            @on("vector")
                            def _(e, sl=sl, stok=stok):
                                e.wait_ge(sA.h, stok)
                                e.wait_ge(sT.h, nq_tok[sl])
                                e.wait_ge(dF.h, dF.n)
                                sV.inc(ladder(e, lad_v, hq[sl][:], nq[sl][:], None))
                            lad_tok[sl] = sV.n
                            etok = sV.n

                            @on("tensor")
                            def _(t, etok=etok, sl=sl):
                                t.wait_ge(sV.h, etok)
                                t.wait_ge(sA.h, sA.n)
                                for i in range(8):
                                    li = t.transpose(out=pxb[:, i * 128:(i + 1) * 128],
                                                in_=nq[sl][:, i * 128:(i + 1) * 128],
                                                identity=eyeb[:])
                                sT.inc(li)
                            nq_tok[sl] = sT.n
                            ttok2 = sT.n

                            @on("scalar")
                            def _(a, hp=hp, tt4=tt4, ttok2=ttok2):
                                a.wait_ge(sT.h, ttok2)
                                dst = nhf_img[:, hp * 8:(hp + 1) * 8,
                                              tt4 * 128:(tt4 + 1) * 128]
                                sA.inc(a.copy(out=dst,
                                       in_=pxb[:].rearrange("p (c n) -> p c n", c=8)))
                            yield
                    nh_tok = (sA.n, sV.n, sG.n)

                    for dcol in range(8):
                        @on("sync")
                        def _(sy, half=half, dcol=dcol):
                            sy.wait_ge(sT.h, sT.n)
                            sy.dma_start(out=owc[:],
                                         in_=owt_img[:, half * 32:(half + 1) * 32,
                                                     dcol * 256:(dcol + 1) * 256]
                                         ).then_inc(dR.h, 16)
                            sy.dma_start(out=b2o3[:],
                                         in_=ro_s[:, dcol * 256:(dcol + 1) * 256]
                                         ).then_inc(dB.h, 16)
                            dR.n += 16
                            dB.n += 16
                        otok = (dR.n, dB.n)

                        for tt4 in range(4):
                            tcol = b3 * 4 + tt4
                            tsl = slice(r0 + tt4 * 128, r0 + (tt4 + 1) * 128)
                            pb = pkv[:, tt4 * 512:tt4 * 512 + 256]

                            @on("tensor")
                            def _(t, half=half, tt4=tt4, tsl=tsl, pb=pb, otok=otok,
                                  nh_tok=nh_tok):
                                t.wait_ge(dR.h, otok[0])
                                t.wait_ge(dB.h, otok[1])
                                t.wait_ge(sA.h, nh_tok[0])
                                t.wait_ge(sV.h, sV.n)
                                first = True
                                if half == 0:
                                    lsl = slice(tt4 * 128, (tt4 + 1) * 128)
                                    t.matmul(out=pb, lhsT=lao[b3 % 2][:, lsl],
                                             rhs=b2o3[:, 0:256], start=True, stop=False)
                                    t.matmul(out=pb, lhsT=lbo[b3 % 2][:, lsl],
                                             rhs=b2o3[:, 0:256], start=False, stop=False)
                                    first = False
                                for hc in range(32):
                                    li = t.matmul(out=pb,
                                             lhsT=nhf_img[:, hc, tt4 * 128:(tt4 + 1) * 128],
                                             rhs=owc_img[:, hc, :],
                                             start=first, stop=(hc == 31))
                                    first = False
                                sT.inc(li)
                            mmtok = sT.n

                            @on("vector")
                            def _(v, half=half, tt4=tt4, dcol=dcol, tcol=tcol,
                                  pb=pb, mmtok=mmtok):
                                v.wait_ge(sT.h, mmtok)
                                dsl = facc_img[:, tt4, dcol * 256:(dcol + 1) * 256]
                                if half == 0:
                                    sV.inc(v.tensor_copy(out=dsl, in_=pb))
                                else:
                                    v.wait_ge(dF.h, dF.n)
                                    v.tensor_tensor(out=lad_v[2][:, 0:256], in0=pb,
                                                    in1=dsl, op=ALU.add)
                                    sV.inc(v.tensor_scalar(out=lad_v[3][:, 0:256],
                                                    in0=lad_v[2][:, 0:256],
                                                    scalar1=AR2[:, tcol:tcol + 1],
                                                    scalar2=None, op0=ALU.mult))
                            vtok = sV.n

                            out_pkv_tok[0] = sV.n
                            if half == 1:
                                @on("sync")
                                def _(sy, tsl=tsl, dcol=dcol, vtok=vtok):
                                    sy.wait_ge(sV.h, vtok)
                                    sy.dma_start(out=ffn_s[tsl, dcol * 256:(dcol + 1) * 256],
                                                 in_=lad_v[3][:, 0:256]).then_inc(dF.h, 16)
                                    dF.n += 16
                        yield


            def drain(g, n=10 ** 9):
                for _ in range(n):
                    if next(g, "END") == "END":
                        return True
                return False

            prev_q = None
            for b3 in range(NB3B):
                g = gv_block(b3)
                while True:
                    done_g = drain(g, 1)
                    if prev_q is not None:
                        drain(prev_q, 1)
                    if done_g:
                        break
                if prev_q is not None:
                    drain(prev_q)
                prev_q = qout_block(b3)
            drain(prev_q)

        def emit_p3a(P):
            (q_st, qfh, qfl, xfh, xfl, at1, u_tm, mfh, mfl, wstg, wc_hi, wc_lo,
             ef32, ytile, xtile, ftile, b2u, b2o) = P
            xnhs_img = xn_hi_s.rearrange("p (c n) -> p c n", c=16)
            xnls_img = xn_lo_s.rearrange("p (c n) -> p c n", c=16)
            bank_tok = [(0, 0), (0, 0)]
            mb_tok = [0, 0]
            qpx_tok = [0, 0]
            wpar_tok = [0, 0]
            stg_tok = [(0, 0), (0, 0)]
            for b4 in range(NB3A):
                r0 = b4 * B3A
                t1_toks = []
                for tl in range(2):
                    rr = r0 + tl * 128

                    @on("sync")
                    def _(sy, rr=rr, tl=tl):
                        sy.wait_ge(sT.h, sT.n)
                        sy.dma_start(out=xfh[tl][:], in_=xnhs_img[:, :, rr:rr + 128]
                                     ).then_inc(dR.h, 16)
                        sy.dma_start(out=xfl[tl][:], in_=xnls_img[:, :, rr:rr + 128]
                                     ).then_inc(dR.h, 16)
                        dR.n += 32

                    for qsrc, dst in ((q_hi_s, qfh[tl]), (q_lo_s, qfl[tl])):
                        @on("sync")
                        def _(sy, qsrc=qsrc, rr=rr):
                            sy.wait_ge(sT.h, sT.n)
                            sy.dma_start(out=q_st[:], in_=qsrc[rr:rr + 128, :]
                                         ).then_inc(dR.h, 16)
                            dR.n += 16
                        qtok = dR.n
                        for grp in range(4):
                            ph_ = grp % 2

                            @on("tensor")
                            def _(t, grp=grp, qtok=qtok, ph_=ph_):
                                t.wait_ge(dR.h, qtok)
                                t.wait_ge(sA.h, qpx_tok[ph_])
                                for i in range(4):
                                    c = grp * 4 + i
                                    li = t.transpose(
                                        out=pxb[:, (ph_ * 4 + i) * 128:(ph_ * 4 + i + 1) * 128],
                                        in_=q_st[:, c * 128:(c + 1) * 128],
                                        identity=eyeb[:])
                                sT.inc(li)
                            ttok = sT.n

                            @on("scalar")
                            def _(a, dst=dst, grp=grp, ttok=ttok, ph_=ph_):
                                a.wait_ge(sT.h, ttok)
                                sA.inc(a.copy(out=dst[:, grp * 512:(grp + 1) * 512],
                                       in_=pxb[:, ph_ * 512:(ph_ + 1) * 512]))
                            qpx_tok[ph_] = sA.n
                    qf_tok = sA.n

                    @on("tensor")
                    def _(t, tl=tl, qf_tok=qf_tok):
                        t.wait_ge(sA.h, qf_tok)
                        t.wait_ge(sV.h, sV.n)
                        for hd in range(16):
                            c = hd * 128
                            for pi, (lh, rh) in enumerate(((qfh[tl], kv_hi),
                                                           (qfh[tl], kv_lo),
                                                           (qfl[tl], kv_hi))):
                                li = t.matmul(out=pkv[:, c:c + 128],
                                              lhsT=lh[:, c:c + 128],
                                              rhs=rh[:, c:c + 128],
                                              start=(pi == 0 and hd % 4 == 0),
                                              stop=(pi == 2 and hd % 4 == 3))
                        sT.inc(li)
                    qkv_tok = sT.n

                    @on("vector")
                    def _(v, tl=tl, qkv_tok=qkv_tok):
                        v.wait_ge(sT.h, qkv_tok)
                        v.tensor_copy(out=at1[tl][:], in_=pkv[:])
                        v.reduce_sum(out=A2[:, 0:1], in_=at1[tl][:], axis=AX.X)
                        v.tensor_tensor(out=at1[tl][:], in0=at1[tl][:], in1=at1[tl][:],
                                        op=ALU.mult)
                        v.reduce_sum(out=A2[:, 1:2], in_=at1[tl][:], axis=AX.X)
                        v.drain()
                        v.tensor_scalar(out=A2[:, 2:3], in0=A2[:, 0:1],
                                        scalar1=1.0 / HH, scalar2=None, op0=ALU.mult)
                        v.drain()
                        v.tensor_tensor(out=A2[:, 3:4], in0=A2[:, 2:3],
                                        in1=A2[:, 2:3], op=ALU.mult)
                        v.tensor_scalar(out=A2[:, 4:5], in0=A2[:, 1:2],
                                        scalar1=1.0 / HH, scalar2=None, op0=ALU.mult)
                        v.drain()
                        sV.inc(v.tensor_tensor(out=A2[:, 4:5], in0=A2[:, 4:5],
                                        in1=A2[:, 3:4], op=ALU.subtract))
                    vtok = sV.n

                    @on("scalar")
                    def _(a, vtok=vtok):
                        a.wait_ge(sV.h, vtok)
                        sA.inc(a.activation(out=A2[:, 5:6], in_=A2[:, 4:5], func=AF.Sqrt,
                                     bias=SC[:, 6:7]))
                    atok = sA.n

                    @on("vector")
                    def _(v, tl=tl, atok=atok):
                        v.wait_ge(sA.h, atok)
                        v.reciprocal(out=A2[:, 5:6], in_=A2[:, 5:6])
                        v.drain()
                        sV.inc(v.tensor_scalar(out=at1[tl][:], in0=pkv[:], scalar1=A2[:, 2:3],
                                        scalar2=A2[:, 5:6], op0=ALU.subtract,
                                        op1=ALU.mult))
                    t1_toks.append(sV.n)

                for fcol in range(4):
                    par = fcol % 2
                    splitok = emit_wsplit(wu_h, fcol, wc_hi[par], wc_lo[par],
                                          wstg, stg_tok, wpar_tok, par)

                    for tl in range(2):
                        bi = tl

                        @on("tensor")
                        def _(t, tl=tl, fcol=fcol, par=par, splitok=splitok, bi=bi):
                            t.wait_ge(sA.h, splitok[0])
                            t.wait_ge(sV.h, splitok[1])
                            t.wait_ge(sA.h, bank_tok[bi][0])
                            t.wait_ge(sV.h, bank_tok[bi][1])
                            t.matmul(out=pj[bi][:], lhsT=ones2[:],
                                     rhs=b2u[:, fcol * 512:(fcol + 1) * 512],
                                     start=True, stop=False)
                            xfh_img = xfh[tl][:].rearrange("p (c n) -> p c n", c=16)
                            xfl_img = xfl[tl][:].rearrange("p (c n) -> p c n", c=16)
                            for dc in range(16):
                                t.matmul(out=pj[bi][:], lhsT=xfh_img[:, dc, :],
                                         rhs=wc_hi[par][:, dc * 512:(dc + 1) * 512],
                                         start=False, stop=False)
                            for dc in range(16):
                                t.matmul(out=pj[bi][:], lhsT=xfh_img[:, dc, :],
                                         rhs=wc_lo[par][:, dc * 512:(dc + 1) * 512],
                                         start=False, stop=False)
                            for dc in range(16):
                                li = t.matmul(out=pj[bi][:], lhsT=xfl_img[:, dc, :],
                                         rhs=wc_hi[par][:, dc * 512:(dc + 1) * 512],
                                         start=False, stop=(dc == 15))
                            sT.inc(li)
                        mmtok = sT.n
                        wpar_tok[par] = mmtok

                        @on("scalar")
                        def _(a, tl=tl, fcol=fcol, bi=bi, mmtok=mmtok):
                            a.wait_ge(sT.h, mmtok)
                            sA.inc(a.activation(out=u_tm[tl][:, fcol * 512:(fcol + 1) * 512],
                                         in_=pj[bi][:], func=AF.Silu))
                        bank_tok[bi] = (sA.n, bank_tok[bi][1])

                u_tok = sA.n
                for tl in range(2):
                    for grp in range(4):
                        p_ = grp % 2
                        pa = ptr[:] if p_ == 0 else pkv[:, 0:512]
                        pb_ = pj[0][:] if p_ == 0 else pj[1][:]

                        @on("tensor")
                        def _(t, tl=tl, grp=grp, u_tok=u_tok, t1t=t1_toks[tl],
                              p_=p_, pa=pa, pb_=pb_):
                            t.wait_ge(sA.h, u_tok)
                            t.wait_ge(sV.h, max(t1t, mb_tok[p_]))
                            for i in range(4):
                                c = grp * 4 + i
                                t.transpose(out=pa[:, i * 128:(i + 1) * 128],
                                            in_=at1[tl][:, c * 128:(c + 1) * 128],
                                            identity=eyef[:])
                            for i in range(4):
                                c = grp * 4 + i
                                li = t.transpose(out=pb_[:, i * 128:(i + 1) * 128],
                                                 in_=u_tm[tl][:, c * 128:(c + 1) * 128],
                                                 identity=eyef[:])
                            sT.inc(li)
                        ttok = sT.n

                        @on("vector")
                        def _(v, tl=tl, grp=grp, ttok=ttok, pa=pa, pb_=pb_):
                            v.wait_ge(sT.h, ttok)
                            for i in range(4):
                                c = grp * 4 + i
                                v.tensor_scalar(out=ef32[0][:, i * 128:(i + 1) * 128],
                                                in0=pa[:, i * 128:(i + 1) * 128],
                                                scalar1=g_cols[:, c:c + 1],
                                                scalar2=b_cols[:, c:c + 1],
                                                op0=ALU.mult, op1=ALU.add)
                            v.drain()
                            v.tensor_tensor(out=ef32[1][:], in0=ef32[0][:],
                                            in1=pb_[:, 0:512], op=ALU.mult)
                            v.tensor_copy(out=mfh[tl][:, grp * 512:(grp + 1) * 512],
                                          in_=ef32[1][:])
                            sV.inc(v.tensor_tensor(
                                out=mfl[tl][:, grp * 512:(grp + 1) * 512],
                                in0=ef32[1][:],
                                in1=mfh[tl][:, grp * 512:(grp + 1) * 512],
                                op=ALU.subtract))
                        mb_tok[p_] = sV.n
                m_tok = (sV.n, sA.n)

                for dcol in range(4):
                    par = dcol % 2
                    splitok = emit_wsplit(wo_h, dcol, wc_hi[par], wc_lo[par],
                                          wstg, stg_tok, wpar_tok, par)

                    for tl in range(2):
                        rr = r0 + tl * 128
                        bi = tl

                        @on("sync")
                        def _(sy, rr=rr, dcol=dcol, tl=tl):
                            sy.wait_ge(sV.h, sV.n)
                            sy.dma_start(out=xtile[:],
                                         in_=x_h[rr:rr + 128,
                                                 dcol * 512:(dcol + 1) * 512]
                                         ).then_inc(dX.h, 16)
                            sy.dma_start(out=ftile[:],
                                         in_=ffn_s[rr:rr + 128,
                                                   dcol * 512:(dcol + 1) * 512]
                                         ).then_inc(dR.h, 16)
                            dX.n += 16
                            dR.n += 16
                        xftok = (dX.n, dR.n)

                        @on("tensor")
                        def _(t, tl=tl, dcol=dcol, par=par, splitok=splitok, bi=bi,
                              m_tok=m_tok):
                            t.wait_ge(sA.h, splitok[0])
                            t.wait_ge(sV.h, splitok[1])
                            t.wait_ge(sV.h, m_tok[0])
                            t.wait_ge(sA.h, m_tok[1])
                            t.matmul(out=pj[bi][:], lhsT=ones2[:],
                                     rhs=b2o[:, dcol * 512:(dcol + 1) * 512],
                                     start=True, stop=False)
                            for hc in range(16):
                                t.matmul(out=pj[bi][:],
                                         lhsT=mfh[tl][:, hc * 128:(hc + 1) * 128],
                                         rhs=wc_hi[par][:, hc * 512:(hc + 1) * 512],
                                         start=False, stop=False)
                            for hc in range(16):
                                t.matmul(out=pj[bi][:],
                                         lhsT=mfh[tl][:, hc * 128:(hc + 1) * 128],
                                         rhs=wc_lo[par][:, hc * 512:(hc + 1) * 512],
                                         start=False, stop=False)
                            for hc in range(16):
                                li = t.matmul(out=pj[bi][:],
                                         lhsT=mfl[tl][:, hc * 128:(hc + 1) * 128],
                                         rhs=wc_hi[par][:, hc * 512:(hc + 1) * 512],
                                         start=False, stop=(hc == 15))
                            sT.inc(li)
                        mmtok = sT.n
                        wpar_tok[par] = mmtok

                        @on("vector")
                        def _(v, tl=tl, bi=bi, mmtok=mmtok, xftok=xftok):
                            v.wait_ge(sT.h, mmtok)
                            v.wait_ge(dX.h, xftok[0])
                            v.wait_ge(dR.h, xftok[1])
                            v.wait_ge(dO.h, dO.n)
                            v.tensor_tensor(out=ytile[:], in0=pj[bi][:],
                                            in1=xtile[:], op=ALU.add)
                            sV.inc(v.tensor_tensor(out=ytile[:], in0=ytile[:],
                                            in1=ftile[:], op=ALU.add))
                        bank_tok[bi] = (bank_tok[bi][0], sV.n)
                        vtok = sV.n

                        @on("sync")
                        def _(sy, rr=rr, dcol=dcol, tl=tl, vtok=vtok):
                            sy.wait_ge(sV.h, vtok)
                            sy.dma_start(out=y_h[rr:rr + 128,
                                                 dcol * 512:(dcol + 1) * 512],
                                         in_=ytile[:]).then_inc(dO.h, 16)
                            dO.n += 16

        with ExitStack() as s0:
            small2k = s0.enter_context(nc.sbuf_tensor("small2k", [2, 2048], F32))
            smbf2k = s0.enter_context(nc.sbuf_tensor("smbf2k", [2, 2048], BF16))
            smlo2k = s0.enter_context(nc.sbuf_tensor("smlo2k", [1, 2048], BF16))
            emit_setup(small2k, smbf2k, smlo2k)
            barrier()
        with ExitStack() as ph:
            def sb1(name, shape, dt):
                return ph.enter_context(nc.sbuf_tensor(name, shape, dt))
            x_t = [sb1(f"x_t{i}", [128, D], F32) for i in range(2)]
            emit_p0(x_t)
            barrier()
            if upto not in ("p0",):
                xn_tm = sb1("xn_tm", [128, D], F32)
                nx_tm = sb1("nx_tm", [128, D], BF16)
                xnf_hi = sb1("xnf_hi", [128, 16 * B1], BF16)
                xnf_lo = sb1("xnf_lo", [128, 16 * B1], BF16)
                wstg = [sb1(f"wstg{i}", [128, 1024], F32) for i in range(2)]
                wc_hi = [sb1(f"wc_hi{i}", [128, 16 * 512], BF16) for i in range(2)]
                wc_lo = [sb1(f"wc_lo{i}", [128, 16 * 512], BF16) for i in range(2)]
                b2qk = sb1("b2qk", [2, HH], BF16)
                b2v = sb1("b2v", [2, HH], BF16)

                @on("sync")
                def _(sy):
                    sy.dma_start(out=b2qk[:], in_=bq2_s[:]).then_inc(dB.h, 16)
                    sy.dma_start(out=b2v[:], in_=bv2_s[:]).then_inc(dB.h, 16)
                    dB.n += 32
                b2tok = dB.n

                @on("tensor")
                def _(t, b2tok=b2tok):
                    t.wait_ge(dB.h, b2tok)

                kh_hi = sb1("kh_hi", [128, 4 * 512], BF16)
                kh_lo = sb1("kh_lo", [128, 4 * 512], BF16)
                vh_hi = sb1("vh_hi", [128, 4 * 512], BF16)
                vh_lo = sb1("vh_lo", [128, 4 * 512], BF16)
                ef32 = [sb1(f"ef32{i}", [128, 512], F32) for i in range(3)]
                lad_v = [sb1(f"lad_v{i}", [128, 512], F32) for i in range(4)]
                emit_p1((x_t, xn_tm, nx_tm, xnf_hi, xnf_lo, wstg, wc_hi, wc_lo,
                         kh_hi, kh_lo, vh_hi, vh_lo, ef32, lad_v, b2qk, b2v))
                barrier()

        if upto in ("tern", "p3b", "all"):
            with ExitStack() as ph2:
                def sb2(name, shape, dt):
                    return ph2.enter_context(nc.sbuf_tensor(name, shape, dt))
                tstr = [sb2(f"tstr{i}", [128, 2048], F32) for i in range(2)]
                tp1 = sb2("tp1", [128, 2048], F32)
                tern2 = [sb2(f"tern2{i}", [128, 2048], BF16) for i in range(2)]
                ttmp = [sb2(f"ttmp{i}", [128, 2048], BF16) for i in range(4)]
                smallF = sb2("smallF", [1, HID], F32)
                smbfF = sb2("smbfF", [1, HID], BF16)
                smloF = sb2("smloF", [1, HID], BF16)
                emit_tern((tstr, tp1, tern2, ttmp, smallF, smbfF, smloF))
                barrier()

        if upto in ("p3b", "all"):
            with ExitStack() as ph3:
                def sb3(name, shape, dt):
                    return ph3.enter_context(nc.sbuf_tensor(name, shape, dt))
                nxl = sb3("nxl", [128, 16 * 512], BF16)
                gwc = sb3("gwc", [128, 16 * 512], BF16)
                vwc = sb3("vwc", [128, 16 * 512], BF16)
                hq = [sb3(f"hq{i}", [128, 1024], F32) for i in range(2)]
                nq = [sb3(f"nq{i}", [128, 1024], BF16) for i in range(2)]
                n_hf = sb3("n_hf", [128, 32 * 512], BF16)
                owc = sb3("owc", [128, 32 * 256], BF16)
                ffn_acc = sb3("ffn_acc", [128, 4 * 2048], F32)
                AR2 = sb3("AR2", [128, 64], F32)
                ef = [sb3(f"ef{i}", [128, 512], F32) for i in range(4)]
                lad_v3 = [sb3(f"lad_v3{i}", [128, 1024], F32) for i in range(4)]
                lad_g3 = None
                lao = [sb3(f"lao{i}", [2, B3B], BF16) for i in range(2)]
                lbo = [sb3(f"lbo{i}", [2, B3B], BF16) for i in range(2)]
                b2o3 = sb3("b2o3", [2, 256], BF16)
                hgv = [sb3(f"hgv{i}", [128, 512], F32) for i in range(2)]
                emit_p3b((nxl, gwc, vwc, hq, nq, n_hf, owc, ffn_acc, AR2, ef,
                          lad_v3, lad_g3, lao, lbo, b2o3, hgv))
                barrier()

        if upto == "all":
            with ExitStack() as ph4:
                def sb4(name, shape, dt):
                    return ph4.enter_context(nc.sbuf_tensor(name, shape, dt))
                q_st = sb4("q_st", [128, 2048], BF16)
                qfh0 = sb4("qfh0", [128, 2048], BF16)
                qfl0 = sb4("qfl0", [128, 2048], BF16)
                qfh = [qfh0, qfh0]
                qfl = [qfl0, qfl0]
                xfh = [sb4(f"xfh{i}", [128, 2048], BF16) for i in range(2)]
                xfl = [sb4(f"xfl{i}", [128, 2048], BF16) for i in range(2)]
                at1 = [sb4(f"at1{i}", [128, 2048], F32) for i in range(2)]
                u_tm = [sb4(f"u_tm{i}", [128, 2048], F32) for i in range(2)]
                mfh = [sb4(f"mfh{i}", [128, 2048], BF16) for i in range(2)]
                mfl = [sb4(f"mfl{i}", [128, 2048], BF16) for i in range(2)]
                wstg4 = [sb4(f"wstg4{i}", [128, 1024], F32) for i in range(2)]
                wc_hi4 = [sb4(f"wc_hi4{i}", [128, 16 * 512], BF16) for i in range(2)]
                wc_lo4 = [sb4(f"wc_lo4{i}", [128, 16 * 512], BF16) for i in range(2)]
                ef324 = [sb4(f"ef324{i}", [128, 512], F32) for i in range(2)]
                b2u = sb4("b2u", [2, HH], BF16)
                b2o = sb4("b2o", [2, D], BF16)

                @on("sync")
                def _(sy):
                    sy.dma_start(out=b2u[:], in_=bu2_s[:]).then_inc(dB.h, 16)
                    sy.dma_start(out=b2o[:], in_=bo2_s[:]).then_inc(dB.h, 16)
                    dB.n += 32
                b4tok = dB.n

                @on("tensor")
                def _(t, b4tok=b4tok):
                    t.wait_ge(dB.h, b4tok)

                ytile = sb4("ytile", [128, 512], F32)
                xtile = sb4("xtile", [128, 512], F32)
                ftile = sb4("ftile", [128, 512], F32)
                emit_p3a((q_st, qfh, qfl, xfh, xfl, at1, u_tm, mfh, mfl, wstg4,
                          wc_hi4, wc_lo4, ef324, ytile, xtile, ftile, b2u, b2o))

        @on("gpsimd")
        def _(g):
            g.wait_ge(dS.h, dS.n)
            g.wait_ge(dX.h, dX.n)
            g.wait_ge(dW.h, dW.n)
            g.wait_ge(dB.h, dB.n)
            g.wait_ge(dR.h, dR.n)
            g.wait_ge(dF.h, dF.n)
            if dO.n:
                g.wait_ge(dO.h, dO.n)

    return nc


def _in_maps(inputs):
    x = np.ascontiguousarray(np.asarray(inputs["L0_attn_x"], dtype=np.float32))
    eyef = np.eye(128, dtype=np.float32)
    eyeb = np.eye(128, dtype=ml_dtypes.bfloat16)
    shared = {
        "Wqk": np.ascontiguousarray(inputs["L0_attn_Wqk"]),
        "bqk": np.ascontiguousarray(inputs["L0_attn_bqk"]),
        "Wv": np.ascontiguousarray(inputs["L0_attn_Wv"]),
        "bv": np.ascontiguousarray(inputs["L0_attn_bv"]),
        "Wu": np.ascontiguousarray(inputs["L0_attn_Wu"]),
        "bu": np.ascontiguousarray(inputs["L0_attn_bu"]),
        "Wo": np.ascontiguousarray(inputs["L0_attn_Wo"]),
        "bo": np.ascontiguousarray(inputs["L0_attn_bo"]),
        "ln_g": np.ascontiguousarray(inputs["L0_attn_ln_g"]),
        "ln_b": np.ascontiguousarray(inputs["L0_attn_ln_b"]),
        "gate_w": np.ascontiguousarray(inputs["L0_ffn_gate_w"]),
        "gate_b": np.ascontiguousarray(inputs["L0_ffn_gate_b"]),
        "val_w": np.ascontiguousarray(inputs["L0_ffn_val_w"]),
        "val_b": np.ascontiguousarray(inputs["L0_ffn_val_b"]),
        "out_w": np.ascontiguousarray(inputs["L0_ffn_out_w"]),
        "out_b": np.ascontiguousarray(inputs["L0_ffn_out_b"]),
        "eyef": eyef, "eyeb": eyeb,
    }
    maps = []
    for c in range(8):
        b, hf = c // 2, c % 2
        own = x[b, hf * OWN:(hf + 1) * OWN]
        oth = x[b, (1 - hf) * OWN:(2 - hf) * OWN]
        m = dict(shared)
        m["x_seq"] = np.ascontiguousarray(np.concatenate([own, oth], axis=0))
        maps.append(m)
    return maps


def kernel(**inputs):
    nc = build_program()
    maps = _in_maps(inputs)
    res = run_bass_kernel_spmd(nc, maps, core_ids=list(range(8)))
    y = np.empty((4, SEQ, D), np.float32)
    for c in range(8):
        b, hf = c // 2, c % 2
        y[b, hf * OWN:(hf + 1) * OWN] = res.results[c]["y"]
    return y
